# version 03
# baseline (speedup 1.0000x reference)
import contextlib
import numpy as np
import ml_dtypes
import concourse.bass as bass
import concourse.mybir as mybir
from concourse.bass_utils import run_bass_kernel_spmd

F32 = mybir.dt.float32
BF16 = mybir.dt.bfloat16
U8 = mybir.dt.uint8
U16 = mybir.dt.uint16
AF = mybir.ActivationFunctionType
ALU = mybir.AluOpType
AX = mybir.AxisListType.X

T = 4096
D = 2048
EP = 8352
ET = 66
DI = 4096
CONV = 4224
NCT = 33
NH = 32
HD = 128
NS = 64
CH = 256
NCH = 16
MLP = 16384
NSB = 64
NEG = -1e30

NSB_LIM = NSB
P2_SKIP = set()
PHASE_LIMIT = 99
DEV_SPILL = False
N_CORES = 8
CORE_IDS = None

_DTSZ = {F32: 4, BF16: 2, U8: 1, U16: 2}


def _perm_d():
    p = []
    for w in range(8):
        p += [256 * w + 2 * i for i in range(128)]
        p += [256 * w + 2 * i + 1 for i in range(128)]
    return np.array(p, np.int64)


def host_prep(inputs):
    h = {}
    perm = _perm_d()
    ipw = np.asarray(inputs["L0_attn_in_proj_w"])
    wt = np.ascontiguousarray(ipw.T[perm])
    wt = np.concatenate([wt, np.zeros((D, ET * 128 - EP), np.float32)], axis=1)
    h["wipt"] = np.ascontiguousarray(
        wt.reshape(16, 128, ET, 128).transpose(1, 2, 0, 3))

    def pack_gateup(wp):
        u8 = np.asarray(wp).astype(np.int32).view(np.uint8)[:, 0::4]
        t = np.ascontiguousarray(u8.T)
        t = t.reshape(8, 128, NSB, 256).transpose(1, 2, 0, 3)
        return np.ascontiguousarray(t.reshape(128, NSB * 8 * 256))

    h["gpk"] = pack_gateup(inputs["L0_ffn_gate_W_packed"])
    h["upk"] = pack_gateup(inputs["L0_ffn_up_W_packed"])

    def pack_scales(sc):
        t = np.asarray(sc).T.reshape(8, 8, NSB, 256)
        g = t.transpose(2, 1, 0, 3)
        return np.ascontiguousarray(np.repeat(g, 16, axis=1))

    h["gsc"] = pack_scales(inputs["L0_ffn_gate_scales"])
    h["usc"] = pack_scales(inputs["L0_ffn_up_scales"])

    dpw = np.asarray(inputs["L0_ffn_down_W_packed"]).astype(np.int32).view(np.uint8)[:, 0::4]
    dw = np.ascontiguousarray(dpw.T)
    dw = dw.reshape(64, 128, 4, 512).transpose(1, 2, 0, 3)
    h["dpk"] = np.ascontiguousarray(dw.reshape(128, 4 * 64 * 512))

    dsc = np.asarray(inputs["L0_ffn_down_scales"]).T.reshape(64, 8, 4, 512)
    dsc = dsc.transpose(2, 0, 1, 3)
    h["dsc"] = np.ascontiguousarray(np.repeat(dsc, 16, axis=2))

    opw = np.asarray(inputs["L0_attn_o_proj_w"])
    ot = np.ascontiguousarray(opw.T)
    h["opt"] = np.ascontiguousarray(ot.reshape(32, 128, D).transpose(1, 0, 2))

    cw = np.asarray(inputs["L0_attn_conv_w"])
    h["convw"] = np.ascontiguousarray(cw.reshape(NCT, 128, 4).transpose(1, 0, 2))
    h["convb"] = np.ascontiguousarray(
        np.asarray(inputs["L0_attn_conv_b"]).reshape(NCT, 128).T)

    h["alog"] = np.ascontiguousarray(np.asarray(inputs["L0_attn_A_log"]).reshape(NH, 1)).astype(np.float32)
    h["dtb"] = np.ascontiguousarray(np.asarray(inputs["L0_attn_dt_bias"]).reshape(NH, 1)).astype(np.float32)
    h["drep"] = np.ascontiguousarray(
        np.broadcast_to(np.asarray(inputs["L0_attn_D_skip"])[None, :], (128, NH))).astype(np.float32)
    h["nwt"] = np.ascontiguousarray(
        np.asarray(inputs["L0_attn_norm_w"]).reshape(NH, 128).T).astype(np.float32)

    h["identbf"] = np.eye(128, dtype=ml_dtypes.bfloat16)
    jj = np.arange(128)[:, None]
    ii = np.arange(CH)[None, :]
    nm = np.zeros((128, 2, CH), np.float32)
    nm[:, 0, :] = np.where(jj <= ii, 0.0, NEG)
    nm[:, 1, :] = np.where(jj + 128 <= ii, 0.0, NEG)
    h["negmask"] = nm
    oh = np.zeros((NH, NH * 128), np.float32)
    for hh in range(NH):
        oh[hh, hh * 128:(hh + 1) * 128] = 1.0
    h["onehot"] = oh
    h["onehot"] = h["onehot"].astype(ml_dtypes.bfloat16)
    h["ones_c"] = np.ones((128, 1), ml_dtypes.bfloat16)
    h["ones_r"] = np.ones((1, 128), ml_dtypes.bfloat16)
    return h


class Arena:

    def __init__(self, nc, base, limit):
        self.nc, self.base, self.limit, self.cur = nc, base, limit, base
        self.n = 0

    def alloc(self, name, shape, dtype):
        nbytes = int(np.prod(shape[1:])) * _DTSZ[dtype]
        nbytes = (nbytes + 31) // 32 * 32
        assert self.cur + nbytes <= self.limit, \
            f"arena overflow at {name}: {self.cur + nbytes} > {self.limit}"
        t = self.nc.alloc_sbuf_tensor_at(f"{name}_{self.n}", shape, dtype, offset=self.cur)
        self.n += 1
        self.cur += nbytes
        return t.ap()

    def mark(self):
        return self.cur

    def reset(self, saved=None):
        self.cur = self.base if saved is None else saved


class Ctl:

    ENGINES = ("tensor", "vector", "scalar", "gpsimd", "sync")
    SELF = {"vector": "sv", "scalar": "ss", "gpsimd": "sg"}

    LANES = {"qs": 6, "qg": 6, "qa": 4}

    def __init__(self, sems):
        self.ops = {e: [] for e in self.ENGINES}
        self.cnt = {}
        self.sems = sems
        self.lanes = {q: [f"{q}{i}" for i in range(n)] for q, n in self.LANES.items()}
        self.lane_rr = {q: 0 for q in self.LANES}
        self.last_w = {}
        self.readers = {}
        self.watermark = {e: {} for e in self.ENGINES}
        self.pe_pending_w = []
        self.pe_pending_r = []

    def _bump(self, sem, k=1):
        self.cnt[sem] = self.cnt.get(sem, 0) + k
        return self.cnt[sem]

    def val(self, sem):
        return self.cnt.get(sem, 0)

    def _edge(self, eng, dep):
        if dep is None:
            return
        sem, v = dep
        if v <= self.watermark[eng].get(sem, 0):
            return
        self.watermark[eng][sem] = v
        s = self.sems[sem]
        self.ops[eng].append(lambda e, s=s, v=v: e.wait_ge(s, v))

    def _deps(self, eng, reads, writes):
        for key in reads:
            self._edge(eng, self.last_w.get(key))
        for key in writes:
            self._edge(eng, self.last_w.get(key))
            for d in self.readers.get(key, ()):
                self._edge(eng, d)

    def _commit(self, reads, writes, completion):
        for key in writes:
            self.last_w[key] = completion
            self.readers[key] = []
        for key in reads:
            self.readers.setdefault(key, []).append(completion)

    def op(self, eng, fn, reads=(), writes=()):
        sem = self.SELF[eng]
        self._deps(eng, reads, writes)
        v = self._bump(sem)
        s = self.sems[sem]
        self.ops[eng].append(lambda e, fn=fn, s=s: fn(e).then_inc(s, 1))
        comp = (sem, v)
        self._commit(reads, writes, comp)
        return comp

    def pe(self, fn, reads=(), writes=(), end=False):
        self._deps("tensor", reads, writes)
        self.pe_pending_w.extend(writes)
        self.pe_pending_r.extend(reads)
        if end:
            v = self._bump("pe")
            s = self.sems["pe"]
            self.ops["tensor"].append(lambda e, fn=fn, s=s: fn(e).then_inc(s, 1))
            comp = ("pe", v)
            self._commit(self.pe_pending_r, self.pe_pending_w, comp)
            self.pe_pending_w, self.pe_pending_r = [], []
            return comp
        self.ops["tensor"].append(fn)
        return None

    def dma(self, eng, q, out, in_, reads=(), writes=()):
        lanes = self.lanes[q]
        i = self.lane_rr[q]
        self.lane_rr[q] = (i + 1) % len(lanes)
        lane = lanes[i]
        prev = self.val(lane)
        self._edge(eng, (lane, prev) if prev > 0 else None)
        self._deps(eng, reads, writes)
        v = self._bump(lane, 16)
        s = self.sems[lane]
        self.ops[eng].append(
            lambda e, out=out, in_=in_, s=s: e.dma_start(out=out, in_=in_).then_inc(s, 16))
        comp = (lane, v)
        self._commit(reads, writes, comp)
        return comp

    def fence(self):
        deps = []
        for sem in ("sv", "ss", "sg", "pe"):
            if self.val(sem) > 0:
                deps.append((sem, self.val(sem)))
        for q in self.lanes:
            for lane in self.lanes[q]:
                if self.val(lane) > 0:
                    deps.append((lane, self.val(lane)))
        for eng in self.ENGINES:
            for d in deps:
                self._edge(eng, d)
        self.last_w = {}
        self.readers = {}


def build_nc(phase_limit=None, dev_spill=None):
    if phase_limit is None:
        phase_limit = PHASE_LIMIT
    if dev_spill is None:
        dev_spill = DEV_SPILL
    nc = bass.Bass("TRN2", target_bir_lowering=False)

    ap = lambda t: t.ap()
    x_in = ap(nc.dram_tensor("x", [T, D], F32, kind="ExternalInput"))
    wipt = ap(nc.dram_tensor("wipt", [128, ET, 16, 128], F32, kind="ExternalInput"))
    gpk = ap(nc.dram_tensor("gpk", [128, NSB * 2048], U8, kind="ExternalInput"))
    upk = ap(nc.dram_tensor("upk", [128, NSB * 2048], U8, kind="ExternalInput"))
    gsc = ap(nc.dram_tensor("gsc", [64, 128, 8, 256], F32, kind="ExternalInput"))
    usc = ap(nc.dram_tensor("usc", [64, 128, 8, 256], F32, kind="ExternalInput"))
    dpk = ap(nc.dram_tensor("dpk", [128, 4 * 64 * 512], U8, kind="ExternalInput"))
    dsc = ap(nc.dram_tensor("dsc", [4, 64, 128, 512], F32, kind="ExternalInput"))
    opt = ap(nc.dram_tensor("opt", [128, 32, D], F32, kind="ExternalInput"))
    convw_d = ap(nc.dram_tensor("convw", [128, NCT, 4], F32, kind="ExternalInput"))
    convb_d = ap(nc.dram_tensor("convb", [128, NCT], F32, kind="ExternalInput"))
    alog_d = ap(nc.dram_tensor("alog", [NH, 1], F32, kind="ExternalInput"))
    dtb_d = ap(nc.dram_tensor("dtb", [NH, 1], F32, kind="ExternalInput"))
    drep_d = ap(nc.dram_tensor("drep", [128, NH], F32, kind="ExternalInput"))
    nwt_d = ap(nc.dram_tensor("nwt", [128, NH], F32, kind="ExternalInput"))
    ident_d = ap(nc.dram_tensor("identbf", [128, 128], BF16, kind="ExternalInput"))
    negm_d = ap(nc.dram_tensor("negmask", [128, 2, CH], F32, kind="ExternalInput"))
    onehot_d = ap(nc.dram_tensor("onehot", [NH, NH * 128], BF16, kind="ExternalInput"))
    ones_c_d = ap(nc.dram_tensor("ones_c", [128, 1], BF16, kind="ExternalInput"))
    ones_r_d = ap(nc.dram_tensor("ones_r", [1, 128], BF16, kind="ExternalInput"))

    skind = "ExternalOutput" if dev_spill else "Internal"
    Z = ap(nc.dram_tensor("Zsp", [65, 128, T], F32, kind=skind))
    Isp = ap(nc.dram_tensor("Isp", [128, 128, T], BF16, kind=skind))
    Ysp = ap(nc.dram_tensor("Ysp", [32, 128, T], BF16, kind=skind))
    Ssp = ap(nc.dram_tensor("Ssp", [T, D], F32, kind=skind))
    DTsp = ap(nc.dram_tensor("DTsp", [NH, T], F32, kind=skind))
    drtmp = ap(nc.dram_tensor("drtmp", [NH], F32, kind="Internal"))
    out_d = ap(nc.dram_tensor("out", [T, D], F32, kind="ExternalOutput"))

    sbuf = lambda name, shape, dt: nc.alloc_sbuf_tensor(name, shape, dt).ap()
    ident_s = sbuf("ident_s", [128, 128], BF16)
    ones_c = sbuf("ones_c_s", [128, 1], BF16)
    ones_r = sbuf("ones_r_s", [1, 128], BF16)
    alog_s = sbuf("alog_s", [NH, 1], F32)
    dtb_s = sbuf("dtb_s", [NH, 1], F32)
    A_s = sbuf("A_s", [NH, 1], F32)

    import concourse.bass as _b
    perm_end = nc.bytes_allocated(_b.MemorySpace.SBUF, "left")
    cap = nc.sbuf_top
    BIGB = (perm_end + 31) // 32 * 32
    PHB = BIGB + 131072
    assert PHB < cap, (BIGB, cap)
    bigA = Arena(nc, BIGB, PHB)
    phA = Arena(nc, PHB, cap)

    big0 = bigA.mark()
    xnT = bigA.alloc("xnT", [128, 16, T], BF16)

    pb = [nc.alloc_psum_tensor(f"pb{i}", [128, 512], F32).ap() for i in range(8)]

    stack = contextlib.ExitStack()
    sems = {}
    semnames = ["pe", "sv", "ss", "sg"]
    for q, n in Ctl.LANES.items():
        semnames += [f"{q}{i}" for i in range(n)]
    for name in semnames:
        sems[name] = stack.enter_context(nc.semaphore(name))
    ctl = Ctl(sems)
    OP, PE, DMA = ctl.op, ctl.pe, ctl.dma

    for dst, src, nm in [(ident_s, ident_d, "ident"), (ones_c, ones_c_d, "onc"),
                         (ones_r, ones_r_d, "onr"), (alog_s, alog_d, "alog"),
                         (dtb_s, dtb_d, "dtb")]:
        DMA("gpsimd", "qg", dst, src, writes=[("c", nm)])
    OP("scalar", lambda e: e.activation(A_s, alog_s, AF.Exp),
       reads=[("c", "alog")], writes=[("A",)])
    OP("vector", lambda e: e.tensor_scalar(A_s, A_s, -1.0, None, ALU.mult),
       reads=[("A",)], writes=[("A",)])

    ph0 = phA.mark()
    xb = phA.alloc("xb", [128, 2, D], F32)
    xnbf = phA.alloc("xnbf", [128, 2, D], BF16)
    stats = phA.alloc("stats", [128, 8], F32)
    sqs = phA.alloc("sqs", [128, D], F32)

    NT = T // 128
    xr = x_in.rearrange("(n p) d -> n p d", p=128)
    for i in range(NT):
        sl = i % 2
        DMA("sync", "qs", xb[:, sl, :], xr[i], writes=[("xb", sl)])
        OP("scalar", lambda e, sl=sl: e.activation(sqs, xb[:, sl, :], AF.Square),
           reads=[("xb", sl)], writes=[("sqs",)])
        OP("vector", lambda e, sl=sl: e.reduce_sum(stats[:, 0:1], xb[:, sl, :], axis=AX),
           reads=[("xb", sl)], writes=[("stats",)])
        OP("vector", lambda e: e.tensor_scalar(stats[:, 2:3], stats[:, 0:1], 1.0 / D, None, ALU.mult),
           reads=[("stats",)], writes=[("stats",)])
        OP("vector", lambda e: e.reduce_sum(stats[:, 1:2], sqs, axis=AX),
           reads=[("sqs",)], writes=[("stats",)])
        OP("vector", lambda e: e.tensor_scalar(stats[:, 3:4], stats[:, 1:2], 1.0 / D, None, ALU.mult),
           reads=[("stats",)], writes=[("stats",)])
        OP("vector", lambda e: e.tensor_tensor(stats[:, 4:5], stats[:, 2:3], stats[:, 2:3], ALU.mult),
           reads=[("stats",)], writes=[("stats",)])
        OP("vector", lambda e: e.tensor_tensor(stats[:, 5:6], stats[:, 3:4], stats[:, 4:5], ALU.subtract),
           reads=[("stats",)], writes=[("stats",)])
        OP("vector", lambda e: e.tensor_scalar(stats[:, 5:6], stats[:, 5:6], 1e-5, None, ALU.add),
           reads=[("stats",)], writes=[("stats",)])
        OP("scalar", lambda e: e.activation(stats[:, 7:8], stats[:, 5:6], AF.Ln),
           reads=[("stats",)], writes=[("stats",)])
        OP("scalar", lambda e: e.activation(stats[:, 6:7], stats[:, 7:8], AF.Exp, scale=-0.5),
           reads=[("stats",)], writes=[("stats",)])
        OP("vector", lambda e, sl=sl: e.tensor_scalar(sqs, xb[:, sl, :], stats[:, 2:3], None, ALU.subtract),
           reads=[("xb", sl), ("stats",)], writes=[("sqs",)])
        OP("vector", lambda e, sl=sl: e.tensor_scalar(xnbf[:, sl, :], sqs, stats[:, 6:7], None, ALU.mult),
           reads=[("sqs",), ("stats",)], writes=[("xnbf", sl)])
        for g in range(4):
            bank = 4 + (i * 4 + g) % 4
            pv = pb[bank].bitcast(BF16)
            for t4 in range(4):
                kt = g * 4 + t4
                w, par = kt // 2, kt % 2
                src = xnbf[:, sl, 256 * w + par: 256 * w + 256: 2]
                fn = lambda e, pv=pv, t4=t4, src=src: e.transpose(
                    pv[:, t4 * 128:(t4 + 1) * 128], src, ident_s)
                PE(fn, reads=[("xnbf", sl), ("c", "ident")], writes=[("pb", bank)],
                   end=(t4 == 3))
            eng = "vector" if g % 2 == 0 else "scalar"
            isv = eng == "vector"
            for t4 in range(4):
                kt = g * 4 + t4
                fn = (lambda e, pv=pv, t4=t4, kt=kt, i=i, isv=isv:
                      (e.tensor_copy if isv else e.copy)(
                          xnT[:, kt, i * 128:(i + 1) * 128], pv[:, t4 * 128:(t4 + 1) * 128]))
                OP(eng, fn, reads=[("pb", bank)], writes=[("xnt", kt, i)])
    ctl.fence()

    if phase_limit >= 1:
        phA.reset(ph0)
        wbuf = phA.alloc("wbuf", [128, 2, 16, 128], BF16)
        stage = phA.alloc("stage", [128, 2, T], F32)
        sqs1 = phA.alloc("sqs1", [128, D], F32)
        dtT = phA.alloc("dtT", [NH, T], F32)

        for et in range(ET):
            sl = et % 2
            DMA("gpsimd", "qg", wbuf[:, sl], wipt[:, et], writes=[("wbuf", sl)])
            me = 128 if et < 65 else 32
            for tb in range(8):
                bank = (et * 8 + tb) % 8
                psv = pb[bank][0:me, :]
                for k in range(16):
                    fn = lambda e, psv=psv, sl=sl, k=k, tb=tb, me=me: e.matmul(
                        psv, wbuf[:, sl, k, 0:me], xnT[:, k, tb * 512:(tb + 1) * 512],
                        start=(k == 0), stop=(k == 15))
                    PE(fn, reads=[("wbuf", sl)], writes=[("pb", bank)], end=(k == 15))
                if et < 65:
                    eng = "vector" if tb % 2 == 0 else "scalar"
                    isv = eng == "vector"
                    fn = (lambda e, bank=bank, sl=sl, tb=tb, isv=isv:
                          (e.tensor_copy if isv else e.copy)(
                              stage[:, sl, tb * 512:(tb + 1) * 512], pb[bank]))
                    OP(eng, fn, reads=[("pb", bank)], writes=[("stage", sl, tb)])
                else:
                    pz = pb[bank][0:32, :]
                    OP("scalar", lambda e, pz=pz: e.activation(
                        sqs1[0:32, 0:512], pz, AF.Exp, bias=dtb_s),
                       reads=[("pb", bank), ("c", "dtb")], writes=[("sqs1",)])
                    OP("scalar", lambda e, tb=tb: e.activation(
                        dtT[:, tb * 512:(tb + 1) * 512], sqs1[0:32, 0:512], AF.Ln, bias=1.0),
                       reads=[("sqs1",)], writes=[("dtT", tb)])
            if et < 65:
                DMA("sync", "qs", Z[et], stage[:, sl],
                    reads=[("stage", sl, tb) for tb in range(8)])
        DMA("sync", "qs", DTsp, dtT, reads=[("dtT", tb) for tb in range(8)])
    ctl.fence()


    if phase_limit >= 2:
        phA.reset(ph0)
        pkb = phA.alloc("pkb", [128, 2, 2, 2048], U8)
        scw = phA.alloc("scw", [128, 1024], U16)
        scl = phA.alloc("scl", [128, 1024], U16)
        sch = phA.alloc("sch", [128, 1024], U16)
        scd = phA.alloc("scd", [128, 1024], U16)
        sce = phA.alloc("sce", [128, 1024], U16)
        deq = phA.alloc("deq", [128, 2, 2, 8, 256], BF16)
        upsb = phA.alloc("upsb", [128, 2, 512], F32)
        scsb = phA.alloc("scsb", [128, 2, 2, 8, 256], BF16)
        glt = phA.alloc("glt", [128, 2, 512], F32)
        istg = phA.alloc("istg", [128, 2, 2048], BF16)


        for sb in range(min(NSB, NSB_LIM)):
            sl = sb % 2
            DMA("gpsimd", "qg", pkb[:, sl, 0, :], gpk[:, sb * 2048:(sb + 1) * 2048],
                writes=[("pkb", sl, 0)])
            DMA("gpsimd", "qg", pkb[:, sl, 1, :], upk[:, sb * 2048:(sb + 1) * 2048],
                writes=[("pkb", sl, 1)])
            DMA("gpsimd", "qg", scsb[:, sl, 0], gsc[sb], writes=[("scsb", sl, 0)])
            DMA("gpsimd", "qg", scsb[:, sl, 1], usc[sb], writes=[("scsb", sl, 1)])
            for mat, half in ((0, 0), (0, 1), (1, 0), (1, 1)):
                ENG = "vector"
                bw, bl, bh, bd, be = scw, scl, sch, scd, sce
                kw, kl, kh, kd, ke = ("scw",), ("scl",), ("sch",), ("scd",), ("sce",)
                pk_ap = pkb[:, sl, mat, half * 1024:(half + 1) * 1024]
                hw0, nw = half * 4, 4
                OP(ENG, lambda e, pk_ap=pk_ap, bw=bw: e.tensor_copy(bw, pk_ap),
                   reads=[("pkb", sl, mat)], writes=[kw])
                OP(ENG, lambda e, bw=bw, bl=bl: e.tensor_scalar(bl, bw, 15, None, ALU.bitwise_and),
                   reads=[kw], writes=[kl])
                OP(ENG, lambda e, bw=bw, bh=bh: e.tensor_scalar(bh, bw, 4, None, ALU.logical_shift_right),
                   reads=[kw], writes=[kh])
                for par in range(2):
                    src, ksrc = (bl, kl) if par == 0 else (bh, kh)
                    if par == 0:
                        d12, kd12, b6, kb6 = bd, kd, bl, kl
                        ff, kf, alt, kalt = be, ke, bl, kl
                        outb, kout = bd, kd
                    else:
                        d12, kd12, b6, kb6 = bl, kl, be, ke
                        ff, kf, alt, kalt = bh, kh, be, ke
                        outb, kout = bl, kl
                    OP(ENG, lambda e, src=src, d12=d12: e.tensor_scalar(
                        d12, src, 8, 12, ALU.bitwise_and, ALU.logical_shift_left),
                       reads=[ksrc], writes=[kd12])
                    OP(ENG, lambda e, src=src, bw=bw: e.tensor_scalar(
                        bw, src, 7, None, ALU.bitwise_and),
                       reads=[ksrc], writes=[kw])
                    OP(ENG, lambda e, b6=b6, bw=bw: e.tensor_scalar(
                        b6, bw, 6, None, ALU.logical_shift_left),
                       reads=[kw], writes=[kb6])
                    OP(ENG, lambda e, b6=b6, d12=d12, ff=ff: e.scalar_tensor_tensor(
                        ff, b6, 16128, d12, ALU.add, ALU.add),
                       reads=[kb6, kd12], writes=[kf])
                    OP(ENG, lambda e, d12=d12, alt=alt, bw=bw: e.scalar_tensor_tensor(
                        alt, bw, 16128, d12, ALU.mult, ALU.add),
                       reads=[kw, kd12], writes=[kalt])
                    OP(ENG, lambda e, bw=bw: e.tensor_scalar(bw, bw, 2, None, ALU.is_lt),
                       reads=[kw], writes=[kw])
                    OP(ENG, lambda e, alt=alt, ff=ff, outb=outb, bw=bw: e.select(
                        outb, bw, alt, ff, add_drain=True),
                       reads=[kw, kalt, kf], writes=[kout])
                    OP(ENG, lambda e, outb=outb, mat=mat, par=par, sl=sl, hw0=hw0, nw=nw: e.tensor_tensor(
                        deq[:, mat, par, hw0:hw0 + nw, :].rearrange("p a b -> p (a b)"),
                        outb.bitcast(BF16),
                        scsb[:, sl, mat, hw0:hw0 + nw, :].rearrange("p a b -> p (a b)"),
                        ALU.mult),
                       reads=[kout, ("scsb", sl, mat)], writes=[("deq", mat)])
            for npar in range(2):
                sl2 = (sb * 2 + npar) % 2
                nt = sb * 2 + npar
                for tb in range(8):
                    gidx = (sb * 2 + npar) * 8 + tb
                    pair = gidx % 3
                    bg, bu = 2 * pair, 2 * pair + 1
                    for mat, bank in ((0, bg), (1, bu)):
                        for k in range(16):
                            par, w = k % 2, k // 2
                            lhsT = deq[:, mat, par, w, npar::2]
                            fn = lambda e, bank=bank, lhsT=lhsT, k=k, tb=tb: e.matmul(
                                pb[bank], lhsT, xnT[:, k, tb * 512:(tb + 1) * 512],
                                start=(k == 0), stop=(k == 15))
                            PE(fn, reads=[("deq", mat)], writes=[("pb", bank)],
                               end=(k == 15))
                    OP("scalar", lambda e, bg=bg, tb=tb: e.activation(
                        glt[:, tb % 2, :], pb[bg], AF.Gelu_apprx_tanh),
                       reads=[("pb", bg)], writes=[("glt", tb % 2)])
                    OP("scalar", lambda e, bu=bu, tb=tb: e.copy(
                        upsb[:, tb % 2, :], pb[bu]),
                       reads=[("pb", bu)], writes=[("upsb", tb % 2)])
                    OP("gpsimd", lambda e, tb=tb, sl2=sl2: e.tensor_tensor(
                        istg[:, sl2, (tb % 4) * 512:(tb % 4 + 1) * 512], glt[:, tb % 2, :],
                        upsb[:, tb % 2, :], ALU.mult),
                       reads=[("glt", tb % 2), ("upsb", tb % 2)], writes=[("istg", sl2, tb % 4)])
                    if tb % 4 == 3:
                        half = tb // 4
                        DMA("sync", "qs", Isp[nt][:, half * 2048:(half + 1) * 2048],
                            istg[:, sl2, :],
                            reads=[("istg", sl2, q) for q in range(4)])
    ctl.fence()


    if phase_limit >= 3:
        phA.reset(ph0)
        bigA.reset(big0)
        dtT3 = bigA.alloc("dtT3", [NH, T], F32)
        xcbf = bigA.alloc("xcbf", [128, 2, NCT, CH], BF16)
        onehbf = bigA.alloc("onehbf", [NH, NH * 128], BF16)
        negm_s = bigA.alloc("negm_s", [128, 2, CH], F32)
        convw_s = bigA.alloc("convw_s", [128, NCT, 4], F32)
        convb_s = bigA.alloc("convb_s", [128, NCT], F32)
        drep_s = bigA.alloc("drep_s", [128, NH], F32)
        nwt_s = bigA.alloc("nwt_s", [128, NH], F32)
        S_b = bigA.alloc("S_b", [NS, NH, HD], F32)
        Sbf = bigA.alloc("Sbf", [NS, NH, HD], BF16)
        win = phA.alloc("win", [128, 6, 260], F32)
        convt = phA.alloc("convt", [128, 2, CH], F32)
        ztile = phA.alloc("ztile", [128, NH, CH], F32)
        scoresT = phA.alloc("scoresT", [128, 2, CH], F32)
        aA = phA.alloc("aA", [NH, CH], F32)
        aC = phA.alloc("aC", [NH, CH], F32)
        naC = phA.alloc("naC", [NH, CH], F32)
        dtdC = phA.alloc("dtdC", [NH, CH], F32)
        GC = phA.alloc("GC", [NH, CH], F32)
        ahi = phA.alloc("ahi", [NH, CH], BF16)
        alo = phA.alloc("alo", [NH, CH], BF16)
        naT = phA.alloc("naT", [128, 2, NH], F32)
        dtcT = phA.alloc("dtcT", [128, 2, NH], F32)
        GT = phA.alloc("GT", [128, 2, NH], F32)
        alr = phA.alloc("alr", [1, NH], F32)
        alrE = phA.alloc("alrE", [1, NH], F32)
        alrP = phA.alloc("alrP", [1, 2, NH], BF16)
        expal_s = phA.alloc("expal_s", [NS, NH], F32)
        x_tm = phA.alloc("x_tm", [128, 4, CH], BF16)
        wq = phA.alloc("wq", [128, 2, 2, CH], BF16)
        wt1 = phA.alloc("wt1", [128, 2, CH], F32)
        lm = phA.alloc("lm", [128, 2, CH], F32)
        Bt_tm = phA.alloc("Bt_tm", [128, 2, NS], BF16)
        Bd = phA.alloc("Bd", [128, 2, 2, NS], BF16)
        Ce = phA.alloc("Ce", [NS, 2, CH], BF16)
        ctbf = phA.alloc("ctbf", [NS, CH], BF16)
        expab = phA.alloc("expab", [NS, 2, CH], F32)
        ysb = phA.alloc("ysb", [128, 2, CH], F32)
        yg = phA.alloc("yg", [128, 2, CH], F32)
        sqbf = phA.alloc("sqbf", [128, 2, CH], BF16)
        szb = phA.alloc("szb", [128, 2, CH], F32)
        rr1 = phA.alloc("rr1", [1, 2, CH], F32)
        rr2 = phA.alloc("rr2", [1, 2, CH], F32)
        rsp = phA.alloc("rsp", [1, 2, 2, CH], BF16)
        Stmp = phA.alloc("Stmp", [NS, 2, HD], F32)
        ynst = phA.alloc("ynst", [128, 4, CH], BF16)

        for dst, src, nm in [(convw_s, convw_d, "cw"), (convb_s, convb_d, "cb"),
                             (drep_s, drep_d, "dr"), (nwt_s, nwt_d, "nw"),
                             (negm_s, negm_d, "nm"), (onehbf, onehot_d, "oh"),
                             (dtT3, DTsp, "dt3")]:
            DMA("gpsimd", "qg", dst, src, writes=[("c", nm)])
        OP("vector", lambda e: e.memset(S_b, 0.0), writes=[("S",)])
        OP("vector", lambda e: e.memset(Sbf, 0.0), writes=[("Sbf",)])

        pv_xtm = pb[1].bitcast(BF16)
        for c in range(NCH):
            cs = c % 2
            c0 = c * CH
            for ct in range(NCT):
                ws = ct % 6
                if c == 0:
                    OP("vector", lambda e, ws=ws: e.memset(win[:, ws, 0:3], 0.0),
                       reads=[], writes=[("win", ws)])
                    DMA("sync", "qs", win[:, ws, 3:259], Z[32 + ct][:, 0:256],
                        writes=[("win", ws)])
                else:
                    DMA("sync", "qs", win[:, ws, 0:259], Z[32 + ct][:, c0 - 3:c0 + 256],
                        writes=[("win", ws)])
                cv = ct % 2
                OP("vector", lambda e, ws=ws, cv=cv, ct=ct: e.tensor_scalar(
                    convt[:, cv, :], win[:, ws, 0:256], convw_s[:, ct, 0:1], None, ALU.mult),
                   reads=[("win", ws), ("c", "cw")], writes=[("cvt", cv)])
                for k in range(1, 4):
                    OP("vector", lambda e, ws=ws, cv=cv, ct=ct, k=k: e.scalar_tensor_tensor(
                        convt[:, cv, :], win[:, ws, k:k + 256], convw_s[:, ct, k:k + 1],
                        convt[:, cv, :], ALU.mult, ALU.add),
                       reads=[("win", ws), ("cvt", cv), ("c", "cw")], writes=[("cvt", cv)])
                OP("scalar", lambda e, cv=cv, ct=ct, cs=cs: e.activation(
                    xcbf[:, cs, ct, :], convt[:, cv, :], AF.Silu, bias=convb_s[:, ct:ct + 1]),
                   reads=[("cvt", cv), ("c", "cb")], writes=[("xc", cs, ct)])
            for hh in range(NH):
                DMA("sync", "qs", ztile[:, hh, :], Z[hh][:, c0:c0 + CH],
                    writes=[("zt", hh)])
                OP("scalar", lambda e, hh=hh: e.activation(
                    ztile[:, hh, :], ztile[:, hh, :], AF.Sigmoid),
                   reads=[("zt", hh)], writes=[("zt", hh)])
            OP("vector", lambda e, c0=c0: e.tensor_scalar(
                aA, dtT3[:, c0:c0 + CH], A_s, None, ALU.mult),
               reads=[("c", "dt3"), ("A",)], writes=[("aA",)])
            OP("vector", lambda e: e.tensor_tensor_scan(aC, aA, aA, 0.0, ALU.add, ALU.bypass),
               reads=[("aA",)], writes=[("aC",)])
            OP("vector", lambda e: e.tensor_scalar(naC, aC, -1.0, None, ALU.mult),
               reads=[("aC",)], writes=[("naC",)])
            OP("scalar", lambda e: e.activation(dtdC, aC, AF.Exp, scale=-1.0,
                                                bias=aC[:, 255:256]),
               reads=[("aC",)], writes=[("dtd",)])
            OP("vector", lambda e, c0=c0: e.tensor_tensor(
                GC, dtT3[:, c0:c0 + CH], dtdC, ALU.mult),
               reads=[("c", "dt3"), ("dtd",)], writes=[("G",)])
            OP("scalar", lambda e: e.copy(ahi, aC), reads=[("aC",)], writes=[("ahi",)])
            OP("vector", lambda e: e.tensor_tensor(alo, aC, ahi, ALU.subtract),
               reads=[("aC",), ("ahi",)], writes=[("alo",)])
            for jt in range(2):
                for mblk in range(4):
                    pbase = 32 * mblk
                    col = 128 * jt + 32 * mblk
                    OP("vector", lambda e, jt=jt, pbase=pbase, col=col: e.transpose(
                        naT[pbase:pbase + 32, jt, :], naC[:, col:col + 32]),
                       reads=[("naC",)], writes=[("naT", jt)])
                    OP("vector", lambda e, jt=jt, pbase=pbase, col=col, c0=c0: e.transpose(
                        dtcT[pbase:pbase + 32, jt, :], dtT3[:, c0 + col:c0 + col + 32]),
                       reads=[("c", "dt3")], writes=[("dtcT", jt)])
                    OP("vector", lambda e, jt=jt, pbase=pbase, col=col: e.transpose(
                        GT[pbase:pbase + 32, jt, :], GC[:, col:col + 32]),
                       reads=[("G",)], writes=[("GT", jt)])
            DMA("gpsimd", "qg", drtmp, aC[:, 255:256], reads=[("aC",)], writes=[("drt",)])
            DMA("gpsimd", "qg", alr, drtmp[None, :], reads=[("drt",)], writes=[("alr",)])
            OP("scalar", lambda e: e.activation(alrE, alr, AF.Exp),
               reads=[("alr",)], writes=[("alrE",)])
            OP("scalar", lambda e: e.copy(alrP[:, 0, :], alrE),
               reads=[("alrE",)], writes=[("alrP",)])
            OP("vector", lambda e: e.tensor_tensor(alrP[:, 1, :], alrE, alrP[:, 0, :], ALU.subtract),
               reads=[("alrE",), ("alrP",)], writes=[("alrP",)])
            PE(lambda e: e.matmul(pb[6][0:NS, 0:NH], ones_r[:, 0:NS], alrP[:, 0, :],
                                  start=True, stop=False),
               reads=[("alrP",), ("c", "onr")], writes=[("P", 6)])
            PE(lambda e: e.matmul(pb[6][0:NS, 0:NH], ones_r[:, 0:NS], alrP[:, 1, :],
                                  start=False, stop=True),
               reads=[("alrP",)], writes=[("P", 6)], end=True)
            OP("vector", lambda e: e.tensor_copy(expal_s, pb[6][0:NS, 0:NH]),
               reads=[("P", 6)], writes=[("expal",)])
            OP("vector", lambda e, cs=cs: e.tensor_copy(ctbf, xcbf[64:128, cs, 32, :]),
               reads=[("xc", cs, 32)], writes=[("ctbf",)])
            for jt in range(2):
                PE(lambda e, jt=jt, cs=cs: e.matmul(
                    pb[0][:, jt * 256:(jt + 1) * 256],
                    xcbf[0:64, cs, 32, jt * 128:(jt + 1) * 128],
                    ctbf, start=True, stop=True),
                   reads=[("xc", cs, 32), ("ctbf",)], writes=[("P", 0)], end=True)
                OP("vector", lambda e, jt=jt: e.tensor_copy(
                    scoresT[:, jt, :], pb[0][:, jt * 256:(jt + 1) * 256]),
                   reads=[("P", 0)], writes=[("scT", jt)])
                PE(lambda e, jt=jt, cs=cs: e.transpose(
                    pv_xtm[:, 512 + 64 * jt:512 + 64 * (jt + 1)],
                    xcbf[0:64, cs, 32, jt * 128:(jt + 1) * 128], ident_s[0:64, 0:64]),
                   reads=[("xc", cs, 32), ("c", "ident")], writes=[("P", 1)], end=True)
                OP("scalar", lambda e, jt=jt: e.copy(
                    Bt_tm[:, jt, :], pv_xtm[:, 512 + 64 * jt:512 + 64 * (jt + 1)]),
                   reads=[("P", 1)], writes=[("Btm", jt)])
            for hh in range(NH):
                hr = hh % 2
                h4 = hh % 4
                abank = 0 if hr == 0 else 2
                ybank = 3 + hr
                PE(lambda e, hh=hh, abank=abank: e.matmul(
                    pb[abank][:, 0:256], onehbf[:, hh * 128:(hh + 1) * 128], ahi,
                    start=True, stop=False),
                   reads=[("ahi",), ("c", "oh")], writes=[("P", 0) if hr == 0 else ("P", 2)])
                PE(lambda e, hh=hh, abank=abank: e.matmul(
                    pb[abank][:, 0:256], onehbf[:, hh * 128:(hh + 1) * 128], alo,
                    start=False, stop=True),
                   reads=[("alo",)], writes=[("P", 0) if hr == 0 else ("P", 2)], end=True)
                for jt in range(2):
                    slot = hr * 2 + jt
                    PE(lambda e, hh=hh, jt=jt, slot=slot, cs=cs: e.transpose(
                        pv_xtm[:, slot * 128:(slot + 1) * 128],
                        xcbf[:, cs, hh, jt * 128:(jt + 1) * 128], ident_s),
                       reads=[("xc", cs, hh), ("c", "ident")], writes=[("P", 1)], end=True)
                    eng = "vector" if jt == 0 else "scalar"
                    OP(eng, (lambda e, jt=jt, slot=slot, h4=h4, isv=(jt == 0):
                             (e.tensor_copy if isv else e.copy)(
                                 x_tm[:, h4, jt * 128:(jt + 1) * 128],
                                 pv_xtm[:, slot * 128:(slot + 1) * 128])),
                       reads=[("P", 1)], writes=[("xtm", h4)])
                for jt in range(2):
                    OP("vector", lambda e, jt=jt, hr=hr, abank=abank: e.tensor_tensor(
                        wt1[:, jt, :], negm_s[:, jt, :], pb[abank][:, 0:256], ALU.add),
                       reads=[("P", 0) if hr == 0 else ("P", 2), ("c", "nm")], writes=[("wt1", jt)])
                    OP("scalar", lambda e, jt=jt, hh=hh: e.activation(
                        lm[:, jt, :], wt1[:, jt, :], AF.Exp, bias=naT[:, jt, hh:hh + 1]),
                       reads=[("wt1", jt), ("naT", jt)], writes=[("lm", jt)])
                    OP("vector", lambda e, jt=jt, hh=hh, hr=hr: e.scalar_tensor_tensor(
                        wq[:, hr, jt, :], lm[:, jt, :], dtcT[:, jt, hh:hh + 1],
                        scoresT[:, jt, :], ALU.mult, ALU.mult),
                       reads=[("lm", jt), ("dtcT", jt), ("scT", jt)], writes=[("wq", hr, jt)])
                OP("scalar", lambda e, hr=hr, abank=abank: e.activation(
                    expab[:, hr, :], pb[abank][0:NS, 0:256], AF.Exp),
                   reads=[("P", 0) if hr == 0 else ("P", 2)], writes=[("expab", hr)])
                OP("vector", lambda e, hr=hr: e.tensor_tensor(
                    Ce[:, hr, :], ctbf, expab[:, hr, :], ALU.mult),
                   reads=[("expab", hr), ("ctbf",)], writes=[("Ce", hr)])
                PE(lambda e, hr=hr, h4=h4, ybank=ybank: e.matmul(
                    pb[ybank][:, 0:256], x_tm[:, h4, 0:128], wq[:, hr, 0, :],
                    start=True, stop=False),
                   reads=[("xtm", h4), ("wq", hr, 0)], writes=[("P", 3 + hr)])
                PE(lambda e, hr=hr, h4=h4, ybank=ybank: e.matmul(
                    pb[ybank][:, 0:256], x_tm[:, h4, 128:256], wq[:, hr, 1, :],
                    start=False, stop=False),
                   reads=[("xtm", h4), ("wq", hr, 1)], writes=[("P", 3 + hr)])
                PE(lambda e, hr=hr, hh=hh, ybank=ybank: e.matmul(
                    pb[ybank][:, 0:256], Sbf[:, hh, :], Ce[:, hr, :],
                    start=False, stop=True),
                   reads=[("Sbf",), ("Ce", hr)], writes=[("P", 3 + hr)], end=True)
                for jt in range(2):
                    OP("vector", lambda e, jt=jt, hh=hh, hr=hr: e.tensor_scalar(
                        Bd[:, hr, jt, :], Bt_tm[:, jt, :], GT[:, jt, hh:hh + 1], None, ALU.mult),
                       reads=[("Btm", jt), ("GT", jt)], writes=[("Bd", hr, jt)])
                PE(lambda e, hr=hr, h4=h4: e.matmul(
                    pb[5][0:NS, 128 * hr:128 * (hr + 1)], Bd[:, hr, 0, :],
                    x_tm[:, h4, 0:128], start=True, stop=False),
                   reads=[("Bd", hr, 0), ("xtm", h4)], writes=[("P", 5)])
                PE(lambda e, hr=hr, h4=h4: e.matmul(
                    pb[5][0:NS, 128 * hr:128 * (hr + 1)], Bd[:, hr, 1, :],
                    x_tm[:, h4, 128:256], start=False, stop=True),
                   reads=[("Bd", hr, 1), ("xtm", h4)], writes=[("P", 5)], end=True)
                OP("vector", lambda e, hr=hr, hh=hh, cs=cs, ybank=ybank: e.scalar_tensor_tensor(
                    ysb[:, hr, :], xcbf[:, cs, hh, :], drep_s[:, hh:hh + 1],
                    pb[ybank][:, 0:256], ALU.mult, ALU.add),
                   reads=[("xc", cs, hh), ("c", "dr"), ("P", 3 + hr)], writes=[("ysb", hr)])
                OP("vector", lambda e, hr=hr, hh=hh: e.tensor_tensor(
                    yg[:, hr, :], ysb[:, hr, :], ztile[:, hh, :], ALU.mult),
                   reads=[("ysb", hr), ("zt", hh)], writes=[("yg", hr)])
                OP("scalar", lambda e, hr=hr: e.activation(
                    sqbf[:, hr, :], yg[:, hr, :], AF.Square),
                   reads=[("yg", hr)], writes=[("sqbf", hr)])
                PE(lambda e, hr=hr: e.matmul(
                    pb[6][0:1, 0:256], ones_c, sqbf[:, hr, :], start=True, stop=True),
                   reads=[("sqbf", hr), ("c", "onc")], writes=[("P", 6)], end=True)
                OP("vector", lambda e, hr=hr: e.tensor_scalar(
                    rr1[:, hr, :], pb[6][0:1, 0:256], 1.0 / HD, 1e-6, ALU.mult, ALU.add),
                   reads=[("P", 6)], writes=[("rr1", hr)])
                OP("scalar", lambda e, hr=hr: e.activation(
                    rr2[:, hr, :], rr1[:, hr, :], AF.Ln),
                   reads=[("rr1", hr)], writes=[("rr2", hr)])
                OP("scalar", lambda e, hr=hr: e.activation(
                    rr1[:, hr, :], rr2[:, hr, :], AF.Exp, scale=-0.5),
                   reads=[("rr2", hr)], writes=[("rr1", hr)])
                OP("scalar", lambda e, hr=hr: e.copy(rsp[:, hr, 0, :], rr1[:, hr, :]),
                   reads=[("rr1", hr)], writes=[("rsp", hr)])
                OP("vector", lambda e, hr=hr: e.tensor_tensor(
                    rsp[:, hr, 1, :], rr1[:, hr, :], rsp[:, hr, 0, :], ALU.subtract),
                   reads=[("rr1", hr), ("rsp", hr)], writes=[("rsp", hr)])
                PE(lambda e, hr=hr: e.matmul(
                    pb[7][:, 0:256], ones_r, rsp[:, hr, 0, :], start=True, stop=False),
                   reads=[("rsp", hr), ("c", "onr")], writes=[("P", 7)])
                PE(lambda e, hr=hr: e.matmul(
                    pb[7][:, 0:256], ones_r, rsp[:, hr, 1, :], start=False, stop=True),
                   reads=[("rsp", hr)], writes=[("P", 7)], end=True)
                OP("vector", lambda e, hr=hr: e.tensor_tensor(
                    ysb[:, hr, :], yg[:, hr, :], pb[7][:, 0:256], ALU.mult),
                   reads=[("yg", hr), ("P", 7)], writes=[("ysb", hr)])
                OP("vector", lambda e, hr=hr, hh=hh, h4=h4: e.tensor_scalar(
                    ynst[:, h4, :], ysb[:, hr, :], nwt_s[:, hh:hh + 1], None, ALU.mult),
                   reads=[("ysb", hr), ("c", "nw")], writes=[("ynst", h4)])
                DMA("sync", "qs", Ysp[hh][:, c0:c0 + CH], ynst[:, h4, :],
                    reads=[("ynst", h4)])
                OP("vector", lambda e, hr=hr, hh=hh: e.tensor_scalar(
                    Stmp[:, hr, :], S_b[:, hh, :], expal_s[:, hh:hh + 1], None, ALU.mult),
                   reads=[("S",), ("expal",)], writes=[("Stmp", hr)])
                OP("vector", lambda e, hr=hr, hh=hh: e.tensor_tensor(
                    S_b[:, hh, :], pb[5][0:NS, 128 * hr:128 * (hr + 1)],
                    Stmp[:, hr, :], ALU.add),
                   reads=[("P", 5), ("Stmp", hr)], writes=[("S",)])
                OP("scalar", lambda e, hh=hh: e.copy(Sbf[:, hh, :], S_b[:, hh, :]),
                   reads=[("S",)], writes=[("Sbf",)])
    ctl.fence()


    if phase_limit >= 4:
        phA.reset(ph0)
        bigA.reset(big0)
        opjT = bigA.alloc("opjT", [128, 32, D], BF16)
        ynb = phA.alloc("ynb", [128, 2, 32, 256], BF16)
        sst = phA.alloc("sst", [128, 2, D], F32)
        for kt in range(32):
            DMA("gpsimd", "qg", opjT[:, kt, :], opt[:, kt, :], writes=[("opj", kt)])
        for ts in range(16):
            sl = ts % 2
            DMA("sync", "qs", ynb[:, sl],
                Ysp[:, :, ts * 256:(ts + 1) * 256].rearrange("k p t -> p k t"),
                writes=[("ynb", sl)])
            for tt2 in range(2):
                tt = ts * 2 + tt2
                bank0 = (tt % 2) * 4
                for k in range(32):
                    for eb in range(4):
                        fn = lambda e, sl=sl, k=k, tt2=tt2, eb=eb, b=bank0: e.matmul(
                            pb[b + eb], ynb[:, sl, k, tt2 * 128:(tt2 + 1) * 128],
                            opjT[:, k, eb * 512:(eb + 1) * 512],
                            start=(k == 0), stop=(k == 31))
                        PE(fn, reads=[("ynb", sl), ("opj", k)],
                           writes=[("P", bank0 + eb)], end=(k == 31 and eb == 3))
                for eb in range(4):
                    eng = "vector" if eb % 2 == 0 else "scalar"
                    fn = (lambda e, eb=eb, b=bank0, tt=tt, isv=(eb % 2 == 0):
                          (e.tensor_copy if isv else e.copy)(
                              sst[:, tt % 2, eb * 512:(eb + 1) * 512], pb[b + eb]))
                    OP(eng, fn, reads=[("P", bank0 + eb)], writes=[("sst", tt % 2, eb)])
                DMA("sync", "qs", Ssp.rearrange("(n p) d -> n p d", p=128)[tt],
                    sst[:, tt % 2, :], reads=[("sst", tt % 2, eb) for eb in range(4)])
    ctl.fence()

    if phase_limit >= 5:
        phA.reset(ph0)
        bigA.reset(big0)
        wdT = bigA.alloc("wdT", [128, 128, 512], BF16)
        d_w = phA.alloc("d_w", [128, 512], U16)
        d_l = phA.alloc("d_l", [128, 512], U16)
        d_h = phA.alloc("d_h", [128, 512], U16)
        d_d = phA.alloc("d_d", [128, 512], U16)
        d_e = phA.alloc("d_e", [128, 512], U16)
        dpkb = phA.alloc("dpkb", [128, 2, 512], U8)
        dscb = phA.alloc("dscb", [128, 2, 512], BF16)
        inb = phA.alloc("inb", [128, 4, 8, 256], BF16)
        stb = phA.alloc("stb", [128, 2, 512], F32)
        xtb = phA.alloc("xtb", [128, 2, 512], F32)
        glb = phA.alloc("glb", [128, 2, 512], F32)
        otb = phA.alloc("otb", [128, 2, 512], F32)
        srr = Ssp.rearrange("(n p) d -> n p d", p=128)
        orr = out_d.rearrange("(n p) d -> n p d", p=128)
        for hb in range(4):
            for W in range(64):
                sl = W % 2
                DMA("gpsimd", "qg", dpkb[:, sl, :],
                    dpk[:, (hb * 64 + W) * 512:(hb * 64 + W + 1) * 512],
                    writes=[("dpkb", sl)])
                DMA("gpsimd", "qg", dscb[:, sl, :], dsc[hb, W], writes=[("dscb", sl)])
                OP("vector", lambda e, sl=sl: e.tensor_copy(d_w, dpkb[:, sl, :]),
                   reads=[("dpkb", sl)], writes=[("dw",)])
                OP("vector", lambda e: e.tensor_scalar(d_l, d_w, 15, None, ALU.bitwise_and),
                   reads=[("dw",)], writes=[("dl",)])
                OP("vector", lambda e: e.tensor_scalar(d_h, d_w, 4, None, ALU.logical_shift_right),
                   reads=[("dw",)], writes=[("dh",)])
                for par in range(2):
                    src, ksrc = (d_l, ("dl",)) if par == 0 else (d_h, ("dh",))
                    if par == 0:
                        d12, kd12, b6, kb6 = d_d, ("dd",), d_l, ("dl",)
                        ff, kf, alt, kalt = d_e, ("de",), d_l, ("dl",)
                        outb, kout = d_d, ("dd",)
                    else:
                        d12, kd12, b6, kb6 = d_l, ("dl",), d_e, ("de",)
                        ff, kf, alt, kalt = d_h, ("dh",), d_e, ("de",)
                        outb, kout = d_l, ("dl",)
                    OP("vector", lambda e, src=src: e.tensor_scalar(d_w, src, 7, None, ALU.bitwise_and),
                       reads=[ksrc], writes=[("dw",)])
                    OP("vector", lambda e, src=src, d12=d12: e.tensor_scalar(
                        d12, src, 8, 12, ALU.bitwise_and, ALU.logical_shift_left),
                       reads=[ksrc], writes=[kd12])
                    OP("vector", lambda e, b6=b6: e.tensor_scalar(
                        b6, d_w, 6, None, ALU.logical_shift_left),
                       reads=[("dw",)], writes=[kb6])
                    OP("vector", lambda e, b6=b6, d12=d12, ff=ff: e.scalar_tensor_tensor(
                        ff, b6, 16128, d12, ALU.add, ALU.add),
                       reads=[kb6, kd12], writes=[kf])
                    OP("vector", lambda e, d12=d12, alt=alt: e.scalar_tensor_tensor(
                        alt, d_w, 16128, d12, ALU.mult, ALU.add),
                       reads=[("dw",), kd12], writes=[kalt])
                    OP("vector", lambda e: e.tensor_scalar(d_w, d_w, 2, None, ALU.is_lt),
                       reads=[("dw",)], writes=[("dw",)])
                    OP("vector", lambda e, alt=alt, ff=ff, outb=outb: e.select(
                        outb, d_w, alt, ff, add_drain=True),
                       reads=[("dw",), kalt, kf], writes=[kout])
                    OP("vector", lambda e, outb=outb, W=W, par=par, sl=sl: e.tensor_tensor(
                        wdT[:, 2 * W + par, :], outb.bitcast(BF16), dscb[:, sl, :], ALU.mult),
                       reads=[kout, ("dscb", sl)], writes=[("wdT", 2 * W + par)])
            for tp in range(16):
                slp = tp % 2
                for kgi in range(19):
                    if kgi < 16:
                        kg_p = kgi
                        DMA("sync", "qs", inb[:, kg_p % 4, :, :],
                            Isp[kg_p * 8:(kg_p + 1) * 8, :, tp * 256:(tp + 1) * 256].rearrange(
                                "k p t -> p k t"),
                            writes=[("inb", kg_p % 4)])
                    if kgi < 3:
                        continue
                    kg = kgi - 3
                    for k8 in range(8):
                        k = kg * 8 + k8
                        for tt2 in range(2):
                            bank = slp * 2 + tt2
                            fn = lambda e, kg=kg, k8=k8, k=k, tt2=tt2, bank=bank: e.matmul(
                                pb[bank], inb[:, kg % 4, k8, tt2 * 128:(tt2 + 1) * 128],
                                wdT[:, k, :], start=(k == 0), stop=(k == 127))
                            PE(fn, reads=[("inb", kg % 4), ("wdT", k)],
                               writes=[("P", bank)],
                               end=(k8 == 7 and tt2 == 1))
                for tt2 in range(2):
                    tt = tp * 2 + tt2
                    bank = slp * 2 + tt2
                    DMA("sync", "qs", stb[:, tt2, :], srr[tt][:, hb * 512:(hb + 1) * 512],
                        writes=[("stb", tt2)])
                    DMA("sync", "qs", xtb[:, tt2, :],
                        xr[tt][:, hb * 512:(hb + 1) * 512], writes=[("xtb", tt2)])
                    OP("scalar", lambda e, bank=bank, tt2=tt2: e.activation(
                        glb[:, tt2, :], pb[bank], AF.Silu),
                       reads=[("P", bank)], writes=[("glb", tt2)])
                    OP("vector", lambda e, tt2=tt2: e.tensor_tensor(
                        otb[:, tt2, :], glb[:, tt2, :], stb[:, tt2, :], ALU.mult),
                       reads=[("glb", tt2), ("stb", tt2)], writes=[("otb", tt2)])
                    OP("vector", lambda e, tt2=tt2: e.tensor_tensor(
                        otb[:, tt2, :], otb[:, tt2, :], xtb[:, tt2, :], ALU.add),
                       reads=[("otb", tt2), ("xtb", tt2)], writes=[("otb", tt2)])
                    DMA("sync", "qs", orr[tt][:, hb * 512:(hb + 1) * 512],
                        otb[:, tt2, :], reads=[("otb", tt2)])
    ctl.fence()

    if dev_spill and phase_limit < 3:
        xntdump = ap(nc.dram_tensor("xntdump", [128, 16, T], BF16, kind="ExternalOutput"))
        DMA("sync", "qs", xntdump, xnT)

    ctl.fence()

    with nc.Block() as blk:
        for eng_name, meth in [("tensor", blk.tensor), ("vector", blk.vector),
                               ("scalar", blk.scalar), ("gpsimd", blk.gpsimd),
                               ("sync", blk.sync)]:
            ops = ctl.ops[eng_name]
            def body(e, ops=ops):
                for f in ops:
                    f(e)
            meth(body)
    stack.close()
    return nc


def kernel(**inputs):
    h = host_prep(inputs)
    u = np.asarray(inputs["L0_attn_u"])
    nc = build_nc()
    cores = CORE_IDS if CORE_IDS is not None else list(range(N_CORES))
    in_maps = []
    for b in range(len(cores)):
        m = dict(h)
        m["x"] = np.ascontiguousarray(u[b])
        in_maps.append(m)
    res = run_bass_kernel_spmd(nc, in_maps, core_ids=cores)
    return np.stack([r["out"] for r in res.results], axis=0)


# version 04
# speedup vs baseline: 1.0056x; 1.0056x over previous
import contextlib
import numpy as np
import ml_dtypes
import concourse.bass as bass
import concourse.mybir as mybir
from concourse.bass_utils import run_bass_kernel_spmd

F32 = mybir.dt.float32
BF16 = mybir.dt.bfloat16
U8 = mybir.dt.uint8
U16 = mybir.dt.uint16
AF = mybir.ActivationFunctionType
ALU = mybir.AluOpType
AX = mybir.AxisListType.X

T = 4096
D = 2048
EP = 8352
ET = 66
DI = 4096
CONV = 4224
NCT = 33
NH = 32
HD = 128
NS = 64
CH = 256
NCH = 16
MLP = 16384
NSB = 64
NEG = -1e30

NSB_LIM = NSB
P2_SKIP = set()
PHASE_LIMIT = 99
DEV_SPILL = False
N_CORES = 8
CORE_IDS = None

_DTSZ = {F32: 4, BF16: 2, U8: 1, U16: 2}


def _perm_d():
    p = []
    for w in range(8):
        p += [256 * w + 2 * i for i in range(128)]
        p += [256 * w + 2 * i + 1 for i in range(128)]
    return np.array(p, np.int64)


def host_prep(inputs):
    h = {}
    perm = _perm_d()
    ipw = np.asarray(inputs["L0_attn_in_proj_w"])
    wt = np.ascontiguousarray(ipw.T[perm])
    wt = np.concatenate([wt, np.zeros((D, ET * 128 - EP), np.float32)], axis=1)
    h["wipt"] = np.ascontiguousarray(
        wt.reshape(16, 128, ET, 128).transpose(1, 2, 0, 3))

    def pack_gateup(wp):
        u8 = np.asarray(wp).astype(np.int32).view(np.uint8)[:, 0::4]
        t = np.ascontiguousarray(u8.T)
        t = t.reshape(8, 128, NSB, 256).transpose(1, 2, 0, 3)
        return np.ascontiguousarray(t.reshape(128, NSB * 8 * 256))

    h["gpk"] = pack_gateup(inputs["L0_ffn_gate_W_packed"])
    h["upk"] = pack_gateup(inputs["L0_ffn_up_W_packed"])

    def pack_scales(sc):
        t = np.asarray(sc).T.reshape(8, 8, NSB, 256)
        g = t.transpose(2, 1, 0, 3)
        return np.ascontiguousarray(np.repeat(g, 16, axis=1))

    h["gsc"] = pack_scales(inputs["L0_ffn_gate_scales"])
    h["usc"] = pack_scales(inputs["L0_ffn_up_scales"])

    dpw = np.asarray(inputs["L0_ffn_down_W_packed"]).astype(np.int32).view(np.uint8)[:, 0::4]
    dw = np.ascontiguousarray(dpw.T)
    dw = dw.reshape(64, 128, 4, 512).transpose(1, 2, 0, 3)
    h["dpk"] = np.ascontiguousarray(dw.reshape(128, 4 * 64 * 512))

    dsc = np.asarray(inputs["L0_ffn_down_scales"]).T.reshape(64, 8, 4, 512)
    dsc = dsc.transpose(2, 0, 1, 3)
    h["dsc"] = np.ascontiguousarray(np.repeat(dsc, 16, axis=2))

    opw = np.asarray(inputs["L0_attn_o_proj_w"])
    ot = np.ascontiguousarray(opw.T)
    h["opt"] = np.ascontiguousarray(ot.reshape(32, 128, D).transpose(1, 0, 2))

    cw = np.asarray(inputs["L0_attn_conv_w"])
    h["convw"] = np.ascontiguousarray(cw.reshape(NCT, 128, 4).transpose(1, 0, 2))
    h["convb"] = np.ascontiguousarray(
        np.asarray(inputs["L0_attn_conv_b"]).reshape(NCT, 128).T)

    h["alog"] = np.ascontiguousarray(np.asarray(inputs["L0_attn_A_log"]).reshape(NH, 1)).astype(np.float32)
    h["dtb"] = np.ascontiguousarray(np.asarray(inputs["L0_attn_dt_bias"]).reshape(NH, 1)).astype(np.float32)
    h["drep"] = np.ascontiguousarray(
        np.broadcast_to(np.asarray(inputs["L0_attn_D_skip"])[None, :], (128, NH))).astype(np.float32)
    h["nwt"] = np.ascontiguousarray(
        np.asarray(inputs["L0_attn_norm_w"]).reshape(NH, 128).T).astype(np.float32)

    h["identbf"] = np.eye(128, dtype=ml_dtypes.bfloat16)
    jj = np.arange(128)[:, None]
    ii = np.arange(CH)[None, :]
    nm = np.zeros((128, 2, CH), np.float32)
    nm[:, 0, :] = np.where(jj <= ii, 0.0, NEG)
    nm[:, 1, :] = np.where(jj + 128 <= ii, 0.0, NEG)
    h["negmask"] = nm
    oh = np.zeros((NH, NH * 128), np.float32)
    for hh in range(NH):
        oh[hh, hh * 128:(hh + 1) * 128] = 1.0
    h["onehot"] = oh
    h["onehot"] = h["onehot"].astype(ml_dtypes.bfloat16)
    h["ones_c"] = np.ones((128, 1), ml_dtypes.bfloat16)
    h["ones_r"] = np.ones((1, 128), ml_dtypes.bfloat16)
    return h


class Arena:

    def __init__(self, nc, base, limit):
        self.nc, self.base, self.limit, self.cur = nc, base, limit, base
        self.n = 0

    def alloc(self, name, shape, dtype):
        nbytes = int(np.prod(shape[1:])) * _DTSZ[dtype]
        nbytes = (nbytes + 31) // 32 * 32
        assert self.cur + nbytes <= self.limit, \
            f"arena overflow at {name}: {self.cur + nbytes} > {self.limit}"
        t = self.nc.alloc_sbuf_tensor_at(f"{name}_{self.n}", shape, dtype, offset=self.cur)
        self.n += 1
        self.cur += nbytes
        return t.ap()

    def mark(self):
        return self.cur

    def reset(self, saved=None):
        self.cur = self.base if saved is None else saved


class Ctl:

    ENGINES = ("tensor", "vector", "scalar", "gpsimd", "sync")
    SELF = {"vector": "sv", "scalar": "ss", "gpsimd": "sg"}

    LANES = {"qs": 6, "qg": 6, "qa": 4}

    def __init__(self, sems):
        self.ops = {e: [] for e in self.ENGINES}
        self.cnt = {}
        self.sems = sems
        self.lanes = {q: [f"{q}{i}" for i in range(n)] for q, n in self.LANES.items()}
        self.lane_rr = {q: 0 for q in self.LANES}
        self.last_w = {}
        self.readers = {}
        self.watermark = {e: {} for e in self.ENGINES}
        self.pe_pending_w = []
        self.pe_pending_r = []

    def _bump(self, sem, k=1):
        self.cnt[sem] = self.cnt.get(sem, 0) + k
        return self.cnt[sem]

    def val(self, sem):
        return self.cnt.get(sem, 0)

    def _edge(self, eng, dep):
        if dep is None:
            return
        sem, v = dep
        if v <= self.watermark[eng].get(sem, 0):
            return
        self.watermark[eng][sem] = v
        s = self.sems[sem]
        self.ops[eng].append(lambda e, s=s, v=v: e.wait_ge(s, v))

    def _deps(self, eng, reads, writes):
        for key in reads:
            self._edge(eng, self.last_w.get(key))
        for key in writes:
            self._edge(eng, self.last_w.get(key))
            for d in self.readers.get(key, ()):
                self._edge(eng, d)

    def _commit(self, reads, writes, completion):
        for key in writes:
            self.last_w[key] = completion
            self.readers[key] = []
        for key in reads:
            self.readers.setdefault(key, []).append(completion)

    def op(self, eng, fn, reads=(), writes=()):
        sem = self.SELF[eng]
        self._deps(eng, reads, writes)
        v = self._bump(sem)
        s = self.sems[sem]
        self.ops[eng].append(lambda e, fn=fn, s=s: fn(e).then_inc(s, 1))
        comp = (sem, v)
        self._commit(reads, writes, comp)
        return comp

    def pe(self, fn, reads=(), writes=(), end=False):
        self._deps("tensor", reads, writes)
        self.pe_pending_w.extend(writes)
        self.pe_pending_r.extend(reads)
        if end:
            v = self._bump("pe")
            s = self.sems["pe"]
            self.ops["tensor"].append(lambda e, fn=fn, s=s: fn(e).then_inc(s, 1))
            comp = ("pe", v)
            self._commit(self.pe_pending_r, self.pe_pending_w, comp)
            self.pe_pending_w, self.pe_pending_r = [], []
            return comp
        self.ops["tensor"].append(fn)
        return None

    def dma(self, eng, q, out, in_, reads=(), writes=()):
        lanes = self.lanes[q]
        i = self.lane_rr[q]
        self.lane_rr[q] = (i + 1) % len(lanes)
        lane = lanes[i]
        prev = self.val(lane)
        self._edge(eng, (lane, prev) if prev > 0 else None)
        self._deps(eng, reads, writes)
        v = self._bump(lane, 16)
        s = self.sems[lane]
        self.ops[eng].append(
            lambda e, out=out, in_=in_, s=s: e.dma_start(out=out, in_=in_).then_inc(s, 16))
        comp = (lane, v)
        self._commit(reads, writes, comp)
        return comp

    def fence(self):
        deps = []
        for sem in ("sv", "ss", "sg", "pe"):
            if self.val(sem) > 0:
                deps.append((sem, self.val(sem)))
        for q in self.lanes:
            for lane in self.lanes[q]:
                if self.val(lane) > 0:
                    deps.append((lane, self.val(lane)))
        for eng in self.ENGINES:
            for d in deps:
                self._edge(eng, d)
        self.last_w = {}
        self.readers = {}


def build_nc(phase_limit=None, dev_spill=None):
    if phase_limit is None:
        phase_limit = PHASE_LIMIT
    if dev_spill is None:
        dev_spill = DEV_SPILL
    nc = bass.Bass("TRN2", target_bir_lowering=False)

    ap = lambda t: t.ap()
    x_in = ap(nc.dram_tensor("x", [T, D], F32, kind="ExternalInput"))
    wipt = ap(nc.dram_tensor("wipt", [128, ET, 16, 128], F32, kind="ExternalInput"))
    gpk = ap(nc.dram_tensor("gpk", [128, NSB * 2048], U8, kind="ExternalInput"))
    upk = ap(nc.dram_tensor("upk", [128, NSB * 2048], U8, kind="ExternalInput"))
    gsc = ap(nc.dram_tensor("gsc", [64, 128, 8, 256], F32, kind="ExternalInput"))
    usc = ap(nc.dram_tensor("usc", [64, 128, 8, 256], F32, kind="ExternalInput"))
    dpk = ap(nc.dram_tensor("dpk", [128, 4 * 64 * 512], U8, kind="ExternalInput"))
    dsc = ap(nc.dram_tensor("dsc", [4, 64, 128, 512], F32, kind="ExternalInput"))
    opt = ap(nc.dram_tensor("opt", [128, 32, D], F32, kind="ExternalInput"))
    convw_d = ap(nc.dram_tensor("convw", [128, NCT, 4], F32, kind="ExternalInput"))
    convb_d = ap(nc.dram_tensor("convb", [128, NCT], F32, kind="ExternalInput"))
    alog_d = ap(nc.dram_tensor("alog", [NH, 1], F32, kind="ExternalInput"))
    dtb_d = ap(nc.dram_tensor("dtb", [NH, 1], F32, kind="ExternalInput"))
    drep_d = ap(nc.dram_tensor("drep", [128, NH], F32, kind="ExternalInput"))
    nwt_d = ap(nc.dram_tensor("nwt", [128, NH], F32, kind="ExternalInput"))
    ident_d = ap(nc.dram_tensor("identbf", [128, 128], BF16, kind="ExternalInput"))
    negm_d = ap(nc.dram_tensor("negmask", [128, 2, CH], F32, kind="ExternalInput"))
    onehot_d = ap(nc.dram_tensor("onehot", [NH, NH * 128], BF16, kind="ExternalInput"))
    ones_c_d = ap(nc.dram_tensor("ones_c", [128, 1], BF16, kind="ExternalInput"))
    ones_r_d = ap(nc.dram_tensor("ones_r", [1, 128], BF16, kind="ExternalInput"))

    skind = "ExternalOutput" if dev_spill else "Internal"
    Z = ap(nc.dram_tensor("Zsp", [65, 128, T], F32, kind=skind))
    Isp = ap(nc.dram_tensor("Isp", [128, 128, T], BF16, kind=skind))
    Ysp = ap(nc.dram_tensor("Ysp", [32, 128, T], BF16, kind=skind))
    Ssp = ap(nc.dram_tensor("Ssp", [T, D], F32, kind=skind))
    DTsp = ap(nc.dram_tensor("DTsp", [NH, T], F32, kind=skind))
    drtmp = ap(nc.dram_tensor("drtmp", [NH], F32, kind="Internal"))
    out_d = ap(nc.dram_tensor("out", [T, D], F32, kind="ExternalOutput"))

    sbuf = lambda name, shape, dt: nc.alloc_sbuf_tensor(name, shape, dt).ap()
    ident_s = sbuf("ident_s", [128, 128], BF16)
    ones_c = sbuf("ones_c_s", [128, 1], BF16)
    ones_r = sbuf("ones_r_s", [1, 128], BF16)
    alog_s = sbuf("alog_s", [NH, 1], F32)
    dtb_s = sbuf("dtb_s", [NH, 1], F32)
    A_s = sbuf("A_s", [NH, 1], F32)

    import concourse.bass as _b
    perm_end = nc.bytes_allocated(_b.MemorySpace.SBUF, "left")
    cap = nc.sbuf_top
    BIGB = (perm_end + 31) // 32 * 32
    PHB = BIGB + 131072
    assert PHB < cap, (BIGB, cap)
    bigA = Arena(nc, BIGB, PHB)
    phA = Arena(nc, PHB, cap)

    big0 = bigA.mark()
    xnT = bigA.alloc("xnT", [128, 16, T], BF16)

    pb = [nc.alloc_psum_tensor(f"pb{i}", [128, 512], F32).ap() for i in range(8)]

    stack = contextlib.ExitStack()
    sems = {}
    semnames = ["pe", "sv", "ss", "sg"]
    for q, n in Ctl.LANES.items():
        semnames += [f"{q}{i}" for i in range(n)]
    for name in semnames:
        sems[name] = stack.enter_context(nc.semaphore(name))
    ctl = Ctl(sems)
    OP, PE, DMA = ctl.op, ctl.pe, ctl.dma

    for dst, src, nm in [(ident_s, ident_d, "ident"), (ones_c, ones_c_d, "onc"),
                         (ones_r, ones_r_d, "onr"), (alog_s, alog_d, "alog"),
                         (dtb_s, dtb_d, "dtb")]:
        DMA("gpsimd", "qg", dst, src, writes=[("c", nm)])
    OP("scalar", lambda e: e.activation(A_s, alog_s, AF.Exp),
       reads=[("c", "alog")], writes=[("A",)])
    OP("vector", lambda e: e.tensor_scalar(A_s, A_s, -1.0, None, ALU.mult),
       reads=[("A",)], writes=[("A",)])

    ph0 = phA.mark()
    xb = phA.alloc("xb", [128, 2, D], F32)
    xnbf = phA.alloc("xnbf", [128, 2, D], BF16)
    stats = phA.alloc("stats", [128, 8], F32)
    sqs = phA.alloc("sqs", [128, D], F32)

    NT = T // 128
    xr = x_in.rearrange("(n p) d -> n p d", p=128)
    for i in range(NT):
        sl = i % 2
        DMA("sync", "qs", xb[:, sl, :], xr[i], writes=[("xb", sl)])
        OP("scalar", lambda e, sl=sl: e.activation(sqs, xb[:, sl, :], AF.Square),
           reads=[("xb", sl)], writes=[("sqs",)])
        OP("vector", lambda e, sl=sl: e.reduce_sum(stats[:, 0:1], xb[:, sl, :], axis=AX),
           reads=[("xb", sl)], writes=[("stats",)])
        OP("vector", lambda e: e.tensor_scalar(stats[:, 2:3], stats[:, 0:1], 1.0 / D, None, ALU.mult),
           reads=[("stats",)], writes=[("stats",)])
        OP("vector", lambda e: e.reduce_sum(stats[:, 1:2], sqs, axis=AX),
           reads=[("sqs",)], writes=[("stats",)])
        OP("vector", lambda e: e.tensor_scalar(stats[:, 3:4], stats[:, 1:2], 1.0 / D, None, ALU.mult),
           reads=[("stats",)], writes=[("stats",)])
        OP("vector", lambda e: e.tensor_tensor(stats[:, 4:5], stats[:, 2:3], stats[:, 2:3], ALU.mult),
           reads=[("stats",)], writes=[("stats",)])
        OP("vector", lambda e: e.tensor_tensor(stats[:, 5:6], stats[:, 3:4], stats[:, 4:5], ALU.subtract),
           reads=[("stats",)], writes=[("stats",)])
        OP("vector", lambda e: e.tensor_scalar(stats[:, 5:6], stats[:, 5:6], 1e-5, None, ALU.add),
           reads=[("stats",)], writes=[("stats",)])
        OP("scalar", lambda e: e.activation(stats[:, 7:8], stats[:, 5:6], AF.Ln),
           reads=[("stats",)], writes=[("stats",)])
        OP("scalar", lambda e: e.activation(stats[:, 6:7], stats[:, 7:8], AF.Exp, scale=-0.5),
           reads=[("stats",)], writes=[("stats",)])
        OP("vector", lambda e, sl=sl: e.tensor_scalar(sqs, xb[:, sl, :], stats[:, 2:3], None, ALU.subtract),
           reads=[("xb", sl), ("stats",)], writes=[("sqs",)])
        OP("vector", lambda e, sl=sl: e.tensor_scalar(xnbf[:, sl, :], sqs, stats[:, 6:7], None, ALU.mult),
           reads=[("sqs",), ("stats",)], writes=[("xnbf", sl)])
        for g in range(4):
            bank = 4 + (i * 4 + g) % 4
            pv = pb[bank].bitcast(BF16)
            for t4 in range(4):
                kt = g * 4 + t4
                w, par = kt // 2, kt % 2
                src = xnbf[:, sl, 256 * w + par: 256 * w + 256: 2]
                fn = lambda e, pv=pv, t4=t4, src=src: e.transpose(
                    pv[:, t4 * 128:(t4 + 1) * 128], src, ident_s)
                PE(fn, reads=[("xnbf", sl), ("c", "ident")], writes=[("pb", bank)],
                   end=(t4 == 3))
            eng = "vector" if g % 2 == 0 else "scalar"
            isv = eng == "vector"
            for t4 in range(4):
                kt = g * 4 + t4
                fn = (lambda e, pv=pv, t4=t4, kt=kt, i=i, isv=isv:
                      (e.tensor_copy if isv else e.copy)(
                          xnT[:, kt, i * 128:(i + 1) * 128], pv[:, t4 * 128:(t4 + 1) * 128]))
                OP(eng, fn, reads=[("pb", bank)], writes=[("xnt", kt, i)])
    ctl.fence()

    if phase_limit >= 1:
        phA.reset(ph0)
        wbuf = phA.alloc("wbuf", [128, 2, 16, 128], BF16)
        stage = phA.alloc("stage", [128, 2, T], F32)
        sqs1 = phA.alloc("sqs1", [128, D], F32)
        dtT = phA.alloc("dtT", [NH, T], F32)

        for et in range(ET):
            sl = et % 2
            DMA("gpsimd", "qg", wbuf[:, sl], wipt[:, et], writes=[("wbuf", sl)])
            me = 128 if et < 65 else 32
            for tb in range(8):
                bank = (et * 8 + tb) % 8
                psv = pb[bank][0:me, :]
                for k in range(16):
                    fn = lambda e, psv=psv, sl=sl, k=k, tb=tb, me=me: e.matmul(
                        psv, wbuf[:, sl, k, 0:me], xnT[:, k, tb * 512:(tb + 1) * 512],
                        start=(k == 0), stop=(k == 15))
                    PE(fn, reads=[("wbuf", sl)], writes=[("pb", bank)], end=(k == 15))
                if et < 65:
                    eng = "vector" if tb % 2 == 0 else "scalar"
                    isv = eng == "vector"
                    fn = (lambda e, bank=bank, sl=sl, tb=tb, isv=isv:
                          (e.tensor_copy if isv else e.copy)(
                              stage[:, sl, tb * 512:(tb + 1) * 512], pb[bank]))
                    OP(eng, fn, reads=[("pb", bank)], writes=[("stage", sl, tb)])
                else:
                    pz = pb[bank][0:32, :]
                    OP("scalar", lambda e, pz=pz: e.activation(
                        sqs1[0:32, 0:512], pz, AF.Exp, bias=dtb_s),
                       reads=[("pb", bank), ("c", "dtb")], writes=[("sqs1",)])
                    OP("scalar", lambda e, tb=tb: e.activation(
                        dtT[:, tb * 512:(tb + 1) * 512], sqs1[0:32, 0:512], AF.Ln, bias=1.0),
                       reads=[("sqs1",)], writes=[("dtT", tb)])
            if et < 65:
                DMA("sync", "qs", Z[et], stage[:, sl],
                    reads=[("stage", sl, tb) for tb in range(8)])
        DMA("sync", "qs", DTsp, dtT, reads=[("dtT", tb) for tb in range(8)])
    ctl.fence()


    if phase_limit >= 2:
        phA.reset(ph0)
        pkb = phA.alloc("pkb", [128, 2, 2, 2048], U8)
        scw = phA.alloc("scw", [128, 1024], U16)
        scl = phA.alloc("scl", [128, 1024], U16)
        sch = phA.alloc("sch", [128, 1024], U16)
        scd = phA.alloc("scd", [128, 1024], U16)
        sce = phA.alloc("sce", [128, 1024], U16)
        deq = phA.alloc("deq", [128, 2, 2, 2, 8, 256], BF16)
        upsb = phA.alloc("upsb", [128, 2, 512], F32)
        scsb = phA.alloc("scsb", [128, 1, 2, 8, 256], BF16)
        glt = phA.alloc("glt", [128, 2, 512], F32)
        istg = phA.alloc("istg", [128, 2, 2048], BF16)


        for sb in range(min(NSB, NSB_LIM)):
            sl = sb % 2
            DMA("gpsimd", "qg", pkb[:, sl, 0, :], gpk[:, sb * 2048:(sb + 1) * 2048],
                writes=[("pkb", sl, 0)])
            DMA("gpsimd", "qg", pkb[:, sl, 1, :], upk[:, sb * 2048:(sb + 1) * 2048],
                writes=[("pkb", sl, 1)])
            DMA("gpsimd", "qg", scsb[:, 0, 0], gsc[sb], writes=[("scsb", 0)])
            DMA("gpsimd", "qg", scsb[:, 0, 1], usc[sb], writes=[("scsb", 1)])
            for mat, half in ((0, 0), (0, 1), (1, 0), (1, 1)):
                ENG = "vector"
                bw, bl, bh, bd, be = scw, scl, sch, scd, sce
                kw, kl, kh, kd, ke = ("scw",), ("scl",), ("sch",), ("scd",), ("sce",)
                pk_ap = pkb[:, sl, mat, half * 1024:(half + 1) * 1024]
                hw0, nw = half * 4, 4
                OP(ENG, lambda e, pk_ap=pk_ap, bw=bw: e.tensor_copy(bw, pk_ap),
                   reads=[("pkb", sl, mat)], writes=[kw])
                OP(ENG, lambda e, bw=bw, bl=bl: e.tensor_scalar(bl, bw, 15, None, ALU.bitwise_and),
                   reads=[kw], writes=[kl])
                OP(ENG, lambda e, bw=bw, bh=bh: e.tensor_scalar(bh, bw, 4, None, ALU.logical_shift_right),
                   reads=[kw], writes=[kh])
                for par in range(2):
                    src, ksrc = (bl, kl) if par == 0 else (bh, kh)
                    if par == 0:
                        d12, kd12, b6, kb6 = bd, kd, bl, kl
                        ff, kf, alt, kalt = be, ke, bl, kl
                        outb, kout = bd, kd
                    else:
                        d12, kd12, b6, kb6 = bl, kl, be, ke
                        ff, kf, alt, kalt = bh, kh, be, ke
                        outb, kout = bl, kl
                    OP(ENG, lambda e, src=src, d12=d12: e.tensor_scalar(
                        d12, src, 8, 12, ALU.bitwise_and, ALU.logical_shift_left),
                       reads=[ksrc], writes=[kd12])
                    OP(ENG, lambda e, src=src, bw=bw: e.tensor_scalar(
                        bw, src, 7, None, ALU.bitwise_and),
                       reads=[ksrc], writes=[kw])
                    OP(ENG, lambda e, b6=b6, bw=bw: e.tensor_scalar(
                        b6, bw, 6, None, ALU.logical_shift_left),
                       reads=[kw], writes=[kb6])
                    OP(ENG, lambda e, b6=b6, d12=d12, ff=ff: e.scalar_tensor_tensor(
                        ff, b6, 16128, d12, ALU.add, ALU.add),
                       reads=[kb6, kd12], writes=[kf])
                    OP(ENG, lambda e, d12=d12, alt=alt, bw=bw: e.scalar_tensor_tensor(
                        alt, bw, 16128, d12, ALU.mult, ALU.add),
                       reads=[kw, kd12], writes=[kalt])
                    OP(ENG, lambda e, bw=bw: e.tensor_scalar(bw, bw, 2, None, ALU.is_lt),
                       reads=[kw], writes=[kw])
                    OP(ENG, lambda e, alt=alt, ff=ff, outb=outb, bw=bw: e.select(
                        outb, bw, alt, ff, add_drain=True),
                       reads=[kw, kalt, kf], writes=[kout])
                    OP(ENG, lambda e, outb=outb, mat=mat, par=par, sl=sl, hw0=hw0, nw=nw: e.tensor_tensor(
                        deq[:, sl, mat, par, hw0:hw0 + nw, :].rearrange("p a b -> p (a b)"),
                        outb.bitcast(BF16),
                        scsb[:, 0, mat, hw0:hw0 + nw, :].rearrange("p a b -> p (a b)"),
                        ALU.mult),
                       reads=[kout, ("scsb", mat)], writes=[("deq", sl, mat)])
            for npar in range(2):
                sl2 = (sb * 2 + npar) % 2
                nt = sb * 2 + npar
                for tb in range(8):
                    gidx = (sb * 2 + npar) * 8 + tb
                    pair = gidx % 3
                    bg, bu = 2 * pair, 2 * pair + 1
                    for mat, bank in ((0, bg), (1, bu)):
                        for k in range(16):
                            par, w = k % 2, k // 2
                            lhsT = deq[:, sl, mat, par, w, npar::2]
                            fn = lambda e, bank=bank, lhsT=lhsT, k=k, tb=tb: e.matmul(
                                pb[bank], lhsT, xnT[:, k, tb * 512:(tb + 1) * 512],
                                start=(k == 0), stop=(k == 15))
                            PE(fn, reads=[("deq", sl, mat)], writes=[("pb", bank)],
                               end=(k == 15))
                    OP("scalar", lambda e, bg=bg, tb=tb: e.activation(
                        glt[:, tb % 2, :], pb[bg], AF.Gelu_apprx_tanh),
                       reads=[("pb", bg)], writes=[("glt", tb % 2)])
                    OP("scalar", lambda e, bu=bu, tb=tb: e.copy(
                        upsb[:, tb % 2, :], pb[bu]),
                       reads=[("pb", bu)], writes=[("upsb", tb % 2)])
                    OP("gpsimd", lambda e, tb=tb, sl2=sl2: e.tensor_tensor(
                        istg[:, sl2, (tb % 4) * 512:(tb % 4 + 1) * 512], glt[:, tb % 2, :],
                        upsb[:, tb % 2, :], ALU.mult),
                       reads=[("glt", tb % 2), ("upsb", tb % 2)], writes=[("istg", sl2, tb % 4)])
                    if tb % 4 == 3:
                        half = tb // 4
                        DMA("sync", "qs", Isp[nt][:, half * 2048:(half + 1) * 2048],
                            istg[:, sl2, :],
                            reads=[("istg", sl2, q) for q in range(4)])
    ctl.fence()


    if phase_limit >= 3:
        phA.reset(ph0)
        bigA.reset(big0)
        dtT3 = bigA.alloc("dtT3", [NH, T], F32)
        xcbf = bigA.alloc("xcbf", [128, 2, NCT, CH], BF16)
        onehbf = bigA.alloc("onehbf", [NH, NH * 128], BF16)
        negm_s = bigA.alloc("negm_s", [128, 2, CH], F32)
        convw_s = bigA.alloc("convw_s", [128, NCT, 4], F32)
        convb_s = bigA.alloc("convb_s", [128, NCT], F32)
        drep_s = bigA.alloc("drep_s", [128, NH], F32)
        nwt_s = bigA.alloc("nwt_s", [128, NH], F32)
        ztile = bigA.alloc("ztile", [128, NH, CH], F32)
        S_b = bigA.alloc("S_b", [NS, NH, HD], F32)
        Sbf = bigA.alloc("Sbf", [NS, NH, HD], BF16)
        win = phA.alloc("win", [128, 6, 260], F32)
        convt = phA.alloc("convt", [128, 2, CH], F32)

        scoresT = phA.alloc("scoresT", [128, 2, CH], F32)
        aA = phA.alloc("aA", [NH, CH], F32)
        aC = phA.alloc("aC", [NH, CH], F32)
        naC = phA.alloc("naC", [NH, CH], F32)
        dtdC = phA.alloc("dtdC", [NH, CH], F32)
        GC = phA.alloc("GC", [NH, CH], F32)
        ahi = phA.alloc("ahi", [NH, CH], BF16)
        alo = phA.alloc("alo", [NH, CH], BF16)
        naT = phA.alloc("naT", [128, 2, NH], F32)
        dtcT = phA.alloc("dtcT", [128, 2, NH], F32)
        GT = phA.alloc("GT", [128, 2, NH], F32)
        alr = phA.alloc("alr", [1, NH], F32)
        alrE = phA.alloc("alrE", [1, NH], F32)
        alrP = phA.alloc("alrP", [1, 2, NH], BF16)
        expal_s = phA.alloc("expal_s", [NS, NH], F32)
        x_tm = phA.alloc("x_tm", [128, 4, CH], BF16)
        wq = phA.alloc("wq", [128, 4, 2, CH], BF16)
        wt1 = phA.alloc("wt1", [128, 4, CH], F32)
        lm = phA.alloc("lm", [128, 4, CH], F32)
        Bt_tm = phA.alloc("Bt_tm", [128, 2, NS], BF16)
        Bd = phA.alloc("Bd", [128, 4, 2, NS], BF16)
        Ce = phA.alloc("Ce", [NS, 4, CH], BF16)
        ctbf = phA.alloc("ctbf", [NS, CH], BF16)
        expab = phA.alloc("expab", [NS, 4, CH], F32)
        ysb = phA.alloc("ysb", [128, 4, CH], F32)
        yg = phA.alloc("yg", [128, 4, CH], F32)
        sqbf = phA.alloc("sqbf", [128, 4, CH], BF16)

        rr1 = phA.alloc("rr1", [1, 4, CH], F32)
        rr2 = phA.alloc("rr2", [1, 4, CH], F32)
        rsp = phA.alloc("rsp", [1, 4, 2, CH], BF16)
        Stmp = phA.alloc("Stmp", [NS, 4, HD], F32)
        ynst = phA.alloc("ynst", [128, 4, CH], BF16)

        for dst, src, nm in [(convw_s, convw_d, "cw"), (convb_s, convb_d, "cb"),
                             (drep_s, drep_d, "dr"), (nwt_s, nwt_d, "nw"),
                             (negm_s, negm_d, "nm"), (onehbf, onehot_d, "oh"),
                             (dtT3, DTsp, "dt3")]:
            DMA("gpsimd", "qg", dst, src, writes=[("c", nm)])
        OP("vector", lambda e: e.memset(S_b, 0.0), writes=[("S",)])
        OP("vector", lambda e: e.memset(Sbf, 0.0), writes=[("Sbf",)])

        pv_xtm = pb[1].bitcast(BF16)
        for c in range(NCH):
            cs = c % 2
            c0 = c * CH
            for ct in range(NCT):
                ws = ct % 6
                if c == 0:
                    OP("vector", lambda e, ws=ws: e.memset(win[:, ws, 0:3], 0.0),
                       reads=[], writes=[("win", ws)])
                    DMA("sync", "qs", win[:, ws, 3:259], Z[32 + ct][:, 0:256],
                        writes=[("win", ws)])
                else:
                    DMA("sync", "qs", win[:, ws, 0:259], Z[32 + ct][:, c0 - 3:c0 + 256],
                        writes=[("win", ws)])
                cv = ct % 2
                OP("vector", lambda e, ws=ws, cv=cv, ct=ct: e.tensor_scalar(
                    convt[:, cv, :], win[:, ws, 0:256], convw_s[:, ct, 0:1], None, ALU.mult),
                   reads=[("win", ws), ("c", "cw")], writes=[("cvt", cv)])
                for k in range(1, 4):
                    OP("vector", lambda e, ws=ws, cv=cv, ct=ct, k=k: e.scalar_tensor_tensor(
                        convt[:, cv, :], win[:, ws, k:k + 256], convw_s[:, ct, k:k + 1],
                        convt[:, cv, :], ALU.mult, ALU.add),
                       reads=[("win", ws), ("cvt", cv), ("c", "cw")], writes=[("cvt", cv)])
                OP("scalar", lambda e, cv=cv, ct=ct, cs=cs: e.activation(
                    xcbf[:, cs, ct, :], convt[:, cv, :], AF.Silu, bias=convb_s[:, ct:ct + 1]),
                   reads=[("cvt", cv), ("c", "cb")], writes=[("xc", cs, ct)])
            for hh in range(NH):
                DMA("sync", "qs", ztile[:, hh, :], Z[hh][:, c0:c0 + CH],
                    writes=[("zt", hh)])
                OP("scalar", lambda e, hh=hh: e.activation(
                    ztile[:, hh, :], ztile[:, hh, :], AF.Sigmoid),
                   reads=[("zt", hh)], writes=[("zt", hh)])
            OP("vector", lambda e, c0=c0: e.tensor_scalar(
                aA, dtT3[:, c0:c0 + CH], A_s, None, ALU.mult),
               reads=[("c", "dt3"), ("A",)], writes=[("aA",)])
            OP("vector", lambda e: e.tensor_tensor_scan(aC, aA, aA, 0.0, ALU.add, ALU.bypass),
               reads=[("aA",)], writes=[("aC",)])
            OP("vector", lambda e: e.tensor_scalar(naC, aC, -1.0, None, ALU.mult),
               reads=[("aC",)], writes=[("naC",)])
            OP("scalar", lambda e: e.activation(dtdC, aC, AF.Exp, scale=-1.0,
                                                bias=aC[:, 255:256]),
               reads=[("aC",)], writes=[("dtd",)])
            OP("vector", lambda e, c0=c0: e.tensor_tensor(
                GC, dtT3[:, c0:c0 + CH], dtdC, ALU.mult),
               reads=[("c", "dt3"), ("dtd",)], writes=[("G",)])
            OP("scalar", lambda e: e.copy(ahi, aC), reads=[("aC",)], writes=[("ahi",)])
            OP("vector", lambda e: e.tensor_tensor(alo, aC, ahi, ALU.subtract),
               reads=[("aC",), ("ahi",)], writes=[("alo",)])
            for jt in range(2):
                for mblk in range(4):
                    pbase = 32 * mblk
                    col = 128 * jt + 32 * mblk
                    OP("vector", lambda e, jt=jt, pbase=pbase, col=col: e.transpose(
                        naT[pbase:pbase + 32, jt, :], naC[:, col:col + 32]),
                       reads=[("naC",)], writes=[("naT", jt)])
                    OP("vector", lambda e, jt=jt, pbase=pbase, col=col, c0=c0: e.transpose(
                        dtcT[pbase:pbase + 32, jt, :], dtT3[:, c0 + col:c0 + col + 32]),
                       reads=[("c", "dt3")], writes=[("dtcT", jt)])
                    OP("vector", lambda e, jt=jt, pbase=pbase, col=col: e.transpose(
                        GT[pbase:pbase + 32, jt, :], GC[:, col:col + 32]),
                       reads=[("G",)], writes=[("GT", jt)])
            DMA("gpsimd", "qg", drtmp, aC[:, 255:256], reads=[("aC",)], writes=[("drt",)])
            DMA("gpsimd", "qg", alr, drtmp[None, :], reads=[("drt",)], writes=[("alr",)])
            OP("scalar", lambda e: e.activation(alrE, alr, AF.Exp),
               reads=[("alr",)], writes=[("alrE",)])
            OP("scalar", lambda e: e.copy(alrP[:, 0, :], alrE),
               reads=[("alrE",)], writes=[("alrP",)])
            OP("vector", lambda e: e.tensor_tensor(alrP[:, 1, :], alrE, alrP[:, 0, :], ALU.subtract),
               reads=[("alrE",), ("alrP",)], writes=[("alrP",)])
            PE(lambda e: e.matmul(pb[7][0:NS, 0:NH], ones_r[:, 0:NS], alrP[:, 0, :],
                                  start=True, stop=False),
               reads=[("alrP",), ("c", "onr")], writes=[("P", 7)])
            PE(lambda e: e.matmul(pb[7][0:NS, 0:NH], ones_r[:, 0:NS], alrP[:, 1, :],
                                  start=False, stop=True),
               reads=[("alrP",)], writes=[("P", 7)], end=True)
            OP("vector", lambda e: e.tensor_copy(expal_s, pb[7][0:NS, 0:NH]),
               reads=[("P", 7)], writes=[("expal",)])
            OP("vector", lambda e, cs=cs: e.tensor_copy(ctbf, xcbf[64:128, cs, 32, :]),
               reads=[("xc", cs, 32)], writes=[("ctbf",)])
            for jt in range(2):
                PE(lambda e, jt=jt, cs=cs: e.matmul(
                    pb[0][:, jt * 256:(jt + 1) * 256],
                    xcbf[0:64, cs, 32, jt * 128:(jt + 1) * 128],
                    ctbf, start=True, stop=True),
                   reads=[("xc", cs, 32), ("ctbf",)], writes=[("P", 0)], end=True)
                OP("vector", lambda e, jt=jt: e.tensor_copy(
                    scoresT[:, jt, :], pb[0][:, jt * 256:(jt + 1) * 256]),
                   reads=[("P", 0)], writes=[("scT", jt)])
                PE(lambda e, jt=jt, cs=cs: e.transpose(
                    pv_xtm[:, 512 + 64 * jt:512 + 64 * (jt + 1)],
                    xcbf[0:64, cs, 32, jt * 128:(jt + 1) * 128], ident_s[0:64, 0:64]),
                   reads=[("xc", cs, 32), ("c", "ident")], writes=[("P", 1)], end=True)
                OP("scalar", lambda e, jt=jt: e.copy(
                    Bt_tm[:, jt, :], pv_xtm[:, 512 + 64 * jt:512 + 64 * (jt + 1)]),
                   reads=[("P", 1)], writes=[("Btm", jt)])
            for hh in range(NH):
                hr = hh % 4
                h4 = hh % 4
                abank = 0 if hh % 2 == 0 else 2
                ybank = (3, 4, 6)[hh % 3]
                PE(lambda e, hh=hh, abank=abank: e.matmul(
                    pb[abank][:, 0:256], onehbf[:, hh * 128:(hh + 1) * 128], ahi,
                    start=True, stop=False),
                   reads=[("ahi",), ("c", "oh")], writes=[("P", 0) if hr == 0 else ("P", 2)])
                PE(lambda e, hh=hh, abank=abank: e.matmul(
                    pb[abank][:, 0:256], onehbf[:, hh * 128:(hh + 1) * 128], alo,
                    start=False, stop=True),
                   reads=[("alo",)], writes=[("P", 0) if hr == 0 else ("P", 2)], end=True)
                for jt in range(2):
                    slot = hr * 2 + jt
                    PE(lambda e, hh=hh, jt=jt, slot=slot, cs=cs: e.transpose(
                        pv_xtm[:, slot * 128:(slot + 1) * 128],
                        xcbf[:, cs, hh, jt * 128:(jt + 1) * 128], ident_s),
                       reads=[("xc", cs, hh), ("c", "ident")], writes=[("P", 1)], end=True)
                    eng = "vector" if jt == 0 else "scalar"
                    OP(eng, (lambda e, jt=jt, slot=slot, h4=h4, isv=(jt == 0):
                             (e.tensor_copy if isv else e.copy)(
                                 x_tm[:, h4, jt * 128:(jt + 1) * 128],
                                 pv_xtm[:, slot * 128:(slot + 1) * 128])),
                       reads=[("P", 1)], writes=[("xtm", h4)])
                for jt in range(2):
                    OP("vector", lambda e, jt=jt, hr=hr, abank=abank: e.tensor_tensor(
                        wt1[:, jt, :], negm_s[:, jt, :], pb[abank][:, 0:256], ALU.add),
                       reads=[("P", 0) if hr == 0 else ("P", 2), ("c", "nm")], writes=[("wt1", jt)])
                    OP("scalar", lambda e, jt=jt, hh=hh: e.activation(
                        lm[:, jt, :], wt1[:, jt, :], AF.Exp, bias=naT[:, jt, hh:hh + 1]),
                       reads=[("wt1", jt), ("naT", jt)], writes=[("lm", jt)])
                    OP("vector", lambda e, jt=jt, hh=hh, hr=hr: e.scalar_tensor_tensor(
                        wq[:, hr, jt, :], lm[:, jt, :], dtcT[:, jt, hh:hh + 1],
                        scoresT[:, jt, :], ALU.mult, ALU.mult),
                       reads=[("lm", jt), ("dtcT", jt), ("scT", jt)], writes=[("wq", hr, jt)])
                OP("scalar", lambda e, hr=hr, abank=abank: e.activation(
                    expab[:, hr, :], pb[abank][0:NS, 0:256], AF.Exp),
                   reads=[("P", 0) if hr == 0 else ("P", 2)], writes=[("expab", hr)])
                OP("vector", lambda e, hr=hr: e.tensor_tensor(
                    Ce[:, hr, :], ctbf, expab[:, hr, :], ALU.mult),
                   reads=[("expab", hr), ("ctbf",)], writes=[("Ce", hr)])
                PE(lambda e, hr=hr, h4=h4, ybank=ybank: e.matmul(
                    pb[ybank][:, 0:256], x_tm[:, h4, 0:128], wq[:, hr, 0, :],
                    start=True, stop=False),
                   reads=[("xtm", h4), ("wq", hr, 0)], writes=[("P", 3 + hr)])
                PE(lambda e, hr=hr, h4=h4, ybank=ybank: e.matmul(
                    pb[ybank][:, 0:256], x_tm[:, h4, 128:256], wq[:, hr, 1, :],
                    start=False, stop=False),
                   reads=[("xtm", h4), ("wq", hr, 1)], writes=[("P", 3 + hr)])
                PE(lambda e, hr=hr, hh=hh, ybank=ybank: e.matmul(
                    pb[ybank][:, 0:256], Sbf[:, hh, :], Ce[:, hr, :],
                    start=False, stop=True),
                   reads=[("Sbf",), ("Ce", hr)], writes=[("P", 3 + hr)], end=True)
                for jt in range(2):
                    OP("vector", lambda e, jt=jt, hh=hh, hr=hr: e.tensor_scalar(
                        Bd[:, hr, jt, :], Bt_tm[:, jt, :], GT[:, jt, hh:hh + 1], None, ALU.mult),
                       reads=[("Btm", jt), ("GT", jt)], writes=[("Bd", hr, jt)])
                sh = hh % 2
                PE(lambda e, hr=hr, h4=h4, sh=sh: e.matmul(
                    pb[5][0:NS, 128 * sh:128 * (sh + 1)], Bd[:, hr, 0, :],
                    x_tm[:, h4, 0:128], start=True, stop=False),
                   reads=[("Bd", hr, 0), ("xtm", h4)], writes=[("P", 5)])
                PE(lambda e, hr=hr, h4=h4, sh=sh: e.matmul(
                    pb[5][0:NS, 128 * sh:128 * (sh + 1)], Bd[:, hr, 1, :],
                    x_tm[:, h4, 128:256], start=False, stop=True),
                   reads=[("Bd", hr, 1), ("xtm", h4)], writes=[("P", 5)], end=True)
                OP("vector", lambda e, hr=hr, hh=hh, cs=cs, ybank=ybank: e.scalar_tensor_tensor(
                    ysb[:, hr, :], xcbf[:, cs, hh, :], drep_s[:, hh:hh + 1],
                    pb[ybank][:, 0:256], ALU.mult, ALU.add),
                   reads=[("xc", cs, hh), ("c", "dr"), ("P", 3 + hr)], writes=[("ysb", hr)])
                OP("vector", lambda e, hr=hr, hh=hh: e.tensor_tensor(
                    yg[:, hr, :], ysb[:, hr, :], ztile[:, hh, :], ALU.mult),
                   reads=[("ysb", hr), ("zt", hh)], writes=[("yg", hr)])
                OP("scalar", lambda e, hr=hr: e.activation(
                    sqbf[:, hr, :], yg[:, hr, :], AF.Square),
                   reads=[("yg", hr)], writes=[("sqbf", hr)])
                PE(lambda e, hr=hr, ybank=ybank: e.matmul(
                    pb[ybank][0:1, 0:256], ones_c, sqbf[:, hr, :], start=True, stop=True),
                   reads=[("sqbf", hr), ("c", "onc")], writes=[("P", ybank)], end=True)
                OP("vector", lambda e, hr=hr, ybank=ybank: e.tensor_scalar(
                    rr1[:, hr, :], pb[ybank][0:1, 0:256], 1.0 / HD, 1e-6, ALU.mult, ALU.add),
                   reads=[("P", ybank)], writes=[("rr1", hr)])
                OP("scalar", lambda e, hr=hr: e.activation(
                    rr2[:, hr, :], rr1[:, hr, :], AF.Ln),
                   reads=[("rr1", hr)], writes=[("rr2", hr)])
                OP("scalar", lambda e, hr=hr: e.activation(
                    rr1[:, hr, :], rr2[:, hr, :], AF.Exp, scale=-0.5),
                   reads=[("rr2", hr)], writes=[("rr1", hr)])
                OP("scalar", lambda e, hr=hr: e.copy(rsp[:, hr, 0, :], rr1[:, hr, :]),
                   reads=[("rr1", hr)], writes=[("rsp", hr)])
                OP("vector", lambda e, hr=hr: e.tensor_tensor(
                    rsp[:, hr, 1, :], rr1[:, hr, :], rsp[:, hr, 0, :], ALU.subtract),
                   reads=[("rr1", hr), ("rsp", hr)], writes=[("rsp", hr)])
                PE(lambda e, hr=hr: e.matmul(
                    pb[7][:, 0:256], ones_r, rsp[:, hr, 0, :], start=True, stop=False),
                   reads=[("rsp", hr), ("c", "onr")], writes=[("P", 7)])
                PE(lambda e, hr=hr: e.matmul(
                    pb[7][:, 0:256], ones_r, rsp[:, hr, 1, :], start=False, stop=True),
                   reads=[("rsp", hr)], writes=[("P", 7)], end=True)
                OP("vector", lambda e, hr=hr: e.tensor_tensor(
                    ysb[:, hr, :], yg[:, hr, :], pb[7][:, 0:256], ALU.mult),
                   reads=[("yg", hr), ("P", 7)], writes=[("ysb", hr)])
                OP("vector", lambda e, hr=hr, hh=hh, h4=h4: e.tensor_scalar(
                    ynst[:, h4, :], ysb[:, hr, :], nwt_s[:, hh:hh + 1], None, ALU.mult),
                   reads=[("ysb", hr), ("c", "nw")], writes=[("ynst", h4)])
                DMA("sync", "qs", Ysp[hh][:, c0:c0 + CH], ynst[:, h4, :],
                    reads=[("ynst", h4)])
                OP("vector", lambda e, hr=hr, hh=hh: e.tensor_scalar(
                    Stmp[:, hr, :], S_b[:, hh, :], expal_s[:, hh:hh + 1], None, ALU.mult),
                   reads=[("S",), ("expal",)], writes=[("Stmp", hr)])
                OP("vector", lambda e, hr=hr, hh=hh, sh=sh: e.tensor_tensor(
                    S_b[:, hh, :], pb[5][0:NS, 128 * sh:128 * (sh + 1)],
                    Stmp[:, hr, :], ALU.add),
                   reads=[("P", 5), ("Stmp", hr)], writes=[("S",)])
                OP("scalar", lambda e, hh=hh: e.copy(Sbf[:, hh, :], S_b[:, hh, :]),
                   reads=[("S",)], writes=[("Sbf",)])
    ctl.fence()


    if phase_limit >= 4:
        phA.reset(ph0)
        bigA.reset(big0)
        opjT = bigA.alloc("opjT", [128, 32, D], BF16)
        ynb = phA.alloc("ynb", [128, 2, 32, 256], BF16)
        sst = phA.alloc("sst", [128, 2, D], F32)
        for kt in range(32):
            DMA("gpsimd", "qg", opjT[:, kt, :], opt[:, kt, :], writes=[("opj", kt)])
        for ts in range(16):
            sl = ts % 2
            DMA("sync", "qs", ynb[:, sl],
                Ysp[:, :, ts * 256:(ts + 1) * 256].rearrange("k p t -> p k t"),
                writes=[("ynb", sl)])
            for tt2 in range(2):
                tt = ts * 2 + tt2
                bank0 = (tt % 2) * 4
                for k in range(32):
                    for eb in range(4):
                        fn = lambda e, sl=sl, k=k, tt2=tt2, eb=eb, b=bank0: e.matmul(
                            pb[b + eb], ynb[:, sl, k, tt2 * 128:(tt2 + 1) * 128],
                            opjT[:, k, eb * 512:(eb + 1) * 512],
                            start=(k == 0), stop=(k == 31))
                        PE(fn, reads=[("ynb", sl), ("opj", k)],
                           writes=[("P", bank0 + eb)], end=(k == 31 and eb == 3))
                for eb in range(4):
                    eng = "vector" if eb % 2 == 0 else "scalar"
                    fn = (lambda e, eb=eb, b=bank0, tt=tt, isv=(eb % 2 == 0):
                          (e.tensor_copy if isv else e.copy)(
                              sst[:, tt % 2, eb * 512:(eb + 1) * 512], pb[b + eb]))
                    OP(eng, fn, reads=[("P", bank0 + eb)], writes=[("sst", tt % 2, eb)])
                DMA("sync", "qs", Ssp.rearrange("(n p) d -> n p d", p=128)[tt],
                    sst[:, tt % 2, :], reads=[("sst", tt % 2, eb) for eb in range(4)])
    ctl.fence()

    if phase_limit >= 5:
        phA.reset(ph0)
        bigA.reset(big0)
        wdT = bigA.alloc("wdT", [128, 128, 512], BF16)
        d_w = phA.alloc("d_w", [128, 512], U16)
        d_l = phA.alloc("d_l", [128, 512], U16)
        d_h = phA.alloc("d_h", [128, 512], U16)
        d_d = phA.alloc("d_d", [128, 512], U16)
        d_e = phA.alloc("d_e", [128, 512], U16)
        dpkb = phA.alloc("dpkb", [128, 2, 512], U8)
        dscb = phA.alloc("dscb", [128, 2, 512], BF16)
        inb = phA.alloc("inb", [128, 4, 8, 256], BF16)
        stb = phA.alloc("stb", [128, 2, 512], F32)
        xtb = phA.alloc("xtb", [128, 2, 512], F32)
        glb = phA.alloc("glb", [128, 2, 512], F32)
        otb = phA.alloc("otb", [128, 2, 512], F32)
        srr = Ssp.rearrange("(n p) d -> n p d", p=128)
        orr = out_d.rearrange("(n p) d -> n p d", p=128)
        for hb in range(4):
            for W in range(64):
                sl = W % 2
                DMA("gpsimd", "qg", dpkb[:, sl, :],
                    dpk[:, (hb * 64 + W) * 512:(hb * 64 + W + 1) * 512],
                    writes=[("dpkb", sl)])
                DMA("gpsimd", "qg", dscb[:, sl, :], dsc[hb, W], writes=[("dscb", sl)])
                OP("vector", lambda e, sl=sl: e.tensor_copy(d_w, dpkb[:, sl, :]),
                   reads=[("dpkb", sl)], writes=[("dw",)])
                OP("vector", lambda e: e.tensor_scalar(d_l, d_w, 15, None, ALU.bitwise_and),
                   reads=[("dw",)], writes=[("dl",)])
                OP("vector", lambda e: e.tensor_scalar(d_h, d_w, 4, None, ALU.logical_shift_right),
                   reads=[("dw",)], writes=[("dh",)])
                for par in range(2):
                    src, ksrc = (d_l, ("dl",)) if par == 0 else (d_h, ("dh",))
                    if par == 0:
                        d12, kd12, b6, kb6 = d_d, ("dd",), d_l, ("dl",)
                        ff, kf, alt, kalt = d_e, ("de",), d_l, ("dl",)
                        outb, kout = d_d, ("dd",)
                    else:
                        d12, kd12, b6, kb6 = d_l, ("dl",), d_e, ("de",)
                        ff, kf, alt, kalt = d_h, ("dh",), d_e, ("de",)
                        outb, kout = d_l, ("dl",)
                    OP("vector", lambda e, src=src: e.tensor_scalar(d_w, src, 7, None, ALU.bitwise_and),
                       reads=[ksrc], writes=[("dw",)])
                    OP("vector", lambda e, src=src, d12=d12: e.tensor_scalar(
                        d12, src, 8, 12, ALU.bitwise_and, ALU.logical_shift_left),
                       reads=[ksrc], writes=[kd12])
                    OP("vector", lambda e, b6=b6: e.tensor_scalar(
                        b6, d_w, 6, None, ALU.logical_shift_left),
                       reads=[("dw",)], writes=[kb6])
                    OP("vector", lambda e, b6=b6, d12=d12, ff=ff: e.scalar_tensor_tensor(
                        ff, b6, 16128, d12, ALU.add, ALU.add),
                       reads=[kb6, kd12], writes=[kf])
                    OP("vector", lambda e, d12=d12, alt=alt: e.scalar_tensor_tensor(
                        alt, d_w, 16128, d12, ALU.mult, ALU.add),
                       reads=[("dw",), kd12], writes=[kalt])
                    OP("vector", lambda e: e.tensor_scalar(d_w, d_w, 2, None, ALU.is_lt),
                       reads=[("dw",)], writes=[("dw",)])
                    OP("vector", lambda e, alt=alt, ff=ff, outb=outb: e.select(
                        outb, d_w, alt, ff, add_drain=True),
                       reads=[("dw",), kalt, kf], writes=[kout])
                    OP("vector", lambda e, outb=outb, W=W, par=par, sl=sl: e.tensor_tensor(
                        wdT[:, 2 * W + par, :], outb.bitcast(BF16), dscb[:, sl, :], ALU.mult),
                       reads=[kout, ("dscb", sl)], writes=[("wdT", 2 * W + par)])
            for tp in range(16):
                slp = tp % 2
                for kgi in range(19):
                    if kgi < 16:
                        kg_p = kgi
                        DMA("sync" if kg_p % 2 == 0 else "scalar",
                            "qs" if kg_p % 2 == 0 else "qa", inb[:, kg_p % 4, :, :],
                            Isp[kg_p * 8:(kg_p + 1) * 8, :, tp * 256:(tp + 1) * 256].rearrange(
                                "k p t -> p k t"),
                            writes=[("inb", kg_p % 4)])
                    if kgi < 3:
                        continue
                    kg = kgi - 3
                    for k8 in range(8):
                        k = kg * 8 + k8
                        for tt2 in range(2):
                            bank = slp * 2 + tt2
                            fn = lambda e, kg=kg, k8=k8, k=k, tt2=tt2, bank=bank: e.matmul(
                                pb[bank], inb[:, kg % 4, k8, tt2 * 128:(tt2 + 1) * 128],
                                wdT[:, k, :], start=(k == 0), stop=(k == 127))
                            PE(fn, reads=[("inb", kg % 4), ("wdT", k)],
                               writes=[("P", bank)],
                               end=(k8 == 7 and tt2 == 1))
                for tt2 in range(2):
                    tt = tp * 2 + tt2
                    bank = slp * 2 + tt2
                    DMA("gpsimd", "qg", stb[:, tt2, :], srr[tt][:, hb * 512:(hb + 1) * 512],
                        writes=[("stb", tt2)])
                    DMA("gpsimd", "qg", xtb[:, tt2, :],
                        xr[tt][:, hb * 512:(hb + 1) * 512], writes=[("xtb", tt2)])
                    OP("scalar", lambda e, bank=bank, tt2=tt2: e.activation(
                        glb[:, tt2, :], pb[bank], AF.Silu),
                       reads=[("P", bank)], writes=[("glb", tt2)])
                    OP("vector", lambda e, tt2=tt2: e.tensor_tensor(
                        otb[:, tt2, :], glb[:, tt2, :], stb[:, tt2, :], ALU.mult),
                       reads=[("glb", tt2), ("stb", tt2)], writes=[("otb", tt2)])
                    OP("vector", lambda e, tt2=tt2: e.tensor_tensor(
                        otb[:, tt2, :], otb[:, tt2, :], xtb[:, tt2, :], ALU.add),
                       reads=[("otb", tt2), ("xtb", tt2)], writes=[("otb", tt2)])
                    DMA("gpsimd", "qg", orr[tt][:, hb * 512:(hb + 1) * 512],
                        otb[:, tt2, :], reads=[("otb", tt2)])
    ctl.fence()

    if dev_spill and phase_limit < 3:
        xntdump = ap(nc.dram_tensor("xntdump", [128, 16, T], BF16, kind="ExternalOutput"))
        DMA("sync", "qs", xntdump, xnT)

    ctl.fence()

    with nc.Block() as blk:
        for eng_name, meth in [("tensor", blk.tensor), ("vector", blk.vector),
                               ("scalar", blk.scalar), ("gpsimd", blk.gpsimd),
                               ("sync", blk.sync)]:
            ops = ctl.ops[eng_name]
            def body(e, ops=ops):
                for f in ops:
                    f(e)
            meth(body)
    stack.close()
    return nc


def kernel(**inputs):
    h = host_prep(inputs)
    u = np.asarray(inputs["L0_attn_u"])
    nc = build_nc()
    cores = CORE_IDS if CORE_IDS is not None else list(range(N_CORES))
    in_maps = []
    for b in range(len(cores)):
        m = dict(h)
        m["x"] = np.ascontiguousarray(u[b])
        in_maps.append(m)
    res = run_bass_kernel_spmd(nc, in_maps, core_ids=cores)
    return np.stack([r["out"] for r in res.results], axis=0)


# version 05
# speedup vs baseline: 1.1779x; 1.1713x over previous
import contextlib
import numpy as np
import ml_dtypes
import concourse.bass as bass
import concourse.mybir as mybir
from concourse.bass_utils import run_bass_kernel_spmd

F32 = mybir.dt.float32
BF16 = mybir.dt.bfloat16
U8 = mybir.dt.uint8
U16 = mybir.dt.uint16
AF = mybir.ActivationFunctionType
ALU = mybir.AluOpType
AX = mybir.AxisListType.X

T = 4096
D = 2048
EP = 8352
ET = 66
DI = 4096
CONV = 4224
NCT = 33
NH = 32
HD = 128
NS = 64
CH = 256
NCH = 16
MLP = 16384
NSB = 64
NEG = -1e30

NSB_LIM = NSB
P2_SKIP = set()
PHASE_LIMIT = 99
DEV_SPILL = False
N_CORES = 8
CORE_IDS = None

_DTSZ = {F32: 4, BF16: 2, U8: 1, U16: 2}


def _perm_d():
    p = []
    for w in range(8):
        p += [256 * w + 2 * i for i in range(128)]
        p += [256 * w + 2 * i + 1 for i in range(128)]
    return np.array(p, np.int64)


def host_prep(inputs):
    h = {}
    perm = _perm_d()
    ipw = np.asarray(inputs["L0_attn_in_proj_w"])
    wt = np.ascontiguousarray(ipw.T[perm])
    wt = np.concatenate([wt, np.zeros((D, ET * 128 - EP), np.float32)], axis=1)
    h["wipt"] = np.ascontiguousarray(
        wt.reshape(16, 128, ET, 128).transpose(1, 2, 0, 3))

    def pack_gateup(wp):
        u8 = np.asarray(wp).astype(np.int32).view(np.uint8)[:, 0::4]
        t = np.ascontiguousarray(u8.T)
        t = t.reshape(8, 128, NSB, 256).transpose(1, 2, 0, 3)
        return np.ascontiguousarray(t.reshape(128, NSB * 8 * 256))

    h["gpk"] = pack_gateup(inputs["L0_ffn_gate_W_packed"])
    h["upk"] = pack_gateup(inputs["L0_ffn_up_W_packed"])

    def pack_scales(sc):
        t = np.asarray(sc).T.reshape(8, 8, NSB, 256)
        g = t.transpose(2, 1, 0, 3)
        return np.ascontiguousarray(np.repeat(g, 16, axis=1))

    h["gsc"] = pack_scales(inputs["L0_ffn_gate_scales"])
    h["usc"] = pack_scales(inputs["L0_ffn_up_scales"])

    dpw = np.asarray(inputs["L0_ffn_down_W_packed"]).astype(np.int32).view(np.uint8)[:, 0::4]
    dw = np.ascontiguousarray(dpw.T)
    dw = dw.reshape(64, 128, 4, 512).transpose(1, 2, 0, 3)
    h["dpk"] = np.ascontiguousarray(dw.reshape(128, 4 * 64 * 512))

    dsc = np.asarray(inputs["L0_ffn_down_scales"]).T.reshape(64, 8, 4, 512)
    dsc = dsc.transpose(2, 0, 1, 3)
    h["dsc"] = np.ascontiguousarray(np.repeat(dsc, 16, axis=2))

    opw = np.asarray(inputs["L0_attn_o_proj_w"])
    ot = np.ascontiguousarray(opw.T)
    h["opt"] = np.ascontiguousarray(ot.reshape(32, 128, D).transpose(1, 0, 2))

    cw = np.asarray(inputs["L0_attn_conv_w"])
    h["convw"] = np.ascontiguousarray(cw.reshape(NCT, 128, 4).transpose(1, 0, 2))
    h["convb"] = np.ascontiguousarray(
        np.asarray(inputs["L0_attn_conv_b"]).reshape(NCT, 128).T)

    h["alog"] = np.ascontiguousarray(np.asarray(inputs["L0_attn_A_log"]).reshape(NH, 1)).astype(np.float32)
    h["dtb"] = np.ascontiguousarray(np.asarray(inputs["L0_attn_dt_bias"]).reshape(NH, 1)).astype(np.float32)
    h["drep"] = np.ascontiguousarray(
        np.broadcast_to(np.asarray(inputs["L0_attn_D_skip"])[None, :], (128, NH))).astype(np.float32)
    h["nwt"] = np.ascontiguousarray(
        np.asarray(inputs["L0_attn_norm_w"]).reshape(NH, 128).T).astype(np.float32)

    h["identbf"] = np.eye(128, dtype=ml_dtypes.bfloat16)
    jj = np.arange(128)[:, None]
    ii = np.arange(CH)[None, :]
    nm = np.zeros((128, 2, CH), np.float32)
    nm[:, 0, :] = np.where(jj <= ii, 0.0, NEG)
    nm[:, 1, :] = np.where(jj + 128 <= ii, 0.0, NEG)
    h["negmask"] = nm
    oh = np.zeros((NH, NH * 128), np.float32)
    for hh in range(NH):
        oh[hh, hh * 128:(hh + 1) * 128] = 1.0
    h["onehot"] = oh
    h["onehot"] = h["onehot"].astype(ml_dtypes.bfloat16)
    h["ones_c"] = np.ones((128, 1), ml_dtypes.bfloat16)
    h["ones_r"] = np.ones((1, 128), ml_dtypes.bfloat16)
    return h


class Arena:

    def __init__(self, nc, base, limit):
        self.nc, self.base, self.limit, self.cur = nc, base, limit, base
        self.n = 0

    def alloc(self, name, shape, dtype):
        nbytes = int(np.prod(shape[1:])) * _DTSZ[dtype]
        nbytes = (nbytes + 31) // 32 * 32
        assert self.cur + nbytes <= self.limit, \
            f"arena overflow at {name}: {self.cur + nbytes} > {self.limit}"
        t = self.nc.alloc_sbuf_tensor_at(f"{name}_{self.n}", shape, dtype, offset=self.cur)
        self.n += 1
        self.cur += nbytes
        return t.ap()

    def mark(self):
        return self.cur

    def reset(self, saved=None):
        self.cur = self.base if saved is None else saved


class Ctl:

    ENGINES = ("tensor", "vector", "scalar", "gpsimd", "sync")
    SELF = {"vector": "sv", "scalar": "ss", "gpsimd": "sg"}

    LANES = {"qs": 6, "qg": 6, "qa": 4}

    def __init__(self, sems):
        self.ops = {e: [] for e in self.ENGINES}
        self.cnt = {}
        self.sems = sems
        self.lanes = {q: [f"{q}{i}" for i in range(n)] for q, n in self.LANES.items()}
        self.lane_rr = {q: 0 for q in self.LANES}
        self.last_w = {}
        self.readers = {}
        self.watermark = {e: {} for e in self.ENGINES}
        self.pe_pending_w = []
        self.pe_pending_r = []

    def _bump(self, sem, k=1):
        self.cnt[sem] = self.cnt.get(sem, 0) + k
        return self.cnt[sem]

    def val(self, sem):
        return self.cnt.get(sem, 0)

    def _edge(self, eng, dep):
        if dep is None:
            return
        sem, v = dep
        if v <= self.watermark[eng].get(sem, 0):
            return
        self.watermark[eng][sem] = v
        s = self.sems[sem]
        self.ops[eng].append(lambda e, s=s, v=v: e.wait_ge(s, v))

    def _deps(self, eng, reads, writes):
        for key in reads:
            self._edge(eng, self.last_w.get(key))
        for key in writes:
            self._edge(eng, self.last_w.get(key))
            for d in self.readers.get(key, ()):
                self._edge(eng, d)

    def _commit(self, reads, writes, completion):
        for key in writes:
            self.last_w[key] = completion
            self.readers[key] = []
        for key in reads:
            self.readers.setdefault(key, []).append(completion)

    def op(self, eng, fn, reads=(), writes=()):
        sem = self.SELF[eng]
        self._deps(eng, reads, writes)
        v = self._bump(sem)
        s = self.sems[sem]
        self.ops[eng].append(lambda e, fn=fn, s=s: fn(e).then_inc(s, 1))
        comp = (sem, v)
        self._commit(reads, writes, comp)
        return comp

    def pe(self, fn, reads=(), writes=(), end=False):
        self._deps("tensor", reads, writes)
        self.pe_pending_w.extend(writes)
        self.pe_pending_r.extend(reads)
        if end:
            v = self._bump("pe")
            s = self.sems["pe"]
            self.ops["tensor"].append(lambda e, fn=fn, s=s: fn(e).then_inc(s, 1))
            comp = ("pe", v)
            self._commit(self.pe_pending_r, self.pe_pending_w, comp)
            self.pe_pending_w, self.pe_pending_r = [], []
            return comp
        self.ops["tensor"].append(fn)
        return None

    def dma(self, eng, q, out, in_, reads=(), writes=()):
        lanes = self.lanes[q]
        i = self.lane_rr[q]
        self.lane_rr[q] = (i + 1) % len(lanes)
        lane = lanes[i]
        prev = self.val(lane)
        self._edge(eng, (lane, prev) if prev > 0 else None)
        self._deps(eng, reads, writes)
        v = self._bump(lane, 16)
        s = self.sems[lane]
        self.ops[eng].append(
            lambda e, out=out, in_=in_, s=s: e.dma_start(out=out, in_=in_).then_inc(s, 16))
        comp = (lane, v)
        self._commit(reads, writes, comp)
        return comp

    def fence(self):
        deps = []
        for sem in ("sv", "ss", "sg", "pe"):
            if self.val(sem) > 0:
                deps.append((sem, self.val(sem)))
        for q in self.lanes:
            for lane in self.lanes[q]:
                if self.val(lane) > 0:
                    deps.append((lane, self.val(lane)))
        for eng in self.ENGINES:
            for d in deps:
                self._edge(eng, d)
        self.last_w = {}
        self.readers = {}


def build_nc(phase_limit=None, dev_spill=None):
    if phase_limit is None:
        phase_limit = PHASE_LIMIT
    if dev_spill is None:
        dev_spill = DEV_SPILL
    nc = bass.Bass("TRN2", target_bir_lowering=False)

    ap = lambda t: t.ap()
    x_in = ap(nc.dram_tensor("x", [T, D], F32, kind="ExternalInput"))
    wipt = ap(nc.dram_tensor("wipt", [128, ET, 16, 128], F32, kind="ExternalInput"))
    gpk = ap(nc.dram_tensor("gpk", [128, NSB * 2048], U8, kind="ExternalInput"))
    upk = ap(nc.dram_tensor("upk", [128, NSB * 2048], U8, kind="ExternalInput"))
    gsc = ap(nc.dram_tensor("gsc", [64, 128, 8, 256], F32, kind="ExternalInput"))
    usc = ap(nc.dram_tensor("usc", [64, 128, 8, 256], F32, kind="ExternalInput"))
    dpk = ap(nc.dram_tensor("dpk", [128, 4 * 64 * 512], U8, kind="ExternalInput"))
    dsc = ap(nc.dram_tensor("dsc", [4, 64, 128, 512], F32, kind="ExternalInput"))
    opt = ap(nc.dram_tensor("opt", [128, 32, D], F32, kind="ExternalInput"))
    convw_d = ap(nc.dram_tensor("convw", [128, NCT, 4], F32, kind="ExternalInput"))
    convb_d = ap(nc.dram_tensor("convb", [128, NCT], F32, kind="ExternalInput"))
    alog_d = ap(nc.dram_tensor("alog", [NH, 1], F32, kind="ExternalInput"))
    dtb_d = ap(nc.dram_tensor("dtb", [NH, 1], F32, kind="ExternalInput"))
    drep_d = ap(nc.dram_tensor("drep", [128, NH], F32, kind="ExternalInput"))
    nwt_d = ap(nc.dram_tensor("nwt", [128, NH], F32, kind="ExternalInput"))
    ident_d = ap(nc.dram_tensor("identbf", [128, 128], BF16, kind="ExternalInput"))
    negm_d = ap(nc.dram_tensor("negmask", [128, 2, CH], F32, kind="ExternalInput"))
    onehot_d = ap(nc.dram_tensor("onehot", [NH, NH * 128], BF16, kind="ExternalInput"))
    ones_c_d = ap(nc.dram_tensor("ones_c", [128, 1], BF16, kind="ExternalInput"))
    ones_r_d = ap(nc.dram_tensor("ones_r", [1, 128], BF16, kind="ExternalInput"))

    skind = "ExternalOutput" if dev_spill else "Internal"
    Z = ap(nc.dram_tensor("Zsp", [65, 128, T], F32, kind=skind))
    Isp = ap(nc.dram_tensor("Isp", [128, 128, T], BF16, kind=skind))
    Ysp = ap(nc.dram_tensor("Ysp", [32, 128, T], BF16, kind=skind))
    Ssp = ap(nc.dram_tensor("Ssp", [T, D], F32, kind=skind))
    DTsp = ap(nc.dram_tensor("DTsp", [NH, T], F32, kind=skind))
    drtmp = ap(nc.dram_tensor("drtmp", [NH], F32, kind="Internal"))
    out_d = ap(nc.dram_tensor("out", [T, D], F32, kind="ExternalOutput"))

    sbuf = lambda name, shape, dt: nc.alloc_sbuf_tensor(name, shape, dt).ap()
    ident_s = sbuf("ident_s", [128, 128], BF16)
    ones_c = sbuf("ones_c_s", [128, 1], BF16)
    ones_r = sbuf("ones_r_s", [1, 128], BF16)
    alog_s = sbuf("alog_s", [NH, 1], F32)
    dtb_s = sbuf("dtb_s", [NH, 1], F32)
    A_s = sbuf("A_s", [NH, 1], F32)

    import concourse.bass as _b
    perm_end = nc.bytes_allocated(_b.MemorySpace.SBUF, "left")
    cap = nc.sbuf_top
    BIGB = (perm_end + 31) // 32 * 32
    PHB = BIGB + 131072
    assert PHB < cap, (BIGB, cap)
    bigA = Arena(nc, BIGB, PHB)
    phA = Arena(nc, PHB, cap)

    big0 = bigA.mark()
    xnT = bigA.alloc("xnT", [128, 16, T], BF16)

    pb = [nc.alloc_psum_tensor(f"pb{i}", [128, 512], F32).ap() for i in range(8)]

    stack = contextlib.ExitStack()
    sems = {}
    semnames = ["pe", "sv", "ss", "sg"]
    for q, n in Ctl.LANES.items():
        semnames += [f"{q}{i}" for i in range(n)]
    for name in semnames:
        sems[name] = stack.enter_context(nc.semaphore(name))
    ctl = Ctl(sems)
    OP, PE, DMA = ctl.op, ctl.pe, ctl.dma

    for dst, src, nm in [(ident_s, ident_d, "ident"), (ones_c, ones_c_d, "onc"),
                         (ones_r, ones_r_d, "onr"), (alog_s, alog_d, "alog"),
                         (dtb_s, dtb_d, "dtb")]:
        DMA("gpsimd", "qg", dst, src, writes=[("c", nm)])
    OP("scalar", lambda e: e.activation(A_s, alog_s, AF.Exp),
       reads=[("c", "alog")], writes=[("A",)])
    OP("vector", lambda e: e.tensor_scalar(A_s, A_s, -1.0, None, ALU.mult),
       reads=[("A",)], writes=[("A",)])

    ph0 = phA.mark()
    xb = phA.alloc("xb", [128, 2, D], F32)
    xnbf = phA.alloc("xnbf", [128, 2, D], BF16)
    stats = phA.alloc("stats", [128, 8], F32)
    sqs = phA.alloc("sqs", [128, D], F32)

    NT = T // 128
    xr = x_in.rearrange("(n p) d -> n p d", p=128)
    for i in range(NT):
        sl = i % 2
        DMA("sync", "qs", xb[:, sl, :], xr[i], writes=[("xb", sl)])
        OP("scalar", lambda e, sl=sl: e.activation(sqs, xb[:, sl, :], AF.Square),
           reads=[("xb", sl)], writes=[("sqs",)])
        OP("vector", lambda e, sl=sl: e.reduce_sum(stats[:, 0:1], xb[:, sl, :], axis=AX),
           reads=[("xb", sl)], writes=[("stats",)])
        OP("vector", lambda e: e.tensor_scalar(stats[:, 2:3], stats[:, 0:1], 1.0 / D, None, ALU.mult),
           reads=[("stats",)], writes=[("stats",)])
        OP("vector", lambda e: e.reduce_sum(stats[:, 1:2], sqs, axis=AX),
           reads=[("sqs",)], writes=[("stats",)])
        OP("vector", lambda e: e.tensor_scalar(stats[:, 3:4], stats[:, 1:2], 1.0 / D, None, ALU.mult),
           reads=[("stats",)], writes=[("stats",)])
        OP("vector", lambda e: e.tensor_tensor(stats[:, 4:5], stats[:, 2:3], stats[:, 2:3], ALU.mult),
           reads=[("stats",)], writes=[("stats",)])
        OP("vector", lambda e: e.tensor_tensor(stats[:, 5:6], stats[:, 3:4], stats[:, 4:5], ALU.subtract),
           reads=[("stats",)], writes=[("stats",)])
        OP("vector", lambda e: e.tensor_scalar(stats[:, 5:6], stats[:, 5:6], 1e-5, None, ALU.add),
           reads=[("stats",)], writes=[("stats",)])
        OP("scalar", lambda e: e.activation(stats[:, 7:8], stats[:, 5:6], AF.Ln),
           reads=[("stats",)], writes=[("stats",)])
        OP("scalar", lambda e: e.activation(stats[:, 6:7], stats[:, 7:8], AF.Exp, scale=-0.5),
           reads=[("stats",)], writes=[("stats",)])
        OP("vector", lambda e, sl=sl: e.tensor_scalar(sqs, xb[:, sl, :], stats[:, 2:3], None, ALU.subtract),
           reads=[("xb", sl), ("stats",)], writes=[("sqs",)])
        OP("vector", lambda e, sl=sl: e.tensor_scalar(xnbf[:, sl, :], sqs, stats[:, 6:7], None, ALU.mult),
           reads=[("sqs",), ("stats",)], writes=[("xnbf", sl)])
        for g in range(4):
            bank = 4 + (i * 4 + g) % 4
            pv = pb[bank].bitcast(BF16)
            for t4 in range(4):
                kt = g * 4 + t4
                w, par = kt // 2, kt % 2
                src = xnbf[:, sl, 256 * w + par: 256 * w + 256: 2]
                fn = lambda e, pv=pv, t4=t4, src=src: e.transpose(
                    pv[:, t4 * 128:(t4 + 1) * 128], src, ident_s)
                PE(fn, reads=[("xnbf", sl), ("c", "ident")], writes=[("pb", bank)],
                   end=(t4 == 3))
            eng = "vector" if g % 2 == 0 else "scalar"
            isv = eng == "vector"
            for t4 in range(4):
                kt = g * 4 + t4
                fn = (lambda e, pv=pv, t4=t4, kt=kt, i=i, isv=isv:
                      (e.tensor_copy if isv else e.copy)(
                          xnT[:, kt, i * 128:(i + 1) * 128], pv[:, t4 * 128:(t4 + 1) * 128]))
                OP(eng, fn, reads=[("pb", bank)], writes=[("xnt", kt, i)])
    ctl.fence()

    if phase_limit >= 1:
        phA.reset(ph0)
        wbuf = phA.alloc("wbuf", [128, 2, 16, 128], BF16)
        stage = phA.alloc("stage", [128, 2, T], F32)
        sqs1 = phA.alloc("sqs1", [128, D], F32)
        dtT = phA.alloc("dtT", [NH, T], F32)

        for et in range(ET):
            sl = et % 2
            DMA("gpsimd", "qg", wbuf[:, sl], wipt[:, et], writes=[("wbuf", sl)])
            me = 128 if et < 65 else 32
            for tb in range(8):
                bank = (et * 8 + tb) % 8
                psv = pb[bank][0:me, :]
                for k in range(16):
                    fn = lambda e, psv=psv, sl=sl, k=k, tb=tb, me=me: e.matmul(
                        psv, wbuf[:, sl, k, 0:me], xnT[:, k, tb * 512:(tb + 1) * 512],
                        start=(k == 0), stop=(k == 15))
                    PE(fn, reads=[("wbuf", sl)], writes=[("pb", bank)], end=(k == 15))
                if et < 65:
                    eng = "vector" if tb % 2 == 0 else "scalar"
                    isv = eng == "vector"
                    fn = (lambda e, bank=bank, sl=sl, tb=tb, isv=isv:
                          (e.tensor_copy if isv else e.copy)(
                              stage[:, sl, tb * 512:(tb + 1) * 512], pb[bank]))
                    OP(eng, fn, reads=[("pb", bank)], writes=[("stage", sl, tb)])
                else:
                    pz = pb[bank][0:32, :]
                    OP("scalar", lambda e, pz=pz: e.activation(
                        sqs1[0:32, 0:512], pz, AF.Exp, bias=dtb_s),
                       reads=[("pb", bank), ("c", "dtb")], writes=[("sqs1",)])
                    OP("scalar", lambda e, tb=tb: e.activation(
                        dtT[:, tb * 512:(tb + 1) * 512], sqs1[0:32, 0:512], AF.Ln, bias=1.0),
                       reads=[("sqs1",)], writes=[("dtT", tb)])
            if et < 65:
                DMA("sync", "qs", Z[et], stage[:, sl],
                    reads=[("stage", sl, tb) for tb in range(8)])
        DMA("sync", "qs", DTsp, dtT, reads=[("dtT", tb) for tb in range(8)])
    ctl.fence()


    if phase_limit >= 2:
        phA.reset(ph0)
        pkb = phA.alloc("pkb", [128, 3, 2, 2048], U8)
        scw = phA.alloc("scw", [128, 1024], U16)
        scl = phA.alloc("scl", [128, 1024], U16)
        sch = phA.alloc("sch", [128, 1024], U16)
        scd = phA.alloc("scd", [128, 1024], U16)
        sce = phA.alloc("sce", [128, 1024], U16)
        deq = phA.alloc("deq", [128, 2, 2, 2, 8, 256], BF16)
        scsb = phA.alloc("scsb", [128, 2, 2, 8, 256], BF16)
        glt = phA.alloc("glt", [128, 2, 512], BF16)
        upsb = phA.alloc("upsb", [128, 2, 512], BF16)
        istg = phA.alloc("istg", [128, 2, 1024], BF16)

        def p2_loads(sb):
            if sb >= NSB:
                return
            r3, sl = sb % 3, sb % 2
            DMA("scalar", "qa", pkb[:, r3, 0, :], gpk[:, sb * 2048:(sb + 1) * 2048],
                writes=[("pkb", r3, 0)])
            DMA("scalar", "qa", pkb[:, r3, 1, :], upk[:, sb * 2048:(sb + 1) * 2048],
                writes=[("pkb", r3, 1)])
            DMA("gpsimd", "qg", scsb[:, sl, 0], gsc[sb], writes=[("scsb", sl, 0)])
            DMA("gpsimd", "qg", scsb[:, sl, 1], usc[sb], writes=[("scsb", sl, 1)])

        def p2_deq(sb):
            if sb >= NSB:
                return
            r3, sl = sb % 3, sb % 2
            for mat, half in ((0, 0), (0, 1), (1, 0), (1, 1)):
                pk_ap = pkb[:, r3, mat, half * 1024:(half + 1) * 1024]
                hw0 = half * 4
                OP("vector", lambda e, pk_ap=pk_ap: e.tensor_copy(scw, pk_ap),
                   reads=[("pkb", r3, mat)], writes=[("scw",)])
                OP("vector", lambda e: e.tensor_scalar(scl, scw, 15, None, ALU.bitwise_and),
                   reads=[("scw",)], writes=[("scl",)])
                OP("vector", lambda e: e.tensor_scalar(sch, scw, 4, None, ALU.logical_shift_right),
                   reads=[("scw",)], writes=[("sch",)])
                for par in range(2):
                    src, ksrc = (scl, ("scl",)) if par == 0 else (sch, ("sch",))
                    if par == 0:
                        d12, kd12, b6, kb6 = scd, ("scd",), scl, ("scl",)
                        ff, kf, alt, kalt = sce, ("sce",), scl, ("scl",)
                        outb, kout = scd, ("scd",)
                    else:
                        d12, kd12, b6, kb6 = scl, ("scl",), sce, ("sce",)
                        ff, kf, alt, kalt = sch, ("sch",), sce, ("sce",)
                        outb, kout = scl, ("scl",)
                    OP("vector", lambda e, src=src, d12=d12: e.tensor_scalar(
                        d12, src, 8, 12, ALU.bitwise_and, ALU.logical_shift_left),
                       reads=[ksrc], writes=[kd12])
                    OP("vector", lambda e, src=src: e.tensor_scalar(
                        scw, src, 7, None, ALU.bitwise_and),
                       reads=[ksrc], writes=[("scw",)])
                    OP("vector", lambda e, b6=b6: e.tensor_scalar(
                        b6, scw, 6, None, ALU.logical_shift_left),
                       reads=[("scw",)], writes=[kb6])
                    OP("vector", lambda e, b6=b6, d12=d12, ff=ff: e.scalar_tensor_tensor(
                        ff, b6, 16128, d12, ALU.add, ALU.add),
                       reads=[kb6, kd12], writes=[kf])
                    OP("vector", lambda e, d12=d12, alt=alt: e.scalar_tensor_tensor(
                        alt, scw, 16128, d12, ALU.mult, ALU.add),
                       reads=[("scw",), kd12], writes=[kalt])
                    OP("vector", lambda e: e.tensor_scalar(scw, scw, 2, None, ALU.is_lt),
                       reads=[("scw",)], writes=[("scw",)])
                    OP("vector", lambda e, alt=alt, ff=ff, outb=outb: e.select(
                        outb, scw, alt, ff, add_drain=True),
                       reads=[("scw",), kalt, kf], writes=[kout])
                    OP("vector", lambda e, outb=outb, mat=mat, par=par, sl=sl, hw0=hw0: e.tensor_tensor(
                        deq[:, sl, mat, par, hw0:hw0 + 4, :].rearrange("p a b -> p (a b)"),
                        outb.bitcast(BF16),
                        scsb[:, sl, mat, hw0:hw0 + 4, :].rearrange("p a b -> p (a b)"),
                        ALU.mult),
                       reads=[kout, ("scsb", sl, mat)], writes=[("deq", sl, mat)])

        def p2_mm(sb):
            sl = sb % 2
            for npar in range(2):
                sl2 = (sb * 2 + npar) % 2
                nt = sb * 2 + npar
                for tb in range(8):
                    gidx = (sb * 2 + npar) * 8 + tb
                    pair = gidx % 3
                    bg, bu = 2 * pair, 2 * pair + 1
                    for mat, bank in ((0, bg), (1, bu)):
                        for k in range(16):
                            par, w = k % 2, k // 2
                            lhsT = deq[:, sl, mat, par, w, npar::2]
                            fn = lambda e, bank=bank, lhsT=lhsT, k=k, tb=tb: e.matmul(
                                pb[bank], lhsT, xnT[:, k, tb * 512:(tb + 1) * 512],
                                start=(k == 0), stop=(k == 15))
                            PE(fn, reads=[("deq", sl, mat)], writes=[("pb", bank)],
                               end=(k == 15))
                    OP("scalar", lambda e, bg=bg, tb=tb: e.activation(
                        glt[:, tb % 2, :], pb[bg], AF.Gelu_apprx_tanh),
                       reads=[("pb", bg)], writes=[("glt", tb % 2)])
                    OP("scalar", lambda e, bu=bu, tb=tb: e.copy(
                        upsb[:, tb % 2, :], pb[bu]),
                       reads=[("pb", bu)], writes=[("upsb", tb % 2)])
                    OP("gpsimd", lambda e, tb=tb, sl2=sl2: e.tensor_tensor(
                        istg[:, sl2, (tb % 2) * 512:(tb % 2 + 1) * 512], glt[:, tb % 2, :],
                        upsb[:, tb % 2, :], ALU.mult),
                       reads=[("glt", tb % 2), ("upsb", tb % 2)], writes=[("istg", sl2, tb % 2)])
                    if tb % 2 == 1:
                        half = tb // 2
                        DMA("sync", "qs", Isp[nt][:, half * 1024:(half + 1) * 1024],
                            istg[:, sl2, :],
                            reads=[("istg", sl2, 0), ("istg", sl2, 1)])

        p2_loads(0)
        p2_loads(1)
        p2_deq(0)
        for sb in range(NSB):
            p2_loads(sb + 2)
            p2_deq(sb + 1)
            p2_mm(sb)
    ctl.fence()

    if phase_limit >= 3:
        phA.reset(ph0)
        bigA.reset(big0)
        dtT3 = bigA.alloc("dtT3", [NH, T], F32)
        xcbf = bigA.alloc("xcbf", [128, 2, NCT, CH], BF16)
        onehbf = bigA.alloc("onehbf", [NH, NH * 128], BF16)
        negm_s = bigA.alloc("negm_s", [128, 2, CH], F32)
        convw_s = bigA.alloc("convw_s", [128, NCT, 4], F32)
        convb_s = bigA.alloc("convb_s", [128, NCT], F32)
        drep_s = bigA.alloc("drep_s", [128, NH], F32)
        nwt_s = bigA.alloc("nwt_s", [128, NH], F32)
        ztile = bigA.alloc("ztile", [128, NH, CH], F32)
        S_b = bigA.alloc("S_b", [NS, NH, HD], F32)
        Sbf = bigA.alloc("Sbf", [NS, NH, HD], BF16)
        win = phA.alloc("win", [128, 6, 260], F32)
        convt = phA.alloc("convt", [128, 2, CH], F32)

        scoresT = phA.alloc("scoresT", [128, 2, CH], F32)
        aA = phA.alloc("aA", [NH, CH], F32)
        aC = phA.alloc("aC", [NH, CH], F32)
        naC = phA.alloc("naC", [NH, CH], F32)
        dtdC = phA.alloc("dtdC", [NH, CH], F32)
        GC = phA.alloc("GC", [NH, CH], F32)
        ahi = phA.alloc("ahi", [NH, CH], BF16)
        alo = phA.alloc("alo", [NH, CH], BF16)
        naT = phA.alloc("naT", [128, 2, NH], F32)
        dtcT = phA.alloc("dtcT", [128, 2, NH], F32)
        GT = phA.alloc("GT", [128, 2, NH], F32)
        alr = phA.alloc("alr", [1, NH], F32)
        alrE = phA.alloc("alrE", [1, NH], F32)
        alrP = phA.alloc("alrP", [1, 2, NH], BF16)
        expal_s = phA.alloc("expal_s", [NS, NH], F32)
        x_tm = phA.alloc("x_tm", [128, 4, CH], BF16)
        wq = phA.alloc("wq", [128, 4, 2, CH], BF16)
        wt1 = phA.alloc("wt1", [128, 4, CH], F32)
        lm = phA.alloc("lm", [128, 4, CH], F32)
        Bt_tm = phA.alloc("Bt_tm", [128, 2, NS], BF16)
        Bd = phA.alloc("Bd", [128, 4, 2, NS], BF16)
        Ce = phA.alloc("Ce", [NS, 4, CH], BF16)
        ctbf = phA.alloc("ctbf", [NS, CH], BF16)
        expab = phA.alloc("expab", [NS, 4, CH], F32)
        ysb = phA.alloc("ysb", [128, 4, CH], F32)
        yg = phA.alloc("yg", [128, 4, CH], F32)
        sqbf = phA.alloc("sqbf", [128, 4, CH], BF16)

        rr1 = phA.alloc("rr1", [1, 4, CH], F32)
        rr2 = phA.alloc("rr2", [1, 4, CH], F32)
        rsp = phA.alloc("rsp", [1, 4, 2, CH], BF16)
        Stmp = phA.alloc("Stmp", [NS, 4, HD], F32)
        ynst = phA.alloc("ynst", [128, 4, CH], BF16)

        for dst, src, nm in [(convw_s, convw_d, "cw"), (convb_s, convb_d, "cb"),
                             (drep_s, drep_d, "dr"), (nwt_s, nwt_d, "nw"),
                             (negm_s, negm_d, "nm"), (onehbf, onehot_d, "oh"),
                             (dtT3, DTsp, "dt3")]:
            DMA("gpsimd", "qg", dst, src, writes=[("c", nm)])
        OP("vector", lambda e: e.memset(S_b, 0.0), writes=[("S",)])
        OP("vector", lambda e: e.memset(Sbf, 0.0), writes=[("Sbf",)])

        pv_xtm = pb[1].bitcast(BF16)
        for c in range(NCH):
            cs = c % 2
            c0 = c * CH
            for ct in range(NCT):
                ws = ct % 6
                if c == 0:
                    OP("vector", lambda e, ws=ws: e.memset(win[:, ws, 0:3], 0.0),
                       reads=[], writes=[("win", ws)])
                    DMA("sync", "qs", win[:, ws, 3:259], Z[32 + ct][:, 0:256],
                        writes=[("win", ws)])
                else:
                    DMA("sync", "qs", win[:, ws, 0:259], Z[32 + ct][:, c0 - 3:c0 + 256],
                        writes=[("win", ws)])
                cv = ct % 2
                OP("vector", lambda e, ws=ws, cv=cv, ct=ct: e.tensor_scalar(
                    convt[:, cv, :], win[:, ws, 0:256], convw_s[:, ct, 0:1], None, ALU.mult),
                   reads=[("win", ws), ("c", "cw")], writes=[("cvt", cv)])
                for k in range(1, 4):
                    OP("vector", lambda e, ws=ws, cv=cv, ct=ct, k=k: e.scalar_tensor_tensor(
                        convt[:, cv, :], win[:, ws, k:k + 256], convw_s[:, ct, k:k + 1],
                        convt[:, cv, :], ALU.mult, ALU.add),
                       reads=[("win", ws), ("cvt", cv), ("c", "cw")], writes=[("cvt", cv)])
                OP("scalar", lambda e, cv=cv, ct=ct, cs=cs: e.activation(
                    xcbf[:, cs, ct, :], convt[:, cv, :], AF.Silu, bias=convb_s[:, ct:ct + 1]),
                   reads=[("cvt", cv), ("c", "cb")], writes=[("xc", cs, ct)])
            for hh in range(NH):
                DMA("sync", "qs", ztile[:, hh, :], Z[hh][:, c0:c0 + CH],
                    writes=[("zt", hh)])
                OP("scalar", lambda e, hh=hh: e.activation(
                    ztile[:, hh, :], ztile[:, hh, :], AF.Sigmoid),
                   reads=[("zt", hh)], writes=[("zt", hh)])
            OP("vector", lambda e, c0=c0: e.tensor_scalar(
                aA, dtT3[:, c0:c0 + CH], A_s, None, ALU.mult),
               reads=[("c", "dt3"), ("A",)], writes=[("aA",)])
            OP("vector", lambda e: e.tensor_tensor_scan(aC, aA, aA, 0.0, ALU.add, ALU.bypass),
               reads=[("aA",)], writes=[("aC",)])
            OP("vector", lambda e: e.tensor_scalar(naC, aC, -1.0, None, ALU.mult),
               reads=[("aC",)], writes=[("naC",)])
            OP("scalar", lambda e: e.activation(dtdC, aC, AF.Exp, scale=-1.0,
                                                bias=aC[:, 255:256]),
               reads=[("aC",)], writes=[("dtd",)])
            OP("vector", lambda e, c0=c0: e.tensor_tensor(
                GC, dtT3[:, c0:c0 + CH], dtdC, ALU.mult),
               reads=[("c", "dt3"), ("dtd",)], writes=[("G",)])
            OP("scalar", lambda e: e.copy(ahi, aC), reads=[("aC",)], writes=[("ahi",)])
            OP("vector", lambda e: e.tensor_tensor(alo, aC, ahi, ALU.subtract),
               reads=[("aC",), ("ahi",)], writes=[("alo",)])
            for jt in range(2):
                for mblk in range(4):
                    pbase = 32 * mblk
                    col = 128 * jt + 32 * mblk
                    OP("vector", lambda e, jt=jt, pbase=pbase, col=col: e.transpose(
                        naT[pbase:pbase + 32, jt, :], naC[:, col:col + 32]),
                       reads=[("naC",)], writes=[("naT", jt)])
                    OP("vector", lambda e, jt=jt, pbase=pbase, col=col, c0=c0: e.transpose(
                        dtcT[pbase:pbase + 32, jt, :], dtT3[:, c0 + col:c0 + col + 32]),
                       reads=[("c", "dt3")], writes=[("dtcT", jt)])
                    OP("vector", lambda e, jt=jt, pbase=pbase, col=col: e.transpose(
                        GT[pbase:pbase + 32, jt, :], GC[:, col:col + 32]),
                       reads=[("G",)], writes=[("GT", jt)])
            DMA("gpsimd", "qg", drtmp, aC[:, 255:256], reads=[("aC",)], writes=[("drt",)])
            DMA("gpsimd", "qg", alr, drtmp[None, :], reads=[("drt",)], writes=[("alr",)])
            OP("scalar", lambda e: e.activation(alrE, alr, AF.Exp),
               reads=[("alr",)], writes=[("alrE",)])
            OP("scalar", lambda e: e.copy(alrP[:, 0, :], alrE),
               reads=[("alrE",)], writes=[("alrP",)])
            OP("vector", lambda e: e.tensor_tensor(alrP[:, 1, :], alrE, alrP[:, 0, :], ALU.subtract),
               reads=[("alrE",), ("alrP",)], writes=[("alrP",)])
            PE(lambda e: e.matmul(pb[7][0:NS, 0:NH], ones_r[:, 0:NS], alrP[:, 0, :],
                                  start=True, stop=False),
               reads=[("alrP",), ("c", "onr")], writes=[("P", 7)])
            PE(lambda e: e.matmul(pb[7][0:NS, 0:NH], ones_r[:, 0:NS], alrP[:, 1, :],
                                  start=False, stop=True),
               reads=[("alrP",)], writes=[("P", 7)], end=True)
            OP("vector", lambda e: e.tensor_copy(expal_s, pb[7][0:NS, 0:NH]),
               reads=[("P", 7)], writes=[("expal",)])
            OP("vector", lambda e, cs=cs: e.tensor_copy(ctbf, xcbf[64:128, cs, 32, :]),
               reads=[("xc", cs, 32)], writes=[("ctbf",)])
            for jt in range(2):
                PE(lambda e, jt=jt, cs=cs: e.matmul(
                    pb[0][:, jt * 256:(jt + 1) * 256],
                    xcbf[0:64, cs, 32, jt * 128:(jt + 1) * 128],
                    ctbf, start=True, stop=True),
                   reads=[("xc", cs, 32), ("ctbf",)], writes=[("P", 0)], end=True)
                OP("vector", lambda e, jt=jt: e.tensor_copy(
                    scoresT[:, jt, :], pb[0][:, jt * 256:(jt + 1) * 256]),
                   reads=[("P", 0)], writes=[("scT", jt)])
                PE(lambda e, jt=jt, cs=cs: e.transpose(
                    pv_xtm[:, 512 + 64 * jt:512 + 64 * (jt + 1)],
                    xcbf[0:64, cs, 32, jt * 128:(jt + 1) * 128], ident_s[0:64, 0:64]),
                   reads=[("xc", cs, 32), ("c", "ident")], writes=[("P", 1)], end=True)
                OP("scalar", lambda e, jt=jt: e.copy(
                    Bt_tm[:, jt, :], pv_xtm[:, 512 + 64 * jt:512 + 64 * (jt + 1)]),
                   reads=[("P", 1)], writes=[("Btm", jt)])
            for hh in range(NH):
                hr = hh % 4
                h4 = hh % 4
                abank = 0 if hh % 2 == 0 else 2
                ybank = (3, 4, 6)[hh % 3]
                PE(lambda e, hh=hh, abank=abank: e.matmul(
                    pb[abank][:, 0:256], onehbf[:, hh * 128:(hh + 1) * 128], ahi,
                    start=True, stop=False),
                   reads=[("ahi",), ("c", "oh")], writes=[("P", 0) if hr == 0 else ("P", 2)])
                PE(lambda e, hh=hh, abank=abank: e.matmul(
                    pb[abank][:, 0:256], onehbf[:, hh * 128:(hh + 1) * 128], alo,
                    start=False, stop=True),
                   reads=[("alo",)], writes=[("P", 0) if hr == 0 else ("P", 2)], end=True)
                for jt in range(2):
                    slot = hr * 2 + jt
                    PE(lambda e, hh=hh, jt=jt, slot=slot, cs=cs: e.transpose(
                        pv_xtm[:, slot * 128:(slot + 1) * 128],
                        xcbf[:, cs, hh, jt * 128:(jt + 1) * 128], ident_s),
                       reads=[("xc", cs, hh), ("c", "ident")], writes=[("P", 1)], end=True)
                    eng = "vector" if jt == 0 else "scalar"
                    OP(eng, (lambda e, jt=jt, slot=slot, h4=h4, isv=(jt == 0):
                             (e.tensor_copy if isv else e.copy)(
                                 x_tm[:, h4, jt * 128:(jt + 1) * 128],
                                 pv_xtm[:, slot * 128:(slot + 1) * 128])),
                       reads=[("P", 1)], writes=[("xtm", h4)])
                for jt in range(2):
                    OP("vector", lambda e, jt=jt, hr=hr, abank=abank: e.tensor_tensor(
                        wt1[:, jt, :], negm_s[:, jt, :], pb[abank][:, 0:256], ALU.add),
                       reads=[("P", 0) if hr == 0 else ("P", 2), ("c", "nm")], writes=[("wt1", jt)])
                    OP("scalar", lambda e, jt=jt, hh=hh: e.activation(
                        lm[:, jt, :], wt1[:, jt, :], AF.Exp, bias=naT[:, jt, hh:hh + 1]),
                       reads=[("wt1", jt), ("naT", jt)], writes=[("lm", jt)])
                    OP("vector", lambda e, jt=jt, hh=hh, hr=hr: e.scalar_tensor_tensor(
                        wq[:, hr, jt, :], lm[:, jt, :], dtcT[:, jt, hh:hh + 1],
                        scoresT[:, jt, :], ALU.mult, ALU.mult),
                       reads=[("lm", jt), ("dtcT", jt), ("scT", jt)], writes=[("wq", hr, jt)])
                OP("scalar", lambda e, hr=hr, abank=abank: e.activation(
                    expab[:, hr, :], pb[abank][0:NS, 0:256], AF.Exp),
                   reads=[("P", 0) if hr == 0 else ("P", 2)], writes=[("expab", hr)])
                OP("vector", lambda e, hr=hr: e.tensor_tensor(
                    Ce[:, hr, :], ctbf, expab[:, hr, :], ALU.mult),
                   reads=[("expab", hr), ("ctbf",)], writes=[("Ce", hr)])
                PE(lambda e, hr=hr, h4=h4, ybank=ybank: e.matmul(
                    pb[ybank][:, 0:256], x_tm[:, h4, 0:128], wq[:, hr, 0, :],
                    start=True, stop=False),
                   reads=[("xtm", h4), ("wq", hr, 0)], writes=[("P", 3 + hr)])
                PE(lambda e, hr=hr, h4=h4, ybank=ybank: e.matmul(
                    pb[ybank][:, 0:256], x_tm[:, h4, 128:256], wq[:, hr, 1, :],
                    start=False, stop=False),
                   reads=[("xtm", h4), ("wq", hr, 1)], writes=[("P", 3 + hr)])
                PE(lambda e, hr=hr, hh=hh, ybank=ybank: e.matmul(
                    pb[ybank][:, 0:256], Sbf[:, hh, :], Ce[:, hr, :],
                    start=False, stop=True),
                   reads=[("Sbf",), ("Ce", hr)], writes=[("P", 3 + hr)], end=True)
                for jt in range(2):
                    OP("vector", lambda e, jt=jt, hh=hh, hr=hr: e.tensor_scalar(
                        Bd[:, hr, jt, :], Bt_tm[:, jt, :], GT[:, jt, hh:hh + 1], None, ALU.mult),
                       reads=[("Btm", jt), ("GT", jt)], writes=[("Bd", hr, jt)])
                sh = hh % 2
                PE(lambda e, hr=hr, h4=h4, sh=sh: e.matmul(
                    pb[5][0:NS, 128 * sh:128 * (sh + 1)], Bd[:, hr, 0, :],
                    x_tm[:, h4, 0:128], start=True, stop=False),
                   reads=[("Bd", hr, 0), ("xtm", h4)], writes=[("P", 5)])
                PE(lambda e, hr=hr, h4=h4, sh=sh: e.matmul(
                    pb[5][0:NS, 128 * sh:128 * (sh + 1)], Bd[:, hr, 1, :],
                    x_tm[:, h4, 128:256], start=False, stop=True),
                   reads=[("Bd", hr, 1), ("xtm", h4)], writes=[("P", 5)], end=True)
                OP("vector", lambda e, hr=hr, hh=hh, cs=cs, ybank=ybank: e.scalar_tensor_tensor(
                    ysb[:, hr, :], xcbf[:, cs, hh, :], drep_s[:, hh:hh + 1],
                    pb[ybank][:, 0:256], ALU.mult, ALU.add),
                   reads=[("xc", cs, hh), ("c", "dr"), ("P", 3 + hr)], writes=[("ysb", hr)])
                OP("vector", lambda e, hr=hr, hh=hh: e.tensor_tensor(
                    yg[:, hr, :], ysb[:, hr, :], ztile[:, hh, :], ALU.mult),
                   reads=[("ysb", hr), ("zt", hh)], writes=[("yg", hr)])
                OP("scalar", lambda e, hr=hr: e.activation(
                    sqbf[:, hr, :], yg[:, hr, :], AF.Square),
                   reads=[("yg", hr)], writes=[("sqbf", hr)])
                PE(lambda e, hr=hr, ybank=ybank: e.matmul(
                    pb[ybank][0:1, 0:256], ones_c, sqbf[:, hr, :], start=True, stop=True),
                   reads=[("sqbf", hr), ("c", "onc")], writes=[("P", ybank)], end=True)
                OP("vector", lambda e, hr=hr, ybank=ybank: e.tensor_scalar(
                    rr1[:, hr, :], pb[ybank][0:1, 0:256], 1.0 / HD, 1e-6, ALU.mult, ALU.add),
                   reads=[("P", ybank)], writes=[("rr1", hr)])
                OP("scalar", lambda e, hr=hr: e.activation(
                    rr2[:, hr, :], rr1[:, hr, :], AF.Ln),
                   reads=[("rr1", hr)], writes=[("rr2", hr)])
                OP("scalar", lambda e, hr=hr: e.activation(
                    rr1[:, hr, :], rr2[:, hr, :], AF.Exp, scale=-0.5),
                   reads=[("rr2", hr)], writes=[("rr1", hr)])
                OP("scalar", lambda e, hr=hr: e.copy(rsp[:, hr, 0, :], rr1[:, hr, :]),
                   reads=[("rr1", hr)], writes=[("rsp", hr)])
                OP("vector", lambda e, hr=hr: e.tensor_tensor(
                    rsp[:, hr, 1, :], rr1[:, hr, :], rsp[:, hr, 0, :], ALU.subtract),
                   reads=[("rr1", hr), ("rsp", hr)], writes=[("rsp", hr)])
                PE(lambda e, hr=hr: e.matmul(
                    pb[7][:, 0:256], ones_r, rsp[:, hr, 0, :], start=True, stop=False),
                   reads=[("rsp", hr), ("c", "onr")], writes=[("P", 7)])
                PE(lambda e, hr=hr: e.matmul(
                    pb[7][:, 0:256], ones_r, rsp[:, hr, 1, :], start=False, stop=True),
                   reads=[("rsp", hr)], writes=[("P", 7)], end=True)
                OP("vector", lambda e, hr=hr: e.tensor_tensor(
                    ysb[:, hr, :], yg[:, hr, :], pb[7][:, 0:256], ALU.mult),
                   reads=[("yg", hr), ("P", 7)], writes=[("ysb", hr)])
                OP("vector", lambda e, hr=hr, hh=hh, h4=h4: e.tensor_scalar(
                    ynst[:, h4, :], ysb[:, hr, :], nwt_s[:, hh:hh + 1], None, ALU.mult),
                   reads=[("ysb", hr), ("c", "nw")], writes=[("ynst", h4)])
                DMA("sync", "qs", Ysp[hh][:, c0:c0 + CH], ynst[:, h4, :],
                    reads=[("ynst", h4)])
                OP("vector", lambda e, hr=hr, hh=hh: e.tensor_scalar(
                    Stmp[:, hr, :], S_b[:, hh, :], expal_s[:, hh:hh + 1], None, ALU.mult),
                   reads=[("S",), ("expal",)], writes=[("Stmp", hr)])
                OP("vector", lambda e, hr=hr, hh=hh, sh=sh: e.tensor_tensor(
                    S_b[:, hh, :], pb[5][0:NS, 128 * sh:128 * (sh + 1)],
                    Stmp[:, hr, :], ALU.add),
                   reads=[("P", 5), ("Stmp", hr)], writes=[("S",)])
                OP("scalar", lambda e, hh=hh: e.copy(Sbf[:, hh, :], S_b[:, hh, :]),
                   reads=[("S",)], writes=[("Sbf",)])
    ctl.fence()


    if phase_limit >= 4:
        phA.reset(ph0)
        bigA.reset(big0)
        opjT = bigA.alloc("opjT", [128, 32, D], BF16)
        ynb = phA.alloc("ynb", [128, 2, 32, 256], BF16)
        sst = phA.alloc("sst", [128, 2, D], F32)
        for kt in range(32):
            DMA("gpsimd", "qg", opjT[:, kt, :], opt[:, kt, :], writes=[("opj", kt)])
        for ts in range(16):
            sl = ts % 2
            DMA("sync", "qs", ynb[:, sl],
                Ysp[:, :, ts * 256:(ts + 1) * 256].rearrange("k p t -> p k t"),
                writes=[("ynb", sl)])
            for tt2 in range(2):
                tt = ts * 2 + tt2
                bank0 = (tt % 2) * 4
                for k in range(32):
                    for eb in range(4):
                        fn = lambda e, sl=sl, k=k, tt2=tt2, eb=eb, b=bank0: e.matmul(
                            pb[b + eb], ynb[:, sl, k, tt2 * 128:(tt2 + 1) * 128],
                            opjT[:, k, eb * 512:(eb + 1) * 512],
                            start=(k == 0), stop=(k == 31))
                        PE(fn, reads=[("ynb", sl), ("opj", k)],
                           writes=[("P", bank0 + eb)], end=(k == 31 and eb == 3))
                for eb in range(4):
                    eng = "vector" if eb % 2 == 0 else "scalar"
                    fn = (lambda e, eb=eb, b=bank0, tt=tt, isv=(eb % 2 == 0):
                          (e.tensor_copy if isv else e.copy)(
                              sst[:, tt % 2, eb * 512:(eb + 1) * 512], pb[b + eb]))
                    OP(eng, fn, reads=[("P", bank0 + eb)], writes=[("sst", tt % 2, eb)])
                DMA("sync", "qs", Ssp.rearrange("(n p) d -> n p d", p=128)[tt],
                    sst[:, tt % 2, :], reads=[("sst", tt % 2, eb) for eb in range(4)])
    ctl.fence()

    if phase_limit >= 5:
        phA.reset(ph0)
        bigA.reset(big0)
        wdT = bigA.alloc("wdT", [128, 128, 512], BF16)
        d_w = phA.alloc("d_w", [128, 512], U16)
        d_l = phA.alloc("d_l", [128, 512], U16)
        d_h = phA.alloc("d_h", [128, 512], U16)
        d_d = phA.alloc("d_d", [128, 512], U16)
        d_e = phA.alloc("d_e", [128, 512], U16)
        dpkb = phA.alloc("dpkb", [128, 2, 512], U8)
        dscb = phA.alloc("dscb", [128, 2, 512], BF16)
        inb = phA.alloc("inb", [128, 4, 8, 256], BF16)
        stb = phA.alloc("stb", [128, 2, 512], F32)
        xtb = phA.alloc("xtb", [128, 2, 512], F32)
        glb = phA.alloc("glb", [128, 2, 512], F32)
        otb = phA.alloc("otb", [128, 2, 512], F32)
        srr = Ssp.rearrange("(n p) d -> n p d", p=128)
        orr = out_d.rearrange("(n p) d -> n p d", p=128)
        for hb in range(4):
            for W in range(64):
                sl = W % 2
                DMA("gpsimd", "qg", dpkb[:, sl, :],
                    dpk[:, (hb * 64 + W) * 512:(hb * 64 + W + 1) * 512],
                    writes=[("dpkb", sl)])
                DMA("gpsimd", "qg", dscb[:, sl, :], dsc[hb, W], writes=[("dscb", sl)])
                OP("vector", lambda e, sl=sl: e.tensor_copy(d_w, dpkb[:, sl, :]),
                   reads=[("dpkb", sl)], writes=[("dw",)])
                OP("vector", lambda e: e.tensor_scalar(d_l, d_w, 15, None, ALU.bitwise_and),
                   reads=[("dw",)], writes=[("dl",)])
                OP("vector", lambda e: e.tensor_scalar(d_h, d_w, 4, None, ALU.logical_shift_right),
                   reads=[("dw",)], writes=[("dh",)])
                for par in range(2):
                    src, ksrc = (d_l, ("dl",)) if par == 0 else (d_h, ("dh",))
                    if par == 0:
                        d12, kd12, b6, kb6 = d_d, ("dd",), d_l, ("dl",)
                        ff, kf, alt, kalt = d_e, ("de",), d_l, ("dl",)
                        outb, kout = d_d, ("dd",)
                    else:
                        d12, kd12, b6, kb6 = d_l, ("dl",), d_e, ("de",)
                        ff, kf, alt, kalt = d_h, ("dh",), d_e, ("de",)
                        outb, kout = d_l, ("dl",)
                    OP("vector", lambda e, src=src: e.tensor_scalar(d_w, src, 7, None, ALU.bitwise_and),
                       reads=[ksrc], writes=[("dw",)])
                    OP("vector", lambda e, src=src, d12=d12: e.tensor_scalar(
                        d12, src, 8, 12, ALU.bitwise_and, ALU.logical_shift_left),
                       reads=[ksrc], writes=[kd12])
                    OP("vector", lambda e, b6=b6: e.tensor_scalar(
                        b6, d_w, 6, None, ALU.logical_shift_left),
                       reads=[("dw",)], writes=[kb6])
                    OP("vector", lambda e, b6=b6, d12=d12, ff=ff: e.scalar_tensor_tensor(
                        ff, b6, 16128, d12, ALU.add, ALU.add),
                       reads=[kb6, kd12], writes=[kf])
                    OP("vector", lambda e, d12=d12, alt=alt: e.scalar_tensor_tensor(
                        alt, d_w, 16128, d12, ALU.mult, ALU.add),
                       reads=[("dw",), kd12], writes=[kalt])
                    OP("vector", lambda e: e.tensor_scalar(d_w, d_w, 2, None, ALU.is_lt),
                       reads=[("dw",)], writes=[("dw",)])
                    OP("vector", lambda e, alt=alt, ff=ff, outb=outb: e.select(
                        outb, d_w, alt, ff, add_drain=True),
                       reads=[("dw",), kalt, kf], writes=[kout])
                    OP("vector", lambda e, outb=outb, W=W, par=par, sl=sl: e.tensor_tensor(
                        wdT[:, 2 * W + par, :], outb.bitcast(BF16), dscb[:, sl, :], ALU.mult),
                       reads=[kout, ("dscb", sl)], writes=[("wdT", 2 * W + par)])
            for tp in range(16):
                slp = tp % 2
                for kgi in range(19):
                    if kgi < 16:
                        kg_p = kgi
                        DMA("sync" if kg_p % 2 == 0 else "scalar",
                            "qs" if kg_p % 2 == 0 else "qa", inb[:, kg_p % 4, :, :],
                            Isp[kg_p * 8:(kg_p + 1) * 8, :, tp * 256:(tp + 1) * 256].rearrange(
                                "k p t -> p k t"),
                            writes=[("inb", kg_p % 4)])
                    if kgi < 3:
                        continue
                    kg = kgi - 3
                    for k8 in range(8):
                        k = kg * 8 + k8
                        for tt2 in range(2):
                            bank = slp * 2 + tt2
                            fn = lambda e, kg=kg, k8=k8, k=k, tt2=tt2, bank=bank: e.matmul(
                                pb[bank], inb[:, kg % 4, k8, tt2 * 128:(tt2 + 1) * 128],
                                wdT[:, k, :], start=(k == 0), stop=(k == 127))
                            PE(fn, reads=[("inb", kg % 4), ("wdT", k)],
                               writes=[("P", bank)],
                               end=(k8 == 7 and tt2 == 1))
                for tt2 in range(2):
                    tt = tp * 2 + tt2
                    bank = slp * 2 + tt2
                    DMA("gpsimd", "qg", stb[:, tt2, :], srr[tt][:, hb * 512:(hb + 1) * 512],
                        writes=[("stb", tt2)])
                    DMA("gpsimd", "qg", xtb[:, tt2, :],
                        xr[tt][:, hb * 512:(hb + 1) * 512], writes=[("xtb", tt2)])
                    OP("scalar", lambda e, bank=bank, tt2=tt2: e.activation(
                        glb[:, tt2, :], pb[bank], AF.Silu),
                       reads=[("P", bank)], writes=[("glb", tt2)])
                    OP("vector", lambda e, tt2=tt2: e.tensor_tensor(
                        otb[:, tt2, :], glb[:, tt2, :], stb[:, tt2, :], ALU.mult),
                       reads=[("glb", tt2), ("stb", tt2)], writes=[("otb", tt2)])
                    OP("vector", lambda e, tt2=tt2: e.tensor_tensor(
                        otb[:, tt2, :], otb[:, tt2, :], xtb[:, tt2, :], ALU.add),
                       reads=[("otb", tt2), ("xtb", tt2)], writes=[("otb", tt2)])
                    DMA("gpsimd", "qg", orr[tt][:, hb * 512:(hb + 1) * 512],
                        otb[:, tt2, :], reads=[("otb", tt2)])
    ctl.fence()

    if dev_spill and phase_limit < 3:
        xntdump = ap(nc.dram_tensor("xntdump", [128, 16, T], BF16, kind="ExternalOutput"))
        DMA("sync", "qs", xntdump, xnT)

    ctl.fence()

    with nc.Block() as blk:
        for eng_name, meth in [("tensor", blk.tensor), ("vector", blk.vector),
                               ("scalar", blk.scalar), ("gpsimd", blk.gpsimd),
                               ("sync", blk.sync)]:
            ops = ctl.ops[eng_name]
            def body(e, ops=ops):
                for f in ops:
                    f(e)
            meth(body)
    stack.close()
    return nc


def kernel(**inputs):
    h = host_prep(inputs)
    u = np.asarray(inputs["L0_attn_u"])
    nc = build_nc()
    cores = CORE_IDS if CORE_IDS is not None else list(range(N_CORES))
    in_maps = []
    for b in range(len(cores)):
        m = dict(h)
        m["x"] = np.ascontiguousarray(u[b])
        in_maps.append(m)
    res = run_bass_kernel_spmd(nc, in_maps, core_ids=cores)
    return np.stack([r["out"] for r in res.results], axis=0)


# version 06
# speedup vs baseline: 1.2961x; 1.1004x over previous
import contextlib
import numpy as np
import ml_dtypes
import concourse.bass as bass
import concourse.mybir as mybir
from concourse.bass_utils import run_bass_kernel_spmd

F32 = mybir.dt.float32
BF16 = mybir.dt.bfloat16
U8 = mybir.dt.uint8
U16 = mybir.dt.uint16
AF = mybir.ActivationFunctionType
ALU = mybir.AluOpType
AX = mybir.AxisListType.X

T = 4096
D = 2048
EP = 8352
ET = 66
DI = 4096
CONV = 4224
NCT = 33
NH = 32
HD = 128
NS = 64
CH = 256
NCH = 16
MLP = 16384
NSB = 64
NEG = -1e30

NSB_LIM = NSB
P2_SKIP = set()
PHASE_LIMIT = 99
DEV_SPILL = False
N_CORES = 8
CORE_IDS = None

_DTSZ = {F32: 4, BF16: 2, U8: 1, U16: 2}


def _perm_d():
    p = []
    for w in range(8):
        p += [256 * w + 2 * i for i in range(128)]
        p += [256 * w + 2 * i + 1 for i in range(128)]
    return np.array(p, np.int64)


def host_prep(inputs):
    h = {}
    perm = _perm_d()
    ipw = np.asarray(inputs["L0_attn_in_proj_w"])
    wt = np.ascontiguousarray(ipw.T[perm])
    wt = np.concatenate([wt, np.zeros((D, ET * 128 - EP), np.float32)], axis=1)
    h["wipt"] = np.ascontiguousarray(
        wt.reshape(16, 128, ET, 128).transpose(1, 2, 0, 3))

    def pack_gateup(wp):
        u8 = np.asarray(wp).astype(np.int32).view(np.uint8)[:, 0::4]
        t = np.ascontiguousarray(u8.T)
        t = t.reshape(8, 128, NSB, 256).transpose(1, 2, 0, 3)
        return np.ascontiguousarray(t.reshape(128, NSB * 8 * 256))

    h["gpk"] = pack_gateup(inputs["L0_ffn_gate_W_packed"])
    h["upk"] = pack_gateup(inputs["L0_ffn_up_W_packed"])

    def pack_scales(sc):
        t = np.asarray(sc).T.reshape(8, 8, NSB, 256)
        g = t.transpose(2, 1, 0, 3)
        return np.ascontiguousarray(np.repeat(g, 16, axis=1))

    h["gsc"] = pack_scales(inputs["L0_ffn_gate_scales"])
    h["usc"] = pack_scales(inputs["L0_ffn_up_scales"])

    dpw = np.asarray(inputs["L0_ffn_down_W_packed"]).astype(np.int32).view(np.uint8)[:, 0::4]
    dw = np.ascontiguousarray(dpw.T)
    dw = dw.reshape(64, 128, 4, 512).transpose(1, 2, 0, 3)
    h["dpk"] = np.ascontiguousarray(dw.reshape(128, 4 * 64 * 512))

    dsc = np.asarray(inputs["L0_ffn_down_scales"]).T.reshape(64, 8, 4, 512)
    dsc = dsc.transpose(2, 0, 1, 3)
    h["dsc"] = np.ascontiguousarray(np.repeat(dsc, 16, axis=2))

    opw = np.asarray(inputs["L0_attn_o_proj_w"])
    ot = np.ascontiguousarray(opw.T)
    h["opt"] = np.ascontiguousarray(ot.reshape(32, 128, D).transpose(1, 0, 2))

    cw = np.asarray(inputs["L0_attn_conv_w"])
    h["convw"] = np.ascontiguousarray(cw.reshape(NCT, 128, 4).transpose(1, 0, 2))
    h["convb"] = np.ascontiguousarray(
        np.asarray(inputs["L0_attn_conv_b"]).reshape(NCT, 128).T)

    h["alog"] = np.ascontiguousarray(np.asarray(inputs["L0_attn_A_log"]).reshape(NH, 1)).astype(np.float32)
    h["dtb"] = np.ascontiguousarray(np.asarray(inputs["L0_attn_dt_bias"]).reshape(NH, 1)).astype(np.float32)
    h["drep"] = np.ascontiguousarray(
        np.broadcast_to(np.asarray(inputs["L0_attn_D_skip"])[None, :], (128, NH))).astype(np.float32)
    h["nwt"] = np.ascontiguousarray(
        np.asarray(inputs["L0_attn_norm_w"]).reshape(NH, 128).T).astype(np.float32)

    h["identbf"] = np.eye(128, dtype=ml_dtypes.bfloat16)
    jj = np.arange(128)[:, None]
    ii = np.arange(CH)[None, :]
    nm = np.zeros((128, 2, CH), np.float32)
    nm[:, 0, :] = np.where(jj <= ii, 0.0, NEG)
    nm[:, 1, :] = np.where(jj + 128 <= ii, 0.0, NEG)
    h["negmask"] = nm
    oh = np.zeros((NH, NH * 128), np.float32)
    for hh in range(NH):
        oh[hh, hh * 128:(hh + 1) * 128] = 1.0
    h["onehot"] = oh
    h["onehot"] = h["onehot"].astype(ml_dtypes.bfloat16)
    h["ones_c"] = np.ones((128, 1), ml_dtypes.bfloat16)
    h["ones_r"] = np.ones((1, 128), ml_dtypes.bfloat16)
    return h


class Arena:

    def __init__(self, nc, base, limit):
        self.nc, self.base, self.limit, self.cur = nc, base, limit, base
        self.n = 0

    def alloc(self, name, shape, dtype):
        nbytes = int(np.prod(shape[1:])) * _DTSZ[dtype]
        nbytes = (nbytes + 31) // 32 * 32
        assert self.cur + nbytes <= self.limit, \
            f"arena overflow at {name}: {self.cur + nbytes} > {self.limit}"
        t = self.nc.alloc_sbuf_tensor_at(f"{name}_{self.n}", shape, dtype, offset=self.cur)
        self.n += 1
        self.cur += nbytes
        return t.ap()

    def mark(self):
        return self.cur

    def reset(self, saved=None):
        self.cur = self.base if saved is None else saved


class Ctl:

    ENGINES = ("tensor", "vector", "scalar", "gpsimd", "sync")
    SELF = {"vector": "sv", "scalar": "ss", "gpsimd": "sg"}

    LANES = {"qs": 6, "qg": 6, "qa": 4}

    def __init__(self, sems):
        self.ops = {e: [] for e in self.ENGINES}
        self.cnt = {}
        self.sems = sems
        self.lanes = {q: [f"{q}{i}" for i in range(n)] for q, n in self.LANES.items()}
        self.lane_rr = {q: 0 for q in self.LANES}
        self.last_w = {}
        self.readers = {}
        self.watermark = {e: {} for e in self.ENGINES}
        self.pe_pending_w = []
        self.pe_pending_r = []

    def _bump(self, sem, k=1):
        self.cnt[sem] = self.cnt.get(sem, 0) + k
        return self.cnt[sem]

    def val(self, sem):
        return self.cnt.get(sem, 0)

    def _edge(self, eng, dep):
        if dep is None:
            return
        sem, v = dep
        if v <= self.watermark[eng].get(sem, 0):
            return
        self.watermark[eng][sem] = v
        s = self.sems[sem]
        self.ops[eng].append(lambda e, s=s, v=v: e.wait_ge(s, v))

    def _deps(self, eng, reads, writes):
        for key in reads:
            self._edge(eng, self.last_w.get(key))
        for key in writes:
            self._edge(eng, self.last_w.get(key))
            for d in self.readers.get(key, ()):
                self._edge(eng, d)

    def _commit(self, reads, writes, completion):
        for key in writes:
            self.last_w[key] = completion
            self.readers[key] = []
        for key in reads:
            self.readers.setdefault(key, []).append(completion)

    def op(self, eng, fn, reads=(), writes=()):
        sem = self.SELF[eng]
        self._deps(eng, reads, writes)
        v = self._bump(sem)
        s = self.sems[sem]
        self.ops[eng].append(lambda e, fn=fn, s=s: fn(e).then_inc(s, 1))
        comp = (sem, v)
        self._commit(reads, writes, comp)
        return comp

    def pe(self, fn, reads=(), writes=(), end=False):
        self._deps("tensor", reads, writes)
        self.pe_pending_w.extend(writes)
        self.pe_pending_r.extend(reads)
        if end:
            v = self._bump("pe")
            s = self.sems["pe"]
            self.ops["tensor"].append(lambda e, fn=fn, s=s: fn(e).then_inc(s, 1))
            comp = ("pe", v)
            self._commit(self.pe_pending_r, self.pe_pending_w, comp)
            self.pe_pending_w, self.pe_pending_r = [], []
            return comp
        self.ops["tensor"].append(fn)
        return None

    def dma(self, eng, q, out, in_, reads=(), writes=()):
        lanes = self.lanes[q]
        i = self.lane_rr[q]
        self.lane_rr[q] = (i + 1) % len(lanes)
        lane = lanes[i]
        prev = self.val(lane)
        self._edge(eng, (lane, prev) if prev > 0 else None)
        self._deps(eng, reads, writes)
        v = self._bump(lane, 16)
        s = self.sems[lane]
        self.ops[eng].append(
            lambda e, out=out, in_=in_, s=s: e.dma_start(out=out, in_=in_).then_inc(s, 16))
        comp = (lane, v)
        self._commit(reads, writes, comp)
        return comp

    def fence(self):
        deps = []
        for sem in ("sv", "ss", "sg", "pe"):
            if self.val(sem) > 0:
                deps.append((sem, self.val(sem)))
        for q in self.lanes:
            for lane in self.lanes[q]:
                if self.val(lane) > 0:
                    deps.append((lane, self.val(lane)))
        for eng in self.ENGINES:
            for d in deps:
                self._edge(eng, d)
        self.last_w = {}
        self.readers = {}


def build_nc(phase_limit=None, dev_spill=None):
    if phase_limit is None:
        phase_limit = PHASE_LIMIT
    if dev_spill is None:
        dev_spill = DEV_SPILL
    nc = bass.Bass("TRN2", target_bir_lowering=False)

    ap = lambda t: t.ap()
    x_in = ap(nc.dram_tensor("x", [T, D], F32, kind="ExternalInput"))
    wipt = ap(nc.dram_tensor("wipt", [128, ET, 16, 128], F32, kind="ExternalInput"))
    gpk = ap(nc.dram_tensor("gpk", [128, NSB * 2048], U8, kind="ExternalInput"))
    upk = ap(nc.dram_tensor("upk", [128, NSB * 2048], U8, kind="ExternalInput"))
    gsc = ap(nc.dram_tensor("gsc", [64, 128, 8, 256], F32, kind="ExternalInput"))
    usc = ap(nc.dram_tensor("usc", [64, 128, 8, 256], F32, kind="ExternalInput"))
    dpk = ap(nc.dram_tensor("dpk", [128, 4 * 64 * 512], U8, kind="ExternalInput"))
    dsc = ap(nc.dram_tensor("dsc", [4, 64, 128, 512], F32, kind="ExternalInput"))
    opt = ap(nc.dram_tensor("opt", [128, 32, D], F32, kind="ExternalInput"))
    convw_d = ap(nc.dram_tensor("convw", [128, NCT, 4], F32, kind="ExternalInput"))
    convb_d = ap(nc.dram_tensor("convb", [128, NCT], F32, kind="ExternalInput"))
    alog_d = ap(nc.dram_tensor("alog", [NH, 1], F32, kind="ExternalInput"))
    dtb_d = ap(nc.dram_tensor("dtb", [NH, 1], F32, kind="ExternalInput"))
    drep_d = ap(nc.dram_tensor("drep", [128, NH], F32, kind="ExternalInput"))
    nwt_d = ap(nc.dram_tensor("nwt", [128, NH], F32, kind="ExternalInput"))
    ident_d = ap(nc.dram_tensor("identbf", [128, 128], BF16, kind="ExternalInput"))
    negm_d = ap(nc.dram_tensor("negmask", [128, 2, CH], F32, kind="ExternalInput"))
    onehot_d = ap(nc.dram_tensor("onehot", [NH, NH * 128], BF16, kind="ExternalInput"))
    ones_c_d = ap(nc.dram_tensor("ones_c", [128, 1], BF16, kind="ExternalInput"))
    ones_r_d = ap(nc.dram_tensor("ones_r", [1, 128], BF16, kind="ExternalInput"))

    skind = "ExternalOutput" if dev_spill else "Internal"
    Z = ap(nc.dram_tensor("Zsp", [65, 128, T], F32, kind=skind))
    Isp = ap(nc.dram_tensor("Isp", [128, 128, T], BF16, kind=skind))
    Ysp = ap(nc.dram_tensor("Ysp", [32, 128, T], BF16, kind=skind))
    Ssp = ap(nc.dram_tensor("Ssp", [T, D], F32, kind=skind))
    DTsp = ap(nc.dram_tensor("DTsp", [NH, T], F32, kind=skind))
    drtmp = ap(nc.dram_tensor("drtmp", [NH], F32, kind="Internal"))
    DdT = ap(nc.dram_tensor("DdT", [4, 128, 128, 512], BF16, kind="Internal"))
    out_d = ap(nc.dram_tensor("out", [T, D], F32, kind="ExternalOutput"))

    sbuf = lambda name, shape, dt: nc.alloc_sbuf_tensor(name, shape, dt).ap()
    ident_s = sbuf("ident_s", [128, 128], BF16)
    ones_c = sbuf("ones_c_s", [128, 1], BF16)
    ones_r = sbuf("ones_r_s", [1, 128], BF16)
    alog_s = sbuf("alog_s", [NH, 1], F32)
    dtb_s = sbuf("dtb_s", [NH, 1], F32)
    A_s = sbuf("A_s", [NH, 1], F32)

    import concourse.bass as _b
    perm_end = nc.bytes_allocated(_b.MemorySpace.SBUF, "left")
    cap = nc.sbuf_top
    BIGB = (perm_end + 31) // 32 * 32
    PHB = BIGB + 131072
    assert PHB < cap, (BIGB, cap)
    bigA = Arena(nc, BIGB, PHB)
    phA = Arena(nc, PHB, cap)

    big0 = bigA.mark()
    xnT = bigA.alloc("xnT", [128, 16, T], BF16)

    pb = [nc.alloc_psum_tensor(f"pb{i}", [128, 512], F32).ap() for i in range(8)]

    stack = contextlib.ExitStack()
    sems = {}
    semnames = ["pe", "sv", "ss", "sg"]
    for q, n in Ctl.LANES.items():
        semnames += [f"{q}{i}" for i in range(n)]
    for name in semnames:
        sems[name] = stack.enter_context(nc.semaphore(name))
    ctl = Ctl(sems)
    OP, PE, DMA = ctl.op, ctl.pe, ctl.dma

    for dst, src, nm in [(ident_s, ident_d, "ident"), (ones_c, ones_c_d, "onc"),
                         (ones_r, ones_r_d, "onr"), (alog_s, alog_d, "alog"),
                         (dtb_s, dtb_d, "dtb")]:
        DMA("gpsimd", "qg", dst, src, writes=[("c", nm)])
    OP("scalar", lambda e: e.activation(A_s, alog_s, AF.Exp),
       reads=[("c", "alog")], writes=[("A",)])
    OP("vector", lambda e: e.tensor_scalar(A_s, A_s, -1.0, None, ALU.mult),
       reads=[("A",)], writes=[("A",)])

    ph0 = phA.mark()
    xb = phA.alloc("xb", [128, 2, D], F32)
    xnbf = phA.alloc("xnbf", [128, 2, D], BF16)
    stats = phA.alloc("stats", [128, 8], F32)
    sqs = phA.alloc("sqs", [128, D], F32)

    NT = T // 128
    xr = x_in.rearrange("(n p) d -> n p d", p=128)
    for i in range(NT):
        sl = i % 2
        DMA("sync", "qs", xb[:, sl, :], xr[i], writes=[("xb", sl)])
        OP("scalar", lambda e, sl=sl: e.activation(sqs, xb[:, sl, :], AF.Square),
           reads=[("xb", sl)], writes=[("sqs",)])
        OP("vector", lambda e, sl=sl: e.reduce_sum(stats[:, 0:1], xb[:, sl, :], axis=AX),
           reads=[("xb", sl)], writes=[("stats",)])
        OP("vector", lambda e: e.tensor_scalar(stats[:, 2:3], stats[:, 0:1], 1.0 / D, None, ALU.mult),
           reads=[("stats",)], writes=[("stats",)])
        OP("vector", lambda e: e.reduce_sum(stats[:, 1:2], sqs, axis=AX),
           reads=[("sqs",)], writes=[("stats",)])
        OP("vector", lambda e: e.tensor_scalar(stats[:, 3:4], stats[:, 1:2], 1.0 / D, None, ALU.mult),
           reads=[("stats",)], writes=[("stats",)])
        OP("vector", lambda e: e.tensor_tensor(stats[:, 4:5], stats[:, 2:3], stats[:, 2:3], ALU.mult),
           reads=[("stats",)], writes=[("stats",)])
        OP("vector", lambda e: e.tensor_tensor(stats[:, 5:6], stats[:, 3:4], stats[:, 4:5], ALU.subtract),
           reads=[("stats",)], writes=[("stats",)])
        OP("vector", lambda e: e.tensor_scalar(stats[:, 5:6], stats[:, 5:6], 1e-5, None, ALU.add),
           reads=[("stats",)], writes=[("stats",)])
        OP("scalar", lambda e: e.activation(stats[:, 7:8], stats[:, 5:6], AF.Ln),
           reads=[("stats",)], writes=[("stats",)])
        OP("scalar", lambda e: e.activation(stats[:, 6:7], stats[:, 7:8], AF.Exp, scale=-0.5),
           reads=[("stats",)], writes=[("stats",)])
        OP("vector", lambda e, sl=sl: e.tensor_scalar(sqs, xb[:, sl, :], stats[:, 2:3], None, ALU.subtract),
           reads=[("xb", sl), ("stats",)], writes=[("sqs",)])
        OP("vector", lambda e, sl=sl: e.tensor_scalar(xnbf[:, sl, :], sqs, stats[:, 6:7], None, ALU.mult),
           reads=[("sqs",), ("stats",)], writes=[("xnbf", sl)])
        for g in range(4):
            bank = 4 + (i * 4 + g) % 4
            pv = pb[bank].bitcast(BF16)
            for t4 in range(4):
                kt = g * 4 + t4
                w, par = kt // 2, kt % 2
                src = xnbf[:, sl, 256 * w + par: 256 * w + 256: 2]
                fn = lambda e, pv=pv, t4=t4, src=src: e.transpose(
                    pv[:, t4 * 128:(t4 + 1) * 128], src, ident_s)
                PE(fn, reads=[("xnbf", sl), ("c", "ident")], writes=[("pb", bank)],
                   end=(t4 == 3))
            eng = "vector" if g % 2 == 0 else "scalar"
            isv = eng == "vector"
            for t4 in range(4):
                kt = g * 4 + t4
                fn = (lambda e, pv=pv, t4=t4, kt=kt, i=i, isv=isv:
                      (e.tensor_copy if isv else e.copy)(
                          xnT[:, kt, i * 128:(i + 1) * 128], pv[:, t4 * 128:(t4 + 1) * 128]))
                OP(eng, fn, reads=[("pb", bank)], writes=[("xnt", kt, i)])
    ctl.fence()

    if phase_limit >= 1:
        phA.reset(ph0)
        wbuf = phA.alloc("wbuf", [128, 2, 16, 128], BF16)
        stage = phA.alloc("stage", [128, 2, T], F32)
        sqs1 = phA.alloc("sqs1", [128, D], F32)
        dtT = phA.alloc("dtT", [NH, T], F32)
        dwp = phA.alloc("dwp", [128, 3, 512], U8)
        dsc1 = phA.alloc("dsc1", [128, 4, 512], BF16)
        dw_w = phA.alloc("dw_w", [128, 512], U16)
        dw_l = phA.alloc("dw_l", [128, 512], U16)
        dw_h = phA.alloc("dw_h", [128, 512], U16)
        dw_d = phA.alloc("dw_d", [128, 512], U16)
        dw_e = phA.alloc("dw_e", [128, 512], U16)
        dwo = phA.alloc("dwo", [128, 2, 2, 512], BF16)

        def p1_ddeq_load(u):
            if u >= 256:
                return
            hb, W = u // 64, u % 64
            DMA("scalar", "qa", dwp[:, u % 3, :],
                dpk[:, (hb * 64 + W) * 512:(hb * 64 + W + 1) * 512],
                writes=[("dwp", u % 3)])
            DMA("gpsimd", "qg", dsc1[:, u % 4, :], dsc[hb, W], writes=[("dsc1", u % 4)])

        def p1_ddeq(u):
            if u >= 256:
                return
            hb, W = u // 64, u % 64
            OP("vector", lambda e, u=u: e.tensor_copy(dw_w, dwp[:, u % 3, :]),
               reads=[("dwp", u % 3)], writes=[("dww",)])
            OP("vector", lambda e: e.tensor_scalar(dw_l, dw_w, 15, None, ALU.bitwise_and),
               reads=[("dww",)], writes=[("dwl",)])
            OP("vector", lambda e: e.tensor_scalar(dw_h, dw_w, 4, None, ALU.logical_shift_right),
               reads=[("dww",)], writes=[("dwh",)])
            for par in range(2):
                src, ksrc = (dw_l, ("dwl",)) if par == 0 else (dw_h, ("dwh",))
                if par == 0:
                    d12, kd12, b6, kb6 = dw_d, ("dwd",), dw_l, ("dwl",)
                    ff, kf, alt, kalt = dw_e, ("dwe",), dw_l, ("dwl",)
                    outb, kout = dw_d, ("dwd",)
                else:
                    d12, kd12, b6, kb6 = dw_l, ("dwl",), dw_e, ("dwe",)
                    ff, kf, alt, kalt = dw_h, ("dwh",), dw_e, ("dwe",)
                    outb, kout = dw_l, ("dwl",)
                OP("vector", lambda e, src=src, d12=d12: e.tensor_scalar(
                    d12, src, 8, 12, ALU.bitwise_and, ALU.logical_shift_left),
                   reads=[ksrc], writes=[kd12])
                OP("vector", lambda e, src=src: e.tensor_scalar(
                    dw_w, src, 7, None, ALU.bitwise_and),
                   reads=[ksrc], writes=[("dww",)])
                OP("vector", lambda e, b6=b6: e.tensor_scalar(
                    b6, dw_w, 6, None, ALU.logical_shift_left),
                   reads=[("dww",)], writes=[kb6])
                OP("vector", lambda e, b6=b6, d12=d12, ff=ff: e.scalar_tensor_tensor(
                    ff, b6, 16128, d12, ALU.add, ALU.add),
                   reads=[kb6, kd12], writes=[kf])
                OP("vector", lambda e, d12=d12, alt=alt: e.scalar_tensor_tensor(
                    alt, dw_w, 16128, d12, ALU.mult, ALU.add),
                   reads=[("dww",), kd12], writes=[kalt])
                OP("vector", lambda e: e.tensor_scalar(dw_w, dw_w, 2, None, ALU.is_lt),
                   reads=[("dww",)], writes=[("dww",)])
                OP("vector", lambda e, alt=alt, ff=ff, outb=outb: e.select(
                    outb, dw_w, alt, ff, add_drain=True),
                   reads=[("dww",), kalt, kf], writes=[kout])
                OP("vector", lambda e, outb=outb, u=u, par=par: e.tensor_tensor(
                    dwo[:, u % 2, par, :], outb.bitcast(BF16), dsc1[:, u % 4, :], ALU.mult),
                   reads=[kout, ("dsc1", u % 4)], writes=[("dwo", u % 2, par)])
            for par in range(2):
                DMA("gpsimd", "qg", DdT[hb, 2 * W + par], dwo[:, u % 2, par, :],
                    reads=[("dwo", u % 2, par)])

        p1_ddeq_load(0)
        p1_ddeq_load(1)
        for et in range(ET):
            sl = et % 2
            DMA("gpsimd", "qg", wbuf[:, sl], wipt[:, et], writes=[("wbuf", sl)])
            for uu in range(4 * et, 4 * et + 4):
                p1_ddeq_load(uu + 2)
                p1_ddeq(uu)
            me = 128 if et < 65 else 32
            for tb in range(8):
                bank = (et * 8 + tb) % 8
                psv = pb[bank][0:me, :]
                for k in range(16):
                    fn = lambda e, psv=psv, sl=sl, k=k, tb=tb, me=me: e.matmul(
                        psv, wbuf[:, sl, k, 0:me], xnT[:, k, tb * 512:(tb + 1) * 512],
                        start=(k == 0), stop=(k == 15))
                    PE(fn, reads=[("wbuf", sl)], writes=[("pb", bank)], end=(k == 15))
                if et < 65:
                    eng = "vector" if tb % 2 == 0 else "scalar"
                    isv = eng == "vector"
                    fn = (lambda e, bank=bank, sl=sl, tb=tb, isv=isv:
                          (e.tensor_copy if isv else e.copy)(
                              stage[:, sl, tb * 512:(tb + 1) * 512], pb[bank]))
                    OP(eng, fn, reads=[("pb", bank)], writes=[("stage", sl, tb)])
                else:
                    pz = pb[bank][0:32, :]
                    OP("scalar", lambda e, pz=pz: e.activation(
                        sqs1[0:32, 0:512], pz, AF.Exp, bias=dtb_s),
                       reads=[("pb", bank), ("c", "dtb")], writes=[("sqs1",)])
                    OP("scalar", lambda e, tb=tb: e.activation(
                        dtT[:, tb * 512:(tb + 1) * 512], sqs1[0:32, 0:512], AF.Ln, bias=1.0),
                       reads=[("sqs1",)], writes=[("dtT", tb)])
            if et < 65:
                DMA("sync", "qs", Z[et], stage[:, sl],
                    reads=[("stage", sl, tb) for tb in range(8)])
        DMA("sync", "qs", DTsp, dtT, reads=[("dtT", tb) for tb in range(8)])
    ctl.fence()


    if phase_limit >= 2:
        phA.reset(ph0)
        pkb = phA.alloc("pkb", [128, 3, 2, 2048], U8)
        scw = phA.alloc("scw", [128, 1024], U16)
        scl = phA.alloc("scl", [128, 1024], U16)
        sch = phA.alloc("sch", [128, 1024], U16)
        scd = phA.alloc("scd", [128, 1024], U16)
        sce = phA.alloc("sce", [128, 1024], U16)
        deq = phA.alloc("deq", [128, 2, 2, 2, 8, 256], BF16)
        scsb = phA.alloc("scsb", [128, 2, 2, 8, 256], BF16)
        glt = phA.alloc("glt", [128, 2, 512], BF16)
        upsb = phA.alloc("upsb", [128, 2, 512], BF16)
        istg = phA.alloc("istg", [128, 2, 1024], BF16)

        def p2_loads(sb):
            if sb >= NSB:
                return
            r3, sl = sb % 3, sb % 2
            DMA("scalar", "qa", pkb[:, r3, 0, :], gpk[:, sb * 2048:(sb + 1) * 2048],
                writes=[("pkb", r3, 0)])
            DMA("scalar", "qa", pkb[:, r3, 1, :], upk[:, sb * 2048:(sb + 1) * 2048],
                writes=[("pkb", r3, 1)])
            DMA("gpsimd", "qg", scsb[:, sl, 0], gsc[sb], writes=[("scsb", sl, 0)])
            DMA("gpsimd", "qg", scsb[:, sl, 1], usc[sb], writes=[("scsb", sl, 1)])

        def p2_deq(sb):
            if sb >= NSB:
                return
            r3, sl = sb % 3, sb % 2
            for mat, half in ((0, 0), (0, 1), (1, 0), (1, 1)):
                pk_ap = pkb[:, r3, mat, half * 1024:(half + 1) * 1024]
                hw0 = half * 4
                OP("vector", lambda e, pk_ap=pk_ap: e.tensor_copy(scw, pk_ap),
                   reads=[("pkb", r3, mat)], writes=[("scw",)])
                OP("vector", lambda e: e.tensor_scalar(scl, scw, 15, None, ALU.bitwise_and),
                   reads=[("scw",)], writes=[("scl",)])
                OP("vector", lambda e: e.tensor_scalar(sch, scw, 4, None, ALU.logical_shift_right),
                   reads=[("scw",)], writes=[("sch",)])
                for par in range(2):
                    src, ksrc = (scl, ("scl",)) if par == 0 else (sch, ("sch",))
                    if par == 0:
                        d12, kd12, b6, kb6 = scd, ("scd",), scl, ("scl",)
                        ff, kf, alt, kalt = sce, ("sce",), scl, ("scl",)
                        outb, kout = scd, ("scd",)
                    else:
                        d12, kd12, b6, kb6 = scl, ("scl",), sce, ("sce",)
                        ff, kf, alt, kalt = sch, ("sch",), sce, ("sce",)
                        outb, kout = scl, ("scl",)
                    OP("vector", lambda e, src=src, d12=d12: e.tensor_scalar(
                        d12, src, 8, 12, ALU.bitwise_and, ALU.logical_shift_left),
                       reads=[ksrc], writes=[kd12])
                    OP("vector", lambda e, src=src: e.tensor_scalar(
                        scw, src, 7, None, ALU.bitwise_and),
                       reads=[ksrc], writes=[("scw",)])
                    OP("vector", lambda e, b6=b6: e.tensor_scalar(
                        b6, scw, 6, None, ALU.logical_shift_left),
                       reads=[("scw",)], writes=[kb6])
                    OP("vector", lambda e, b6=b6, d12=d12, ff=ff: e.scalar_tensor_tensor(
                        ff, b6, 16128, d12, ALU.add, ALU.add),
                       reads=[kb6, kd12], writes=[kf])
                    OP("vector", lambda e, d12=d12, alt=alt: e.scalar_tensor_tensor(
                        alt, scw, 16128, d12, ALU.mult, ALU.add),
                       reads=[("scw",), kd12], writes=[kalt])
                    OP("vector", lambda e: e.tensor_scalar(scw, scw, 2, None, ALU.is_lt),
                       reads=[("scw",)], writes=[("scw",)])
                    OP("vector", lambda e, alt=alt, ff=ff, outb=outb: e.select(
                        outb, scw, alt, ff, add_drain=True),
                       reads=[("scw",), kalt, kf], writes=[kout])
                    OP("vector", lambda e, outb=outb, mat=mat, par=par, sl=sl, hw0=hw0: e.tensor_tensor(
                        deq[:, sl, mat, par, hw0:hw0 + 4, :].rearrange("p a b -> p (a b)"),
                        outb.bitcast(BF16),
                        scsb[:, sl, mat, hw0:hw0 + 4, :].rearrange("p a b -> p (a b)"),
                        ALU.mult),
                       reads=[kout, ("scsb", sl, mat)], writes=[("deq", sl, mat)])

        def p2_mm(sb):
            sl = sb % 2
            for npar in range(2):
                sl2 = (sb * 2 + npar) % 2
                nt = sb * 2 + npar
                for tb in range(8):
                    gidx = (sb * 2 + npar) * 8 + tb
                    pair = gidx % 3
                    bg, bu = 2 * pair, 2 * pair + 1
                    for mat, bank in ((0, bg), (1, bu)):
                        for k in range(16):
                            par, w = k % 2, k // 2
                            lhsT = deq[:, sl, mat, par, w, npar::2]
                            fn = lambda e, bank=bank, lhsT=lhsT, k=k, tb=tb: e.matmul(
                                pb[bank], lhsT, xnT[:, k, tb * 512:(tb + 1) * 512],
                                start=(k == 0), stop=(k == 15))
                            PE(fn, reads=[("deq", sl, mat)], writes=[("pb", bank)],
                               end=(k == 15))
                    OP("scalar", lambda e, bg=bg, tb=tb: e.activation(
                        glt[:, tb % 2, :], pb[bg], AF.Gelu_apprx_tanh),
                       reads=[("pb", bg)], writes=[("glt", tb % 2)])
                    OP("scalar", lambda e, bu=bu, tb=tb: e.copy(
                        upsb[:, tb % 2, :], pb[bu]),
                       reads=[("pb", bu)], writes=[("upsb", tb % 2)])
                    OP("gpsimd", lambda e, tb=tb, sl2=sl2: e.tensor_tensor(
                        istg[:, sl2, (tb % 2) * 512:(tb % 2 + 1) * 512], glt[:, tb % 2, :],
                        upsb[:, tb % 2, :], ALU.mult),
                       reads=[("glt", tb % 2), ("upsb", tb % 2)], writes=[("istg", sl2, tb % 2)])
                    if tb % 2 == 1:
                        half = tb // 2
                        DMA("sync", "qs", Isp[nt][:, half * 1024:(half + 1) * 1024],
                            istg[:, sl2, :],
                            reads=[("istg", sl2, 0), ("istg", sl2, 1)])

        p2_loads(0)
        p2_loads(1)
        p2_deq(0)
        for sb in range(NSB):
            p2_loads(sb + 2)
            p2_deq(sb + 1)
            p2_mm(sb)
    ctl.fence()

    if phase_limit >= 3:
        phA.reset(ph0)
        bigA.reset(big0)
        dtT3 = bigA.alloc("dtT3", [NH, T], F32)
        xcbf = bigA.alloc("xcbf", [128, 2, NCT, CH], BF16)
        onehbf = bigA.alloc("onehbf", [NH, NH * 128], BF16)
        negm_s = bigA.alloc("negm_s", [128, 2, CH], F32)
        convw_s = bigA.alloc("convw_s", [128, NCT, 4], F32)
        convb_s = bigA.alloc("convb_s", [128, NCT], F32)
        drep_s = bigA.alloc("drep_s", [128, NH], F32)
        nwt_s = bigA.alloc("nwt_s", [128, NH], F32)
        ztile = bigA.alloc("ztile", [128, NH, CH], F32)
        S_b = bigA.alloc("S_b", [NS, NH, HD], F32)
        Sbf = bigA.alloc("Sbf", [NS, NH, HD], BF16)
        win = phA.alloc("win", [128, 6, 260], F32)
        convt = phA.alloc("convt", [128, 2, CH], F32)

        scoresT = phA.alloc("scoresT", [128, 2, CH], F32)
        aA = phA.alloc("aA", [NH, CH], F32)
        aC = phA.alloc("aC", [NH, CH], F32)
        naC = phA.alloc("naC", [NH, CH], F32)
        dtdC = phA.alloc("dtdC", [NH, CH], F32)
        GC = phA.alloc("GC", [NH, CH], F32)
        ahi = phA.alloc("ahi", [NH, CH], BF16)
        alo = phA.alloc("alo", [NH, CH], BF16)
        naT = phA.alloc("naT", [128, 2, NH], F32)
        dtcT = phA.alloc("dtcT", [128, 2, NH], F32)
        GT = phA.alloc("GT", [128, 2, NH], F32)
        alr = phA.alloc("alr", [1, NH], F32)
        alrE = phA.alloc("alrE", [1, NH], F32)
        alrP = phA.alloc("alrP", [1, 2, NH], BF16)
        expal_s = phA.alloc("expal_s", [NS, NH], F32)
        x_tm = phA.alloc("x_tm", [128, 4, CH], BF16)
        wq = phA.alloc("wq", [128, 4, 2, CH], BF16)
        wt1 = phA.alloc("wt1", [128, 4, CH], F32)
        lm = phA.alloc("lm", [128, 4, CH], F32)
        Bt_tm = phA.alloc("Bt_tm", [128, 2, NS], BF16)
        Bd = phA.alloc("Bd", [128, 4, 2, NS], BF16)
        Ce = phA.alloc("Ce", [NS, 4, CH], BF16)
        ctbf = phA.alloc("ctbf", [NS, CH], BF16)
        expab = phA.alloc("expab", [NS, 4, CH], F32)
        ysb = phA.alloc("ysb", [128, 4, CH], F32)
        yg = phA.alloc("yg", [128, 4, CH], F32)
        sqbf = phA.alloc("sqbf", [128, 4, CH], BF16)

        rr1 = phA.alloc("rr1", [1, 4, CH], F32)
        rr2 = phA.alloc("rr2", [1, 4, CH], F32)
        rsp = phA.alloc("rsp", [1, 4, 2, CH], BF16)
        Stmp = phA.alloc("Stmp", [NS, 4, HD], F32)
        ynst = phA.alloc("ynst", [128, 4, CH], BF16)

        for dst, src, nm in [(convw_s, convw_d, "cw"), (convb_s, convb_d, "cb"),
                             (drep_s, drep_d, "dr"), (nwt_s, nwt_d, "nw"),
                             (negm_s, negm_d, "nm"), (onehbf, onehot_d, "oh"),
                             (dtT3, DTsp, "dt3")]:
            DMA("gpsimd", "qg", dst, src, writes=[("c", nm)])
        OP("vector", lambda e: e.memset(S_b, 0.0), writes=[("S",)])
        OP("vector", lambda e: e.memset(Sbf, 0.0), writes=[("Sbf",)])

        pv_xtm = pb[1].bitcast(BF16)
        for c in range(NCH):
            cs = c % 2
            c0 = c * CH
            for ct in range(NCT):
                ws = ct % 6
                if c == 0:
                    OP("vector", lambda e, ws=ws: e.memset(win[:, ws, 0:3], 0.0),
                       reads=[], writes=[("win", ws)])
                    DMA("sync", "qs", win[:, ws, 3:259], Z[32 + ct][:, 0:256],
                        writes=[("win", ws)])
                else:
                    DMA("sync", "qs", win[:, ws, 0:259], Z[32 + ct][:, c0 - 3:c0 + 256],
                        writes=[("win", ws)])
                cv = ct % 2
                OP("vector", lambda e, ws=ws, cv=cv, ct=ct: e.tensor_scalar(
                    convt[:, cv, :], win[:, ws, 0:256], convw_s[:, ct, 0:1], None, ALU.mult),
                   reads=[("win", ws), ("c", "cw")], writes=[("cvt", cv)])
                for k in range(1, 4):
                    OP("vector", lambda e, ws=ws, cv=cv, ct=ct, k=k: e.scalar_tensor_tensor(
                        convt[:, cv, :], win[:, ws, k:k + 256], convw_s[:, ct, k:k + 1],
                        convt[:, cv, :], ALU.mult, ALU.add),
                       reads=[("win", ws), ("cvt", cv), ("c", "cw")], writes=[("cvt", cv)])
                OP("scalar", lambda e, cv=cv, ct=ct, cs=cs: e.activation(
                    xcbf[:, cs, ct, :], convt[:, cv, :], AF.Silu, bias=convb_s[:, ct:ct + 1]),
                   reads=[("cvt", cv), ("c", "cb")], writes=[("xc", cs, ct)])
            for hh in range(NH):
                DMA("sync", "qs", ztile[:, hh, :], Z[hh][:, c0:c0 + CH],
                    writes=[("zt", hh)])
                OP("scalar", lambda e, hh=hh: e.activation(
                    ztile[:, hh, :], ztile[:, hh, :], AF.Sigmoid),
                   reads=[("zt", hh)], writes=[("zt", hh)])
            OP("vector", lambda e, c0=c0: e.tensor_scalar(
                aA, dtT3[:, c0:c0 + CH], A_s, None, ALU.mult),
               reads=[("c", "dt3"), ("A",)], writes=[("aA",)])
            OP("vector", lambda e: e.tensor_tensor_scan(aC, aA, aA, 0.0, ALU.add, ALU.bypass),
               reads=[("aA",)], writes=[("aC",)])
            OP("vector", lambda e: e.tensor_scalar(naC, aC, -1.0, None, ALU.mult),
               reads=[("aC",)], writes=[("naC",)])
            OP("scalar", lambda e: e.activation(dtdC, aC, AF.Exp, scale=-1.0,
                                                bias=aC[:, 255:256]),
               reads=[("aC",)], writes=[("dtd",)])
            OP("vector", lambda e, c0=c0: e.tensor_tensor(
                GC, dtT3[:, c0:c0 + CH], dtdC, ALU.mult),
               reads=[("c", "dt3"), ("dtd",)], writes=[("G",)])
            OP("scalar", lambda e: e.copy(ahi, aC), reads=[("aC",)], writes=[("ahi",)])
            OP("vector", lambda e: e.tensor_tensor(alo, aC, ahi, ALU.subtract),
               reads=[("aC",), ("ahi",)], writes=[("alo",)])
            for jt in range(2):
                for mblk in range(4):
                    pbase = 32 * mblk
                    col = 128 * jt + 32 * mblk
                    OP("vector", lambda e, jt=jt, pbase=pbase, col=col: e.transpose(
                        naT[pbase:pbase + 32, jt, :], naC[:, col:col + 32]),
                       reads=[("naC",)], writes=[("naT", jt)])
                    OP("vector", lambda e, jt=jt, pbase=pbase, col=col, c0=c0: e.transpose(
                        dtcT[pbase:pbase + 32, jt, :], dtT3[:, c0 + col:c0 + col + 32]),
                       reads=[("c", "dt3")], writes=[("dtcT", jt)])
                    OP("vector", lambda e, jt=jt, pbase=pbase, col=col: e.transpose(
                        GT[pbase:pbase + 32, jt, :], GC[:, col:col + 32]),
                       reads=[("G",)], writes=[("GT", jt)])
            DMA("gpsimd", "qg", drtmp, aC[:, 255:256], reads=[("aC",)], writes=[("drt",)])
            DMA("gpsimd", "qg", alr, drtmp[None, :], reads=[("drt",)], writes=[("alr",)])
            OP("scalar", lambda e: e.activation(alrE, alr, AF.Exp),
               reads=[("alr",)], writes=[("alrE",)])
            OP("scalar", lambda e: e.copy(alrP[:, 0, :], alrE),
               reads=[("alrE",)], writes=[("alrP",)])
            OP("vector", lambda e: e.tensor_tensor(alrP[:, 1, :], alrE, alrP[:, 0, :], ALU.subtract),
               reads=[("alrE",), ("alrP",)], writes=[("alrP",)])
            PE(lambda e: e.matmul(pb[7][0:NS, 0:NH], ones_r[:, 0:NS], alrP[:, 0, :],
                                  start=True, stop=False),
               reads=[("alrP",), ("c", "onr")], writes=[("P", 7)])
            PE(lambda e: e.matmul(pb[7][0:NS, 0:NH], ones_r[:, 0:NS], alrP[:, 1, :],
                                  start=False, stop=True),
               reads=[("alrP",)], writes=[("P", 7)], end=True)
            OP("vector", lambda e: e.tensor_copy(expal_s, pb[7][0:NS, 0:NH]),
               reads=[("P", 7)], writes=[("expal",)])
            OP("vector", lambda e, cs=cs: e.tensor_copy(ctbf, xcbf[64:128, cs, 32, :]),
               reads=[("xc", cs, 32)], writes=[("ctbf",)])
            for jt in range(2):
                PE(lambda e, jt=jt, cs=cs: e.matmul(
                    pb[0][:, jt * 256:(jt + 1) * 256],
                    xcbf[0:64, cs, 32, jt * 128:(jt + 1) * 128],
                    ctbf, start=True, stop=True),
                   reads=[("xc", cs, 32), ("ctbf",)], writes=[("P", 0)], end=True)
                OP("vector", lambda e, jt=jt: e.tensor_copy(
                    scoresT[:, jt, :], pb[0][:, jt * 256:(jt + 1) * 256]),
                   reads=[("P", 0)], writes=[("scT", jt)])
                PE(lambda e, jt=jt, cs=cs: e.transpose(
                    pv_xtm[:, 512 + 64 * jt:512 + 64 * (jt + 1)],
                    xcbf[0:64, cs, 32, jt * 128:(jt + 1) * 128], ident_s[0:64, 0:64]),
                   reads=[("xc", cs, 32), ("c", "ident")], writes=[("P", 1)], end=True)
                OP("scalar", lambda e, jt=jt: e.copy(
                    Bt_tm[:, jt, :], pv_xtm[:, 512 + 64 * jt:512 + 64 * (jt + 1)]),
                   reads=[("P", 1)], writes=[("Btm", jt)])
            for hh in range(NH):
                hr = hh % 4
                h4 = hh % 4
                abank = 0 if hh % 2 == 0 else 2
                ybank = (3, 4, 6)[hh % 3]
                PE(lambda e, hh=hh, abank=abank: e.matmul(
                    pb[abank][:, 0:256], onehbf[:, hh * 128:(hh + 1) * 128], ahi,
                    start=True, stop=False),
                   reads=[("ahi",), ("c", "oh")], writes=[("P", 0) if hr == 0 else ("P", 2)])
                PE(lambda e, hh=hh, abank=abank: e.matmul(
                    pb[abank][:, 0:256], onehbf[:, hh * 128:(hh + 1) * 128], alo,
                    start=False, stop=True),
                   reads=[("alo",)], writes=[("P", 0) if hr == 0 else ("P", 2)], end=True)
                for jt in range(2):
                    slot = hr * 2 + jt
                    PE(lambda e, hh=hh, jt=jt, slot=slot, cs=cs: e.transpose(
                        pv_xtm[:, slot * 128:(slot + 1) * 128],
                        xcbf[:, cs, hh, jt * 128:(jt + 1) * 128], ident_s),
                       reads=[("xc", cs, hh), ("c", "ident")], writes=[("P", 1)], end=True)
                    eng = "vector" if jt == 0 else "scalar"
                    OP(eng, (lambda e, jt=jt, slot=slot, h4=h4, isv=(jt == 0):
                             (e.tensor_copy if isv else e.copy)(
                                 x_tm[:, h4, jt * 128:(jt + 1) * 128],
                                 pv_xtm[:, slot * 128:(slot + 1) * 128])),
                       reads=[("P", 1)], writes=[("xtm", h4)])
                for jt in range(2):
                    OP("vector", lambda e, jt=jt, hr=hr, abank=abank: e.tensor_tensor(
                        wt1[:, jt, :], negm_s[:, jt, :], pb[abank][:, 0:256], ALU.add),
                       reads=[("P", 0) if hr == 0 else ("P", 2), ("c", "nm")], writes=[("wt1", jt)])
                    OP("scalar", lambda e, jt=jt, hh=hh: e.activation(
                        lm[:, jt, :], wt1[:, jt, :], AF.Exp, bias=naT[:, jt, hh:hh + 1]),
                       reads=[("wt1", jt), ("naT", jt)], writes=[("lm", jt)])
                    OP("vector", lambda e, jt=jt, hh=hh, hr=hr: e.scalar_tensor_tensor(
                        wq[:, hr, jt, :], lm[:, jt, :], dtcT[:, jt, hh:hh + 1],
                        scoresT[:, jt, :], ALU.mult, ALU.mult),
                       reads=[("lm", jt), ("dtcT", jt), ("scT", jt)], writes=[("wq", hr, jt)])
                OP("scalar", lambda e, hr=hr, abank=abank: e.activation(
                    expab[:, hr, :], pb[abank][0:NS, 0:256], AF.Exp),
                   reads=[("P", 0) if hr == 0 else ("P", 2)], writes=[("expab", hr)])
                OP("vector", lambda e, hr=hr: e.tensor_tensor(
                    Ce[:, hr, :], ctbf, expab[:, hr, :], ALU.mult),
                   reads=[("expab", hr), ("ctbf",)], writes=[("Ce", hr)])
                PE(lambda e, hr=hr, h4=h4, ybank=ybank: e.matmul(
                    pb[ybank][:, 0:256], x_tm[:, h4, 0:128], wq[:, hr, 0, :],
                    start=True, stop=False),
                   reads=[("xtm", h4), ("wq", hr, 0)], writes=[("P", 3 + hr)])
                PE(lambda e, hr=hr, h4=h4, ybank=ybank: e.matmul(
                    pb[ybank][:, 0:256], x_tm[:, h4, 128:256], wq[:, hr, 1, :],
                    start=False, stop=False),
                   reads=[("xtm", h4), ("wq", hr, 1)], writes=[("P", 3 + hr)])
                PE(lambda e, hr=hr, hh=hh, ybank=ybank: e.matmul(
                    pb[ybank][:, 0:256], Sbf[:, hh, :], Ce[:, hr, :],
                    start=False, stop=True),
                   reads=[("Sbf",), ("Ce", hr)], writes=[("P", 3 + hr)], end=True)
                for jt in range(2):
                    OP("vector", lambda e, jt=jt, hh=hh, hr=hr: e.tensor_scalar(
                        Bd[:, hr, jt, :], Bt_tm[:, jt, :], GT[:, jt, hh:hh + 1], None, ALU.mult),
                       reads=[("Btm", jt), ("GT", jt)], writes=[("Bd", hr, jt)])
                sh = hh % 2
                PE(lambda e, hr=hr, h4=h4, sh=sh: e.matmul(
                    pb[5][0:NS, 128 * sh:128 * (sh + 1)], Bd[:, hr, 0, :],
                    x_tm[:, h4, 0:128], start=True, stop=False),
                   reads=[("Bd", hr, 0), ("xtm", h4)], writes=[("P", 5)])
                PE(lambda e, hr=hr, h4=h4, sh=sh: e.matmul(
                    pb[5][0:NS, 128 * sh:128 * (sh + 1)], Bd[:, hr, 1, :],
                    x_tm[:, h4, 128:256], start=False, stop=True),
                   reads=[("Bd", hr, 1), ("xtm", h4)], writes=[("P", 5)], end=True)
                OP("vector", lambda e, hr=hr, hh=hh, cs=cs, ybank=ybank: e.scalar_tensor_tensor(
                    ysb[:, hr, :], xcbf[:, cs, hh, :], drep_s[:, hh:hh + 1],
                    pb[ybank][:, 0:256], ALU.mult, ALU.add),
                   reads=[("xc", cs, hh), ("c", "dr"), ("P", 3 + hr)], writes=[("ysb", hr)])
                OP("vector", lambda e, hr=hr, hh=hh: e.tensor_tensor(
                    yg[:, hr, :], ysb[:, hr, :], ztile[:, hh, :], ALU.mult),
                   reads=[("ysb", hr), ("zt", hh)], writes=[("yg", hr)])
                OP("scalar", lambda e, hr=hr: e.activation(
                    sqbf[:, hr, :], yg[:, hr, :], AF.Square),
                   reads=[("yg", hr)], writes=[("sqbf", hr)])
                PE(lambda e, hr=hr, ybank=ybank: e.matmul(
                    pb[ybank][0:1, 0:256], ones_c, sqbf[:, hr, :], start=True, stop=True),
                   reads=[("sqbf", hr), ("c", "onc")], writes=[("P", ybank)], end=True)
                OP("vector", lambda e, hr=hr, ybank=ybank: e.tensor_scalar(
                    rr1[:, hr, :], pb[ybank][0:1, 0:256], 1.0 / HD, 1e-6, ALU.mult, ALU.add),
                   reads=[("P", ybank)], writes=[("rr1", hr)])
                OP("scalar", lambda e, hr=hr: e.activation(
                    rr2[:, hr, :], rr1[:, hr, :], AF.Ln),
                   reads=[("rr1", hr)], writes=[("rr2", hr)])
                OP("scalar", lambda e, hr=hr: e.activation(
                    rr1[:, hr, :], rr2[:, hr, :], AF.Exp, scale=-0.5),
                   reads=[("rr2", hr)], writes=[("rr1", hr)])
                OP("scalar", lambda e, hr=hr: e.copy(rsp[:, hr, 0, :], rr1[:, hr, :]),
                   reads=[("rr1", hr)], writes=[("rsp", hr)])
                OP("vector", lambda e, hr=hr: e.tensor_tensor(
                    rsp[:, hr, 1, :], rr1[:, hr, :], rsp[:, hr, 0, :], ALU.subtract),
                   reads=[("rr1", hr), ("rsp", hr)], writes=[("rsp", hr)])
                PE(lambda e, hr=hr: e.matmul(
                    pb[7][:, 0:256], ones_r, rsp[:, hr, 0, :], start=True, stop=False),
                   reads=[("rsp", hr), ("c", "onr")], writes=[("P", 7)])
                PE(lambda e, hr=hr: e.matmul(
                    pb[7][:, 0:256], ones_r, rsp[:, hr, 1, :], start=False, stop=True),
                   reads=[("rsp", hr)], writes=[("P", 7)], end=True)
                OP("vector", lambda e, hr=hr: e.tensor_tensor(
                    ysb[:, hr, :], yg[:, hr, :], pb[7][:, 0:256], ALU.mult),
                   reads=[("yg", hr), ("P", 7)], writes=[("ysb", hr)])
                OP("vector", lambda e, hr=hr, hh=hh, h4=h4: e.tensor_scalar(
                    ynst[:, h4, :], ysb[:, hr, :], nwt_s[:, hh:hh + 1], None, ALU.mult),
                   reads=[("ysb", hr), ("c", "nw")], writes=[("ynst", h4)])
                DMA("sync", "qs", Ysp[hh][:, c0:c0 + CH], ynst[:, h4, :],
                    reads=[("ynst", h4)])
                OP("vector", lambda e, hr=hr, hh=hh: e.tensor_scalar(
                    Stmp[:, hr, :], S_b[:, hh, :], expal_s[:, hh:hh + 1], None, ALU.mult),
                   reads=[("S",), ("expal",)], writes=[("Stmp", hr)])
                OP("vector", lambda e, hr=hr, hh=hh, sh=sh: e.tensor_tensor(
                    S_b[:, hh, :], pb[5][0:NS, 128 * sh:128 * (sh + 1)],
                    Stmp[:, hr, :], ALU.add),
                   reads=[("P", 5), ("Stmp", hr)], writes=[("S",)])
                OP("scalar", lambda e, hh=hh: e.copy(Sbf[:, hh, :], S_b[:, hh, :]),
                   reads=[("S",)], writes=[("Sbf",)])
    ctl.fence()


    if phase_limit >= 4:
        phA.reset(ph0)
        bigA.reset(big0)
        opjT = bigA.alloc("opjT", [128, 32, D], BF16)
        ynb = phA.alloc("ynb", [128, 2, 32, 256], BF16)
        sst = phA.alloc("sst", [128, 2, D], F32)
        for kt in range(32):
            DMA("gpsimd", "qg", opjT[:, kt, :], opt[:, kt, :], writes=[("opj", kt)])
        for ts in range(16):
            sl = ts % 2
            DMA("sync", "qs", ynb[:, sl],
                Ysp[:, :, ts * 256:(ts + 1) * 256].rearrange("k p t -> p k t"),
                writes=[("ynb", sl)])
            for tt2 in range(2):
                tt = ts * 2 + tt2
                bank0 = (tt % 2) * 4
                for k in range(32):
                    for eb in range(4):
                        fn = lambda e, sl=sl, k=k, tt2=tt2, eb=eb, b=bank0: e.matmul(
                            pb[b + eb], ynb[:, sl, k, tt2 * 128:(tt2 + 1) * 128],
                            opjT[:, k, eb * 512:(eb + 1) * 512],
                            start=(k == 0), stop=(k == 31))
                        PE(fn, reads=[("ynb", sl), ("opj", k)],
                           writes=[("P", bank0 + eb)], end=(k == 31 and eb == 3))
                for eb in range(4):
                    eng = "vector" if eb % 2 == 0 else "scalar"
                    fn = (lambda e, eb=eb, b=bank0, tt=tt, isv=(eb % 2 == 0):
                          (e.tensor_copy if isv else e.copy)(
                              sst[:, tt % 2, eb * 512:(eb + 1) * 512], pb[b + eb]))
                    OP(eng, fn, reads=[("P", bank0 + eb)], writes=[("sst", tt % 2, eb)])
                DMA("sync", "qs", Ssp.rearrange("(n p) d -> n p d", p=128)[tt],
                    sst[:, tt % 2, :], reads=[("sst", tt % 2, eb) for eb in range(4)])
    ctl.fence()

    if phase_limit >= 5:
        phA.reset(ph0)
        bigA.reset(big0)
        wdT = bigA.alloc("wdT", [128, 128, 512], BF16)
        inb = phA.alloc("inb", [128, 4, 8, 256], BF16)
        stb = phA.alloc("stb", [128, 2, 512], F32)
        xtb = phA.alloc("xtb", [128, 2, 512], F32)
        glb = phA.alloc("glb", [128, 2, 512], F32)
        otb = phA.alloc("otb", [128, 2, 512], F32)
        srr = Ssp.rearrange("(n p) d -> n p d", p=128)
        orr = out_d.rearrange("(n p) d -> n p d", p=128)
        for hb in range(4):
            for kc in range(16):
                DMA("gpsimd", "qg", wdT[:, kc * 8:(kc + 1) * 8, :],
                    DdT[hb, kc * 8:(kc + 1) * 8].rearrange("k p t -> p k t"),
                    writes=[("wdT", kc)])
            for tp in range(16):
                slp = tp % 2
                for kgi in range(19):
                    if kgi < 16:
                        kg_p = kgi
                        DMA("sync" if kg_p % 2 == 0 else "scalar",
                            "qs" if kg_p % 2 == 0 else "qa", inb[:, kg_p % 4, :, :],
                            Isp[kg_p * 8:(kg_p + 1) * 8, :, tp * 256:(tp + 1) * 256].rearrange(
                                "k p t -> p k t"),
                            writes=[("inb", kg_p % 4)])
                    if kgi < 3:
                        continue
                    kg = kgi - 3
                    for k8 in range(8):
                        k = kg * 8 + k8
                        for tt2 in range(2):
                            bank = slp * 2 + tt2
                            fn = lambda e, kg=kg, k8=k8, k=k, tt2=tt2, bank=bank: e.matmul(
                                pb[bank], inb[:, kg % 4, k8, tt2 * 128:(tt2 + 1) * 128],
                                wdT[:, k, :], start=(k == 0), stop=(k == 127))
                            PE(fn, reads=[("inb", kg % 4), ("wdT", kg)],
                               writes=[("P", bank)],
                               end=(k8 == 7 and tt2 == 1))
                for tt2 in range(2):
                    tt = tp * 2 + tt2
                    bank = slp * 2 + tt2
                    DMA("gpsimd", "qg", stb[:, tt2, :], srr[tt][:, hb * 512:(hb + 1) * 512],
                        writes=[("stb", tt2)])
                    DMA("gpsimd", "qg", xtb[:, tt2, :],
                        xr[tt][:, hb * 512:(hb + 1) * 512], writes=[("xtb", tt2)])
                    OP("scalar", lambda e, bank=bank, tt2=tt2: e.activation(
                        glb[:, tt2, :], pb[bank], AF.Silu),
                       reads=[("P", bank)], writes=[("glb", tt2)])
                    OP("vector", lambda e, tt2=tt2: e.tensor_tensor(
                        otb[:, tt2, :], glb[:, tt2, :], stb[:, tt2, :], ALU.mult),
                       reads=[("glb", tt2), ("stb", tt2)], writes=[("otb", tt2)])
                    OP("vector", lambda e, tt2=tt2: e.tensor_tensor(
                        otb[:, tt2, :], otb[:, tt2, :], xtb[:, tt2, :], ALU.add),
                       reads=[("otb", tt2), ("xtb", tt2)], writes=[("otb", tt2)])
                    DMA("gpsimd", "qg", orr[tt][:, hb * 512:(hb + 1) * 512],
                        otb[:, tt2, :], reads=[("otb", tt2)])
    ctl.fence()


    if dev_spill and phase_limit < 3:
        xntdump = ap(nc.dram_tensor("xntdump", [128, 16, T], BF16, kind="ExternalOutput"))
        DMA("sync", "qs", xntdump, xnT)

    ctl.fence()

    with nc.Block() as blk:
        for eng_name, meth in [("tensor", blk.tensor), ("vector", blk.vector),
                               ("scalar", blk.scalar), ("gpsimd", blk.gpsimd),
                               ("sync", blk.sync)]:
            ops = ctl.ops[eng_name]
            def body(e, ops=ops):
                for f in ops:
                    f(e)
            meth(body)
    stack.close()
    return nc


def kernel(**inputs):
    h = host_prep(inputs)
    u = np.asarray(inputs["L0_attn_u"])
    nc = build_nc()
    cores = CORE_IDS if CORE_IDS is not None else list(range(N_CORES))
    in_maps = []
    for b in range(len(cores)):
        m = dict(h)
        m["x"] = np.ascontiguousarray(u[b])
        in_maps.append(m)
    res = run_bass_kernel_spmd(nc, in_maps, core_ids=cores)
    return np.stack([r["out"] for r in res.results], axis=0)
